# version 01
# baseline (speedup 1.0000x reference)
import numpy as np
import ml_dtypes
from contextlib import ExitStack
import concourse.bass as bass
import concourse.mybir as mybir
from concourse.bass_utils import run_bass_kernel_spmd

F32 = mybir.dt.float32
BF16 = mybir.dt.bfloat16
I8 = mybir.dt.int8
AF = mybir.ActivationFunctionType
OP = mybir.AluOpType
AX = mybir.AxisListType
NPBF16 = ml_dtypes.bfloat16

B, S, D, NH, HD = 2, 4096, 2048, 16, 128
E, F, INTER = 8, 2048, 8192
NCORE = 8
TPC = 1024
NT = TPC // 128
NDC = D // 128
SCALE = np.float32(HD ** (-0.5))
GC = np.float32(1.0 / np.e)
GL = np.float32(0.5 - np.log(4.0 / 3.0))
GU = np.float32(0.5)

ENGS = ("tensor", "vector", "scalar", "sync", "gpsimd")


class Ctx:
    def __init__(self, nc, stack):
        self.nc = nc
        self.scopes = [stack]
        self.stages = []
        self.cur = None
        self.semh = {}
        self.semv = {}
        self.stage()

    def stage(self):
        self.cur = {e: [] for e in ENGS}
        self.stages.append(self.cur)

    def push_scope(self, st):
        self.scopes.append(st)

    def pop_scope(self):
        self.scopes.pop()

    def finalize(self):
        for st in self.stages:
            with self.nc.Block() as blk:
                for eng in ENGS:
                    lst = st[eng]
                    if lst:
                        getattr(blk, eng)(lambda e, lst=lst: [f(e) for f in lst])

    def sbuf(self, name, shape, dtype):
        return self.scopes[-1].enter_context(self.nc.sbuf_tensor(name, shape, dtype))

    def psum(self, name, shape, dtype=F32):
        return self.scopes[-1].enter_context(self.nc.psum_tensor(name, shape, dtype))

    def sem(self, name):
        if name not in self.semh:
            self.semh[name] = self.scopes[0].enter_context(self.nc.semaphore(name))
            self.semv[name] = 0
        return self.semh[name]

    def bump(self, name, by):
        self.sem(name)
        self.semv[name] += by
        return self.semv[name]

    def v(self, name):
        return self.semv.get(name, 0)

    def emit(self, eng, fn):
        self.cur[eng].append(fn)

    def wait(self, eng, name, val):
        if val <= 0:
            return
        h = self.sem(name)
        self.emit(eng, lambda e, h=h, val=val: e.wait_ge(h, val))

    def op(self, eng, fn, inc=None):
        if inc is None:
            self.emit(eng, fn)
            return None
        name, by = inc
        h = self.sem(name)
        v = self.bump(name, by)
        self.emit(eng, lambda e, fn=fn, h=h, by=by: fn(e).then_inc(h, by))
        return v

    def nopinc(self, eng, name, by=1):
        h = self.sem(name)
        v = self.bump(name, by)
        if eng in ("vector",):
            self.emit(eng, lambda e, h=h, by=by: e.engine_nop().then_inc(h, by))
        else:
            self.emit(eng, lambda e, h=h, by=by: e.nop().then_inc(h, by))
        return v

    def dma(self, eng, out_ap, in_ap, sem):
        h = self.sem(sem)
        v = self.bump(sem, 16)
        self.emit(eng, lambda e, o=out_ap, i=in_ap, h=h: e.dma_start(out=o, in_=i).then_inc(h, 16))
        return v


def host_consts():
    c = {}
    c["ident"] = np.eye(128, dtype=NPBF16)
    R = np.zeros((128, 128), dtype=np.float32)
    for i in range(64):
        R[i, i + 64] = -1.0
        R[i + 64, i] = 1.0
    c["rotT"] = np.ascontiguousarray(R.T).astype(NPBF16)
    inv_freq = 1.0 / (10000.0 ** (np.arange(0, HD, 2, dtype=np.float32) / HD))
    t = np.arange(S, dtype=np.float32)
    freqs = np.einsum("j,t->jt", inv_freq, t)
    emb = np.concatenate([freqs, freqs], axis=0)
    c["cosF"] = np.cos(emb).astype(np.float32)
    c["sinF"] = np.sin(emb).astype(np.float32)
    m4 = np.zeros((4, 128, 512), dtype=NPBF16)
    p = np.arange(128)[:, None]
    f = np.arange(512)[None, :]
    for j in range(4):
        m4[j] = (p + 128 * j <= f).astype(NPBF16)
    c["mask4"] = m4
    return c


def prep_weights(inp):
    w = {}
    for nm in ("wq", "wk"):
        wT = np.ascontiguousarray(inp[f"L0_attn_{nm}"].T)
        w[f"l0_{nm}T_h"] = np.ascontiguousarray(wT.reshape(D, NH, HD).transpose(1, 0, 2))
    for nm in ("wv", "wg"):
        wT = np.ascontiguousarray(inp[f"L0_attn_{nm}"].T)
        w[f"l0_{nm}T_c"] = np.ascontiguousarray(wT.reshape(D, 4, 512).transpose(1, 0, 2))
    woT = np.ascontiguousarray(inp["L0_attn_wo"].T)
    w["l0_woT_c"] = np.ascontiguousarray(woT.reshape(D, 4, 512).transpose(1, 0, 2))
    w["l0_Wr_t"] = np.ascontiguousarray(
        inp["L0_ffn_Wr"].reshape(NDC, 128, E).transpose(1, 0, 2).reshape(128, NDC * E))
    w["l0_Wg_t"] = np.ascontiguousarray(inp["L0_ffn_Wg"].reshape(E, D, 4, 512).transpose(0, 2, 1, 3))
    w["l0_Wu_t"] = np.ascontiguousarray(inp["L0_ffn_Wu"].reshape(E, D, 4, 512).transpose(0, 2, 1, 3))
    w["l0_Wd_t"] = np.ascontiguousarray(inp["L0_ffn_Wd"].reshape(E, F, 4, 512).transpose(0, 2, 1, 3))
    for nm in ("wq", "wk"):
        wT = np.ascontiguousarray(inp[f"L1_attn_{nm}"].T)
        w[f"l1_{nm}T_h"] = np.ascontiguousarray(wT.reshape(D, NH, HD).transpose(1, 0, 2))
    wT = np.ascontiguousarray(inp["L1_attn_wv"].T)
    w["l1_wvT_c"] = np.ascontiguousarray(wT.reshape(D, 4, 512).transpose(1, 0, 2))
    woT1 = np.ascontiguousarray(inp["L1_attn_wo"].T)
    w["l1_woT_c"] = np.ascontiguousarray(woT1.reshape(D, 4, 512).transpose(1, 0, 2))
    w["l1_wg_t"] = np.ascontiguousarray(inp["L1_ffn_w_gate"].reshape(D, 16, 512).transpose(1, 0, 2))
    w["l1_wu_t"] = np.ascontiguousarray(inp["L1_ffn_w_up"].reshape(D, 16, 512).transpose(1, 0, 2))
    w["l1_wd_t"] = np.ascontiguousarray(inp["L1_ffn_w_down"].reshape(INTER, 4, 512).transpose(1, 0, 2))
    w["l1_sg_t"] = np.ascontiguousarray(inp["L1_ffn_s_gate"].reshape(64, 128).T)
    w["l1_su_t"] = np.ascontiguousarray(inp["L1_ffn_s_up"].reshape(64, 128).T)
    w["l1_sd_t"] = np.ascontiguousarray(np.tile(inp["L1_ffn_s_down"][None, :], (128, 1)))
    w["l0_normw_t"] = np.ascontiguousarray(np.tile(inp["L0_attn_norm_w"][None, :], (128, 1)))
    w["l0_temp_t"] = np.ascontiguousarray(
        np.tile(np.asarray(inp["L0_ffn_temperature"], dtype=np.float32).reshape(1, 1), (128, 1)))
    for nm in ("lambda_q1", "lambda_q2", "lambda_k1", "lambda_k2"):
        w[f"l1_{nm}"] = np.ascontiguousarray(inp[f"L1_attn_{nm}"].reshape(1, HD))
    return w


def emit_ln_xnT(ctx, x_dram, xnT, xn_out_dram=None, tag="ln"):
    xt = [ctx.sbuf(f"{tag}_xt{i}", [128, D], F32) for i in range(2)]
    xnb = [ctx.sbuf(f"{tag}_xnb{i}", [128, D], BF16) for i in range(2)]
    scr = ctx.sbuf(f"{tag}_scr", [128, D], F32)
    st = [ctx.sbuf(f"{tag}_st{i}", [128, 8], F32) for i in range(2)]
    tp = [ctx.psum(f"{tag}_tp{i}", [128, 128], BF16) for i in range(2)]
    ident = ctx.ident
    for i in range(NT):
        b = i % 2
        if i >= 2:
            ctx.wait("sync", f"{tag}_xn", i - 1)
            ctx.wait("sync", f"{tag}_sq", i - 1)
        ctx.dma("sync", xt[b][:, :], x_dram[i * 128:(i + 1) * 128, :], f"{tag}_xld")
        ctx.wait("scalar", f"{tag}_xld", 16 * (i + 1))
        sti = st[b]
        if i >= 2:
            ctx.wait("scalar", f"{tag}_rc", i - 1)
            ctx.wait("scalar", f"{tag}_xn", i - 1)
        ctx.wait("vector", f"{tag}_xld", 16 * (i + 1))
        ctx.op("vector", lambda e, b=b, sti=sti: e.reduce_sum(sti[:, 0:1], xt[b][:, :], AX.X),
               inc=(f"{tag}_rs", 1))
        ctx.op("scalar", lambda e, b=b, sti=sti: e.activation(scr[:, :], xt[b][:, :], AF.Square,
                                                              accum_out=sti[:, 1:2]),
               inc=(f"{tag}_sq", 1))
        ctx.wait("scalar", f"{tag}_sq", ctx.v(f"{tag}_sq"))
        ctx.wait("scalar", f"{tag}_rs", ctx.v(f"{tag}_rs"))
        ctx.op("scalar", lambda e, sti=sti: e.mul(sti[:, 2:3], sti[:, 0:1], 1.0 / D),
               inc=(f"{tag}_sch", 1))
        ctx.wait("scalar", f"{tag}_sch", ctx.v(f"{tag}_sch"))
        ctx.op("scalar", lambda e, sti=sti: e.mul(sti[:, 3:4], sti[:, 2:3], sti[:, 2:3]),
               inc=(f"{tag}_sch", 1))
        ctx.wait("scalar", f"{tag}_sch", ctx.v(f"{tag}_sch"))
        ctx.op("scalar", lambda e, sti=sti: e.activation(sti[:, 3:4], sti[:, 3:4], AF.Identity,
                                                         scale=np.float32(-1.0),
                                                         bias=np.float32(1e-5)),
               inc=(f"{tag}_sch", 1))
        ctx.wait("scalar", f"{tag}_sch", ctx.v(f"{tag}_sch"))
        ctx.op("scalar", lambda e, sti=sti: e.activation(sti[:, 4:5], sti[:, 1:2], AF.Sqrt,
                                                         scale=np.float32(1.0 / D),
                                                         bias=sti[:, 3:4]),
               inc=(f"{tag}_sqa", 1))
        ctx.wait("vector", f"{tag}_sqa", ctx.v(f"{tag}_sqa"))
        ctx.op("vector", lambda e, sti=sti: e.reciprocal(sti[:, 5:6], sti[:, 4:5]),
               inc=(f"{tag}_rc", 1))
        rcv = ctx.v(f"{tag}_rc")
        ctx.wait("scalar", f"{tag}_rc", rcv)
        ctx.op("scalar", lambda e, sti=sti: e.mul(sti[:, 6:7], sti[:, 2:3], sti[:, 5:6]),
               inc=(f"{tag}_mrs", 1))
        if i >= 2:
            ctx.wait("vector", f"{tag}_tp", (i - 1) * NDC)
            if xn_out_dram is not None:
                ctx.wait("vector", f"{tag}_xnst", 16 * (i - 1))
        ctx.wait("vector", f"{tag}_mrs", ctx.v(f"{tag}_mrs"))
        ctx.wait("vector", f"{tag}_rc", rcv)
        ctx.wait("vector", f"{tag}_xld", 16 * (i + 1))
        ctx.op("vector", lambda e, b=b, sti=sti: e.tensor_scalar(xnb[b][:, :], xt[b][:, :],
                                                                 sti[:, 5:6], sti[:, 6:7],
                                                                 OP.mult, OP.subtract),
               inc=(f"{tag}_xn", 1))
        xnv = ctx.v(f"{tag}_xn")
        if xn_out_dram is not None:
            ctx.wait("sync", f"{tag}_xn", xnv)
            ctx.dma("sync", xn_out_dram[i * 128:(i + 1) * 128, :], xnb[b][:, :], f"{tag}_xnst")
        ctx.wait("tensor", f"{tag}_xn", xnv)
        for dc in range(NDC):
            g = i * NDC + dc
            pb = g % 2
            if g >= 2:
                ctx.wait("tensor", f"{tag}_tev", g - 1)
            ctx.op("tensor", lambda e, b=b, dc=dc, pb=pb: e.transpose(
                tp[pb][:, :], xnb[b][:, dc * 128:(dc + 1) * 128], ident[:, :]),
                inc=(f"{tag}_tp", 1))
            ctx.wait("vector", f"{tag}_tp", ctx.v(f"{tag}_tp"))
            ctx.op("vector", lambda e, dc=dc, i=i, pb=pb: e.tensor_copy(
                xnT[:, dc * TPC + i * 128: dc * TPC + (i + 1) * 128], tp[pb][:, :]),
                inc=(f"{tag}_tev", 1))


def emit_qk_proj(ctx, xnT, wT_h_dram, out_dram, cos_sb, sin_sb, tag):
    wsb = ctx.sbuf(f"{tag}_w", [128, NDC * 128], BF16)
    qraw = ctx.sbuf(f"{tag}_qraw", [128, TPC], BF16)
    t1 = ctx.sbuf(f"{tag}_t1", [128, 512], F32)
    t2 = ctx.sbuf(f"{tag}_t2", [128, 512], F32)
    rop = [ctx.sbuf(f"{tag}_rop{i}", [128, TPC], BF16) for i in range(2)]
    ps = [ctx.psum(f"{tag}_ps{i}", [128, 512]) for i in range(2)]
    psr = ctx.psum(f"{tag}_psr", [128, 512])
    rotT = ctx.rotT
    NSQB = TPC // 512
    for h in range(NH):
        if h > 0:
            ctx.wait("gpsimd", f"{tag}_mm", h * NSQB)
        src = wT_h_dram[h].rearrange("(dc p) c -> p dc c", p=128)
        ctx.dma("gpsimd", wsb[:, :].rearrange("p (dc c) -> p dc c", dc=NDC), src, f"{tag}_wld")
        ctx.wait("tensor", f"{tag}_wld", ctx.v(f"{tag}_wld"))
        for sqb in range(NSQB):
            g = h * NSQB + sqb
            pb = g % 2
            if g >= 2:
                ctx.wait("tensor", f"{tag}_evac", g - 1)
            for dc in range(NDC):
                ctx.op("tensor", lambda e, dc=dc, sqb=sqb, pb=pb: e.matmul(
                    ps[pb][:, :], wsb[:, dc * 128:(dc + 1) * 128],
                    xnT[:, dc * TPC + sqb * 512: dc * TPC + sqb * 512 + 512],
                    start=(dc == 0), stop=(dc == NDC - 1)),
                    inc=(f"{tag}_mm", 1) if dc == NDC - 1 else None)
            mmv = ctx.v(f"{tag}_mm")
            ctx.wait("scalar", f"{tag}_mm", mmv)
            if h > 0:
                ctx.wait("scalar", f"{tag}_rope", (h - 1) * NSQB + sqb + 1)
            ctx.op("scalar", lambda e, sqb=sqb, pb=pb: e.copy(qraw[:, sqb * 512:sqb * 512 + 512],
                                                              ps[pb][:, :]),
                   inc=(f"{tag}_evac", 1))
            evv = ctx.v(f"{tag}_evac")
            ctx.wait("tensor", f"{tag}_evac", evv)
            if g >= 1:
                ctx.wait("tensor", f"{tag}_t2", g)
            ctx.op("tensor", lambda e, sqb=sqb: e.matmul(
                psr[:, :], rotT[:, :], qraw[:, sqb * 512:sqb * 512 + 512], start=True, stop=True),
                inc=(f"{tag}_rot", 1))
            rotv = ctx.v(f"{tag}_rot")
            ctx.wait("vector", f"{tag}_evac", evv)
            rb = h % 2
            if h >= 2 and sqb == 0:
                ctx.wait("vector", f"{tag}_qst", 16 * (h - 1))
            ctx.op("vector", lambda e, sqb=sqb: e.tensor_mul(
                t1[:, :], qraw[:, sqb * 512:sqb * 512 + 512], cos_sb[:, sqb * 512:sqb * 512 + 512]))
            ctx.wait("vector", f"{tag}_rot", rotv)
            ctx.op("vector", lambda e, sqb=sqb: e.tensor_mul(t2[:, :], psr[:, :],
                                                             sin_sb[:, sqb * 512:sqb * 512 + 512]),
                   inc=(f"{tag}_t2", 1))
            ctx.wait("vector", f"{tag}_t2", ctx.v(f"{tag}_t2"))
            ctx.op("vector", lambda e, rb=rb, sqb=sqb: e.tensor_add(
                rop[rb][:, sqb * 512:sqb * 512 + 512], t1[:, :], t2[:, :]),
                inc=(f"{tag}_rope", 1))
        ctx.wait("sync", f"{tag}_rope", ctx.v(f"{tag}_rope"))
        ctx.dma("sync", out_dram[h * 128:(h + 1) * 128, :], rop[h % 2][:, :], f"{tag}_qst")


def emit_vg_proj(ctx, xnT, wT_c_dram, out_dram, tag, silu=False):
    wsb = ctx.sbuf(f"{tag}_w", [128, NDC * 512], BF16)
    stg = [ctx.sbuf(f"{tag}_stg{i}", [128, 512], BF16) for i in range(2)]
    ps = [ctx.psum(f"{tag}_ps{i}", [128, 512]) for i in range(2)]
    for cb in range(4):
        if cb > 0:
            ctx.wait("gpsimd", f"{tag}_mm", cb * NT)
        src = wT_c_dram[cb].rearrange("(dc p) c -> p dc c", p=128)
        ctx.dma("gpsimd", wsb[:, :].rearrange("p (dc c) -> p dc c", dc=NDC), src, f"{tag}_wld")
        ctx.wait("tensor", f"{tag}_wld", ctx.v(f"{tag}_wld"))
        for i in range(NT):
            g = cb * NT + i
            pb = g % 2
            if g >= 2:
                ctx.wait("tensor", f"{tag}_evac", g - 1)
            for dc in range(NDC):
                ctx.op("tensor", lambda e, dc=dc, i=i, pb=pb: e.matmul(
                    ps[pb][:, :], xnT[:, dc * TPC + i * 128: dc * TPC + (i + 1) * 128],
                    wsb[:, dc * 512:(dc + 1) * 512],
                    start=(dc == 0), stop=(dc == NDC - 1)),
                    inc=(f"{tag}_mm", 1) if dc == NDC - 1 else None)
            ctx.wait("scalar", f"{tag}_mm", ctx.v(f"{tag}_mm"))
            if g >= 2:
                ctx.wait("scalar", f"{tag}_st", 16 * (g - 1))
            sb = g % 2
            if silu:
                ctx.op("scalar", lambda e, sb=sb, pb=pb: e.activation(stg[sb][:, :], ps[pb][:, :], AF.Silu),
                       inc=(f"{tag}_evac", 1))
            else:
                ctx.op("scalar", lambda e, sb=sb, pb=pb: e.copy(stg[sb][:, :], ps[pb][:, :]),
                       inc=(f"{tag}_evac", 1))
            ctx.wait("sync", f"{tag}_evac", ctx.v(f"{tag}_evac"))
            ctx.dma("sync", out_dram[i * 128:(i + 1) * 128, cb * 512:(cb + 1) * 512], stg[sb][:, :],
                    f"{tag}_st")


def emit_router(ctx, xnT, wr_sb, rtemp, wfinal, tag="rt", dbg=None):
    ps = ctx.psum(f"{tag}_ps", [128, 16])
    inh = ctx.sbuf(f"{tag}_inh", [128, 8], F32)
    dist = ctx.sbuf(f"{tag}_dist", [128, 8], F32)
    zone = ctx.sbuf(f"{tag}_zone", [128, 8], F32)
    z2 = ctx.sbuf(f"{tag}_z2", [128, 8], F32)
    wz = ctx.sbuf(f"{tag}_wz", [128, 8], F32)
    fb = ctx.sbuf(f"{tag}_fb", [128, 8], F32)
    fbx = ctx.sbuf(f"{tag}_fbx", [128, 8], F32)
    m = ctx.sbuf(f"{tag}_m", [128, 8], F32)
    mask1 = ctx.sbuf(f"{tag}_mask1", [128, 8], F32)
    mask2 = ctx.sbuf(f"{tag}_mask2", [128, 8], F32)
    ctx.wait("tensor", "cwld", 16)

    def vch(fn):
        v = ctx.op("vector", fn, inc=(f"{tag}_vch", 1))
        ctx.wait("vector", f"{tag}_vch", v)

    for i in range(NT):
        if i > 0:
            ctx.wait("tensor", f"{tag}_sg", i)
        for dc in range(NDC):
            ctx.op("tensor", lambda e, dc=dc, i=i: e.matmul(
                ps[:, 0:8], xnT[:, dc * TPC + i * 128: dc * TPC + (i + 1) * 128],
                wr_sb[:, dc * 8:(dc + 1) * 8],
                start=(dc == 0), stop=(dc == NDC - 1)),
                inc=(f"{tag}_mm", 1) if dc == NDC - 1 else None)
        ctx.wait("scalar", f"{tag}_mm", ctx.v(f"{tag}_mm"))
        if i > 0:
            ctx.wait("scalar", f"{tag}_done", i)
        ctx.op("scalar", lambda e: e.activation(inh[:, :], ps[:, 0:8], AF.Sigmoid, scale=rtemp[:, 0:1]),
               inc=(f"{tag}_sg", 1))
        ctx.wait("scalar", f"{tag}_sg", ctx.v(f"{tag}_sg"))
        if dbg is not None:
            ctx.op("scalar", lambda e, i=i: e.copy(dbg[:, i * 8:(i + 1) * 8], inh[:, :]))
        ctx.op("scalar", lambda e: e.activation(dist[:, :], inh[:, :], AF.Abs, bias=np.float32(-GC)))
        ctx.nopinc("scalar", f"{tag}_sch", 1)
        ctx.wait("scalar", f"{tag}_sch", ctx.v(f"{tag}_sch"))
        ctx.nopinc("scalar", f"{tag}_sch", 1)
        ctx.wait("scalar", f"{tag}_sch", ctx.v(f"{tag}_sch"))
        ctx.op("scalar", lambda e: e.activation(wz[:, :], dist[:, :], AF.Exp, scale=np.float32(-10.0)))
        ctx.op("scalar", lambda e: e.activation(fb[:, :], dist[:, :], AF.Exp,
                                                scale=np.float32(-1.0 / 0.3)),
               inc=(f"{tag}_sc", 1))
        ctx.wait("vector", f"{tag}_sc", ctx.v(f"{tag}_sc"))
        vch(lambda e: e.tensor_single_scalar(zone[:, :], inh[:, :], float(GL), OP.is_ge))
        vch(lambda e: e.tensor_single_scalar(z2[:, :], inh[:, :], float(GU), OP.is_le))
        vch(lambda e: e.tensor_mul(zone[:, :], zone[:, :], z2[:, :]))
        vch(lambda e: e.tensor_mul(wz[:, :], wz[:, :], zone[:, :]))
        vch(lambda e: e.reduce_sum(m[:, 0:1], wz[:, :], AX.X))
        vch(lambda e: e.tensor_single_scalar(m[:, 1:2], m[:, 0:1], 1e-8, OP.is_lt))
        vch(lambda e: e.reduce_max(m[:, 2:3], fb[:, :], AX.X))
        vch(lambda e: e.tensor_scalar(mask1[:, :], fb[:, :], m[:, 2:3], None, OP.is_equal))
        vch(lambda e: e.scalar_tensor_tensor(fbx[:, :], mask1[:, :], -1e30, fb[:, :],
                                             OP.mult, OP.add))
        vch(lambda e: e.reduce_max(m[:, 3:4], fbx[:, :], AX.X))
        vch(lambda e: e.tensor_scalar(mask2[:, :], fbx[:, :], m[:, 3:4], None, OP.is_equal))
        vch(lambda e: e.tensor_add(mask1[:, :], mask1[:, :], mask2[:, :]))
        vch(lambda e: e.tensor_mul(mask1[:, :], fb[:, :], mask1[:, :]))
        vch(lambda e: e.reduce_sum(m[:, 4:5], mask1[:, :], AX.X))
        vch(lambda e: e.tensor_single_scalar(m[:, 4:5], m[:, 4:5], 1e-8, OP.max))
        vch(lambda e: e.reciprocal(m[:, 5:6], m[:, 4:5]))
        vch(lambda e: e.tensor_scalar(mask1[:, :], mask1[:, :], m[:, 5:6], None, OP.mult))
        vch(lambda e: e.scalar_tensor_tensor(wz[:, :], mask1[:, :], m[:, 1:2], wz[:, :],
                                             OP.mult, OP.add))
        vch(lambda e: e.reduce_sum(m[:, 6:7], wz[:, :], AX.X))
        vch(lambda e: e.tensor_single_scalar(m[:, 6:7], m[:, 6:7], 1e-8, OP.max))
        vch(lambda e: e.reciprocal(m[:, 7:8], m[:, 6:7]))
        ctx.op("vector", lambda e, i=i: e.tensor_scalar(wfinal[:, i * 8:(i + 1) * 8], wz[:, :],
                                                        m[:, 7:8], None, OP.mult),
               inc=(f"{tag}_done", 1))


def emit_moe(ctx, xnT, wg_dram, wu_dram, wd_dram, wfinal, fg_out_dram, tag="moe"):
    wg_sb = ctx.sbuf(f"{tag}_wg", [128, NDC * 512], BF16)
    wu_sb = ctx.sbuf(f"{tag}_wu", [128, NDC * 512], BF16)
    wd_sb = ctx.sbuf(f"{tag}_wd", [128, NDC * 512], BF16)
    fuseT = ctx.sbuf(f"{tag}_fuseT", [128, NDC * TPC], BF16)
    sg_scr = ctx.sbuf(f"{tag}_sg", [128, 512], BF16)
    acc = ctx.sbuf(f"{tag}_acc", [128, NT * D], F32)
    fg_stg = [ctx.sbuf(f"{tag}_fg{i}", [128, 512], F32) for i in range(2)]
    psg = ctx.psum(f"{tag}_psg", [128, 512])
    psu = ctx.psum(f"{tag}_psu", [128, 512])
    psd = [ctx.psum(f"{tag}_psd{i}", [128, 512]) for i in range(2)]
    NSQB = TPC // 512

    def loadw(dram_slice, dest):
        src = dram_slice.rearrange("(c p) f -> p c f", p=128)
        ctx.dma("gpsimd", dest[:, :].rearrange("p (c f) -> p c f", c=NDC), src, f"{tag}_wld")
        return ctx.v(f"{tag}_wld")

    for e_i in range(E):
        for fs in range(4):
            sidx = e_i * 4 + fs
            if sidx > 0:
                ctx.wait("gpsimd", f"{tag}_gmm", sidx * 4 * NSQB)
                ctx.wait("gpsimd", f"{tag}_umm", sidx * 4 * NSQB)
            loadw(wg_dram[e_i, fs], wg_sb)
            v2 = loadw(wu_dram[e_i, fs], wu_sb)
            ctx.wait("tensor", f"{tag}_wld", v2)
            for fc in range(4):
                fcg = fs * 4 + fc
                for sqb in range(NSQB):
                    git = e_i * 16 * NSQB + fcg * NSQB + sqb
                    if git >= 1:
                        ctx.wait("tensor", f"{tag}_fuse", git)
                    for dc in range(NDC):
                        ctx.op("tensor", lambda e, dc=dc, fc=fc, sqb=sqb: e.matmul(
                            psg[:, :], wg_sb[:, dc * 512 + fc * 128: dc * 512 + (fc + 1) * 128],
                            xnT[:, dc * TPC + sqb * 512: dc * TPC + sqb * 512 + 512],
                            start=(dc == 0), stop=(dc == NDC - 1)),
                            inc=(f"{tag}_gmm", 1) if dc == NDC - 1 else None)
                    for dc in range(NDC):
                        ctx.op("tensor", lambda e, dc=dc, fc=fc, sqb=sqb: e.matmul(
                            psu[:, :], wu_sb[:, dc * 512 + fc * 128: dc * 512 + (fc + 1) * 128],
                            xnT[:, dc * TPC + sqb * 512: dc * TPC + sqb * 512 + 512],
                            start=(dc == 0), stop=(dc == NDC - 1)),
                            inc=(f"{tag}_umm", 1) if dc == NDC - 1 else None)
                    gv = ctx.v(f"{tag}_gmm")
                    uv = ctx.v(f"{tag}_umm")
                    ctx.wait("scalar", f"{tag}_gmm", gv)
                    if git >= 1:
                        ctx.wait("scalar", f"{tag}_fuse", git)
                    ctx.op("scalar", lambda e: e.activation(sg_scr[:, :], psg[:, :], AF.Silu),
                           inc=(f"{tag}_silu", 1))
                    ctx.wait("vector", f"{tag}_silu", ctx.v(f"{tag}_silu"))
                    ctx.wait("vector", f"{tag}_umm", uv)
                    ctx.op("vector", lambda e, fcg=fcg, sqb=sqb: e.tensor_tensor(
                        fuseT[:, fcg * TPC + sqb * 512: fcg * TPC + sqb * 512 + 512],
                        psu[:, :], sg_scr[:, :], OP.mult),
                        inc=(f"{tag}_fuse", 1))
        fusev = ctx.v(f"{tag}_fuse")
        for ds in range(4):
            sidx = e_i * 4 + ds
            if sidx > 0:
                ctx.wait("gpsimd", f"{tag}_dmm", sidx * NT)
            v3 = loadw(wd_dram[e_i, ds], wd_sb)
            ctx.wait("tensor", f"{tag}_wld", v3)
            if ds == 0:
                ctx.wait("tensor", f"{tag}_fuse", fusev)
            for i in range(NT):
                git = sidx * NT + i
                pb = git % 2
                if git >= 2:
                    ctx.wait("tensor", f"{tag}_dps", git - 1)
                for fcg in range(NDC):
                    ctx.op("tensor", lambda e, fcg=fcg, i=i, pb=pb: e.matmul(
                        psd[pb][:, :], fuseT[:, fcg * TPC + i * 128: fcg * TPC + (i + 1) * 128],
                        wd_sb[:, fcg * 512:(fcg + 1) * 512],
                        start=(fcg == 0), stop=(fcg == NDC - 1)),
                        inc=(f"{tag}_dmm", 1) if fcg == NDC - 1 else None)
                ctx.wait("vector", f"{tag}_dmm", ctx.v(f"{tag}_dmm"))
                if e_i == 0:
                    ctx.op("vector", lambda e, i=i, ds=ds, pb=pb, e_i=e_i: e.tensor_scalar(
                        acc[:, i * D + ds * 512: i * D + (ds + 1) * 512], psd[pb][:, :],
                        wfinal[:, i * 8 + e_i: i * 8 + e_i + 1], None, OP.mult),
                        inc=(f"{tag}_dps", 1))
                else:
                    ctx.op("vector", lambda e, i=i, ds=ds, pb=pb, e_i=e_i: e.scalar_tensor_tensor(
                        acc[:, i * D + ds * 512: i * D + (ds + 1) * 512], psd[pb][:, :],
                        wfinal[:, i * 8 + e_i: i * 8 + e_i + 1],
                        acc[:, i * D + ds * 512: i * D + (ds + 1) * 512],
                        OP.mult, OP.add),
                        inc=(f"{tag}_dps", 1))
    ctx.wait("scalar", f"{tag}_dps", ctx.v(f"{tag}_dps"))
    for i in range(NT):
        for dsb in range(4):
            g = i * 4 + dsb
            b = g % 2
            if g >= 2:
                ctx.wait("scalar", f"{tag}_fgst", 16 * (g - 1))
            ctx.op("scalar", lambda e, i=i, dsb=dsb, b=b: e.activation(
                fg_stg[b][:, :], acc[:, i * D + dsb * 512: i * D + (dsb + 1) * 512], AF.Silu),
                inc=(f"{tag}_fgev", 1))
            ctx.wait("sync", f"{tag}_fgev", ctx.v(f"{tag}_fgev"))
            ctx.dma("sync", fg_out_dram[i * 128:(i + 1) * 128, dsb * 512:(dsb + 1) * 512],
                    fg_stg[b][:, :], f"{tag}_fgst")


def build_launch_A():
    nc = bass.Bass("TRN2", target_bir_lowering=False)
    x_tok = nc.dram_tensor("x_tok", [TPC, D], F32, kind="ExternalInput").ap()
    wq_h = nc.dram_tensor("wq_h", [NH, D, HD], F32, kind="ExternalInput").ap()
    wk_h = nc.dram_tensor("wk_h", [NH, D, HD], F32, kind="ExternalInput").ap()
    wv_c = nc.dram_tensor("wv_c", [4, D, 512], F32, kind="ExternalInput").ap()
    wg_c = nc.dram_tensor("wg_c", [4, D, 512], F32, kind="ExternalInput").ap()
    wr_d = nc.dram_tensor("wr_d", [128, NDC * E], F32, kind="ExternalInput").ap()
    temp_d = nc.dram_tensor("temp_d", [128, 1], F32, kind="ExternalInput").ap()
    mwg = nc.dram_tensor("mwg", [E, 4, D, 512], F32, kind="ExternalInput").ap()
    mwu = nc.dram_tensor("mwu", [E, 4, D, 512], F32, kind="ExternalInput").ap()
    mwd = nc.dram_tensor("mwd", [E, 4, F, 512], F32, kind="ExternalInput").ap()
    identd = nc.dram_tensor("identd", [128, 128], BF16, kind="ExternalInput").ap()
    rotTd = nc.dram_tensor("rotTd", [128, 128], BF16, kind="ExternalInput").ap()
    cosd = nc.dram_tensor("cosd", [128, TPC], F32, kind="ExternalInput").ap()
    sind = nc.dram_tensor("sind", [128, TPC], F32, kind="ExternalInput").ap()

    qt_o = nc.dram_tensor("qt_o", [D, TPC], BF16, kind="ExternalOutput").ap()
    kt_o = nc.dram_tensor("kt_o", [D, TPC], BF16, kind="ExternalOutput").ap()
    v_o = nc.dram_tensor("v_o", [TPC, D], BF16, kind="ExternalOutput").ap()
    sg_o = nc.dram_tensor("sg_o", [TPC, D], BF16, kind="ExternalOutput").ap()
    fg_o = nc.dram_tensor("fg_o", [TPC, D], F32, kind="ExternalOutput").ap()
    xn_o = nc.dram_tensor("xn_o", [TPC, D], BF16, kind="ExternalOutput").ap()
    wr_o = nc.dram_tensor("wr_o", [128, NT * E], F32, kind="ExternalOutput").ap()
    inh_o = nc.dram_tensor("inh_o", [128, 2 * NT * E], F32, kind="ExternalOutput").ap()

    with ExitStack() as stack:
        ctx = Ctx(nc, stack)
        xnT = ctx.sbuf("xnT", [128, NDC * TPC], BF16)
        ctx.ident = ctx.sbuf("identsb", [128, 128], BF16)
        ctx.rotT = ctx.sbuf("rotTsb", [128, 128], BF16)
        cosQ = ctx.sbuf("cosQ", [128, TPC], F32)
        sinQ = ctx.sbuf("sinQ", [128, TPC], F32)
        cosK = ctx.sbuf("cosK", [128, TPC], F32)
        sinK = ctx.sbuf("sinK", [128, TPC], F32)
        wr_sb = ctx.sbuf("wr_sb", [128, NDC * E], BF16)
        rtemp = ctx.sbuf("rtemp", [128, 1], F32)
        tempsb = ctx.sbuf("tempsb", [128, 1], F32)
        wfinal = ctx.sbuf("wfinal", [128, NT * E], F32)

        ctx.dma("sync", ctx.ident[:, :], identd, "cld")
        ctx.dma("sync", ctx.rotT[:, :], rotTd, "cld")
        ctx.dma("sync", cosK[:, :], cosd, "cld")
        ctx.dma("sync", sinK[:, :], sind, "cld")
        ctx.dma("sync", tempsb[:, :], temp_d, "cld")
        ctx.dma("gpsimd", wr_sb[:, :], wr_d, "cwld")
        ctx.wait("vector", "cld", ctx.v("cld"))
        ctx.op("vector", lambda e: e.tensor_single_scalar(cosQ[:, :], cosK[:, :], float(SCALE), OP.mult))
        ctx.op("vector", lambda e: e.tensor_single_scalar(sinQ[:, :], sinK[:, :], float(SCALE), OP.mult))
        ctx.op("vector", lambda e: e.reciprocal(rtemp[:, :], tempsb[:, :]))

        def phase(emitfn):
            ctx.stage()
            with ExitStack() as ph:
                ctx.push_scope(ph)
                emitfn()
                ctx.pop_scope()

        phase(lambda: emit_ln_xnT(ctx, x_tok, xnT, xn_out_dram=xn_o, tag="ln0"))
        phase(lambda: emit_qk_proj(ctx, xnT, wq_h, qt_o, cosQ, sinQ, tag="q0"))
        phase(lambda: emit_qk_proj(ctx, xnT, wk_h, kt_o, cosK, sinK, tag="k0"))
        phase(lambda: emit_vg_proj(ctx, xnT, wv_c, v_o, tag="v0", silu=False))
        phase(lambda: emit_vg_proj(ctx, xnT, wg_c, sg_o, tag="g0", silu=True))
        inhdbg = ctx.sbuf("inhdbg", [128, 2 * NT * E], F32)
        phase(lambda: emit_router(ctx, xnT, wr_sb, rtemp, wfinal, tag="rt", dbg=inhdbg))
        ctx.wait("sync", "rt_done", ctx.v("rt_done"))
        ctx.dma("sync", inh_o, inhdbg[:, :], "wr_st")
        ctx.wait("sync", "rt_done", ctx.v("rt_done"))
        ctx.dma("sync", wr_o, wfinal[:, :], "wr_st")
        phase(lambda: emit_moe(ctx, xnT, mwg, mwu, mwd, wfinal, fg_o, tag="moe"))

        ctx.stage()
        for s in ("ln0_xnst", "q0_qst", "k0_qst", "v0_st", "g0_st", "wr_st", "moe_fgst"):
            ctx.wait("sync", s, ctx.v(s))
        ctx.finalize()
    return nc


def inmaps_A(inputs, w, c):
    x = inputs["L0_attn_hidden_states"]
    maps = []
    for core in range(NCORE):
        b, t0 = core // 4, (core % 4) * TPC
        maps.append({
            "x_tok": np.ascontiguousarray(x[b, t0:t0 + TPC, :]),
            "wq_h": w["l0_wqT_h"], "wk_h": w["l0_wkT_h"],
            "wv_c": w["l0_wvT_c"], "wg_c": w["l0_wgT_c"],
            "wr_d": w["l0_Wr_t"], "temp_d": w["l0_temp_t"],
            "mwg": w["l0_Wg_t"], "mwu": w["l0_Wu_t"], "mwd": w["l0_Wd_t"],
            "identd": c["ident"], "rotTd": c["rotT"],
            "cosd": np.ascontiguousarray(c["cosF"][:, t0:t0 + TPC]),
            "sind": np.ascontiguousarray(c["sinF"][:, t0:t0 + TPC]),
        })
    return maps




def emit_attn_l0(ctx, qt_d, kt_d, v_d, mask_sb, o_d, tag="at0"):
    NKC = S // 128
    NSQB = S // 512
    qt = [ctx.sbuf(f"{tag}_qt{i}", [128, S], BF16) for i in range(2)]
    kt = [ctx.sbuf(f"{tag}_kt{i}", [128, S], BF16) for i in range(2)]
    ve = [ctx.sbuf(f"{tag}_ve{i}", [128, NKC * 129], BF16) for i in range(2)]
    pexp = [ctx.sbuf(f"{tag}_px{i}", [128, 512], BF16) for i in range(2)]
    onrm = [ctx.sbuf(f"{tag}_on{i}", [128, 128], BF16) for i in range(2)]
    rden = ctx.sbuf(f"{tag}_rd", [128, 4], F32)
    ps_s = [ctx.psum(f"{tag}_pss{i}", [128, 512]) for i in range(2)]
    ps_o = [ctx.psum(f"{tag}_pso{i}", [128, 132]) for i in range(4)]
    for i in range(2):
        ctx.op("vector", lambda e, i=i: e.memset(
            ve[i][:, :].rearrange("p (c x) -> p c x", x=129)[:, :, 128:129], 1.0))
    ctx.nopinc("vector", f"{tag}_ones", 1)

    for j in range(4):
        hb = j % 2
        if j >= 2:
            ctx.wait("sync", f"{tag}_smm", (j - 1) * 144)
            ctx.wait("sync", f"{tag}_av", (j - 1) * 144)
        ctx.dma("sync", qt[hb][:, :], qt_d[j], f"{tag}_qld")
        ctx.dma("sync", kt[hb][:, :], kt_d[j], f"{tag}_kld")
        vsrc = v_d[j].rearrange("(c p) d -> p c d", p=128)
        vdst = ve[hb][:, :].rearrange("p (c x) -> p c x", x=129)[:, :, 0:128]
        ctx.dma("sync", vdst, vsrc, f"{tag}_vld")
        ldv = (ctx.v(f"{tag}_qld"), ctx.v(f"{tag}_kld"), ctx.v(f"{tag}_vld"))
        ctx.wait("tensor", f"{tag}_qld", ldv[0])
        ctx.wait("tensor", f"{tag}_kld", ldv[1])
        ctx.wait("tensor", f"{tag}_vld", ldv[2])
        if j == 0:
            ctx.wait("tensor", f"{tag}_ones", 1)
        for sqb in range(NSQB):
            nkc = 4 * (sqb + 1)
            base_nrm = ctx.v(f"{tag}_nrm")
            ctx.wait("tensor", f"{tag}_nrm", base_nrm)
            for kc in range(nkc):
                sb = kc % 2
                if kc >= 2:
                    ctx.wait("tensor", f"{tag}_exp", ctx.v(f"{tag}_exp") - 1)
                ctx.op("tensor", lambda e, sb=sb, kc=kc, sqb=sqb, hb=hb: e.matmul(
                    ps_s[sb][:, :], kt[hb][:, kc * 128:(kc + 1) * 128],
                    qt[hb][:, sqb * 512:sqb * 512 + 512], start=True, stop=True),
                    inc=(f"{tag}_smm", 1))
                smmv = ctx.v(f"{tag}_smm")
                ctx.wait("scalar", f"{tag}_smm", smmv)
                if kc >= 2:
                    ctx.wait("scalar", f"{tag}_av", ctx.v(f"{tag}_av") - 1)
                elif sqb > 0 or j > 0:
                    ctx.wait("scalar", f"{tag}_av", ctx.v(f"{tag}_av"))
                ctx.op("scalar", lambda e, sb=sb: e.activation(pexp[sb][:, :], ps_s[sb][:, :], AF.Exp),
                       inc=(f"{tag}_exp", 1))
                expv = ctx.v(f"{tag}_exp")
                jdiag = kc - 4 * sqb
                if jdiag >= 0:
                    ctx.wait("vector", f"{tag}_exp", expv)
                    ctx.op("vector", lambda e, sb=sb, jd=jdiag: e.tensor_mul(
                        pexp[sb][:, :], pexp[sb][:, :], mask_sb[:, jd * 512:(jd + 1) * 512]),
                        inc=(f"{tag}_msk", 1))
                    ctx.wait("tensor", f"{tag}_msk", ctx.v(f"{tag}_msk"))
                else:
                    ctx.wait("tensor", f"{tag}_exp", expv)
                for st4 in range(4):
                    ctx.op("tensor", lambda e, st4=st4, sb=sb, kc=kc, hb=hb, nkc=nkc: e.matmul(
                        ps_o[st4][:, 0:129], pexp[sb][:, st4 * 128:(st4 + 1) * 128],
                        ve[hb][:, kc * 129:(kc + 1) * 129],
                        start=(kc == 0), stop=(kc == nkc - 1)),
                        inc=(f"{tag}_av", 1) if st4 == 3 else None)
            avv = ctx.v(f"{tag}_av")
            ctx.wait("vector", f"{tag}_av", avv)
            for st4 in range(4):
                ctx.op("vector", lambda e, st4=st4: e.reciprocal(rden[:, st4:st4 + 1],
                                                                 ps_o[st4][:, 128:129]),
                       inc=(f"{tag}_rc", 1))
                ctx.wait("vector", f"{tag}_rc", ctx.v(f"{tag}_rc"))
                ob = st4 % 2
                if j > 0 or sqb > 0 or st4 >= 2:
                    ctx.wait("vector", f"{tag}_ost", max(0, 16 * (ctx.v(f"{tag}_nrm") - 1)))
                ctx.op("vector", lambda e, st4=st4, ob=ob: e.tensor_scalar(
                    onrm[ob][:, :], ps_o[st4][:, 0:128], rden[:, st4:st4 + 1], None, OP.mult),
                    inc=(f"{tag}_nrm", 1))
                ctx.wait("sync", f"{tag}_nrm", ctx.v(f"{tag}_nrm"))
                ctx.dma("sync", o_d[j][sqb * 512 + st4 * 128: sqb * 512 + (st4 + 1) * 128, :],
                        onrm[ob][:, :], f"{tag}_ost")


def build_launch_B():
    nc = bass.Bass("TRN2", target_bir_lowering=False)
    qt_d = nc.dram_tensor("qt_d", [4, 128, S], BF16, kind="ExternalInput").ap()
    kt_d = nc.dram_tensor("kt_d", [4, 128, S], BF16, kind="ExternalInput").ap()
    v_d = nc.dram_tensor("v_d", [4, S, HD], BF16, kind="ExternalInput").ap()
    mask_d = nc.dram_tensor("mask_d", [128, 4 * 512], BF16, kind="ExternalInput").ap()
    o_d = nc.dram_tensor("o_d", [4, S, HD], BF16, kind="ExternalOutput").ap()
    with ExitStack() as stack:
        ctx = Ctx(nc, stack)
        mask_sb = ctx.sbuf("mask_sb", [128, 4 * 512], BF16)
        ctx.dma("sync", mask_sb[:, :], mask_d, "cld")
        ctx.wait("tensor", "cld", 16)
        ctx.wait("vector", "cld", 16)
        ctx.stage()
        emit_attn_l0(ctx, qt_d, kt_d, v_d, mask_sb, o_d, tag="at0")
        ctx.stage()
        ctx.wait("sync", "at0_ost", ctx.v("at0_ost"))
        ctx.finalize()
    return nc


def inmaps_B(a_results, c):
    maps = []
    m4 = np.ascontiguousarray(c["mask4"].transpose(1, 0, 2).reshape(128, 4 * 512))
    for core in range(NCORE):
        b, h0 = core // 4, (core % 4) * 4
        qt = np.zeros((4, 128, S), dtype=NPBF16)
        kt = np.zeros((4, 128, S), dtype=NPBF16)
        v = np.zeros((4, S, HD), dtype=NPBF16)
        for r in range(4):
            src = a_results[b * 4 + r]
            sl = slice(r * TPC, (r + 1) * TPC)
            for j in range(4):
                h = h0 + j
                qt[j][:, sl] = src["qt_o"][h * 128:(h + 1) * 128, :]
                kt[j][:, sl] = src["kt_o"][h * 128:(h + 1) * 128, :]
                v[j][sl, :] = src["v_o"][:, h * 128:(h + 1) * 128]
        maps.append({"qt_d": qt, "kt_d": kt, "v_d": v, "mask_d": m4})
    return maps




def emit_l0_epi(ctx, o_dram, sg_dram, normw_sb, gatedT, tag="ep0"):
    ot = [ctx.sbuf(f"{tag}_ot{i}", [128, D], BF16) for i in range(2)]
    sgt = [ctx.sbuf(f"{tag}_sg{i}", [128, D], BF16) for i in range(2)]
    gat = [ctx.sbuf(f"{tag}_ga{i}", [128, D], BF16) for i in range(2)]
    scr = ctx.sbuf(f"{tag}_scr", [128, 128], F32)
    nws = ctx.sbuf(f"{tag}_nws", [128, 128], F32)
    ssq = ctx.sbuf(f"{tag}_ssq", [128, NH], F32)
    rstd = ctx.sbuf(f"{tag}_rstd", [128, NH], F32)
    tp = [ctx.psum(f"{tag}_tp{i}", [128, 128], BF16) for i in range(2)]
    ident = ctx.ident
    for i in range(NT):
        b = i % 2
        if i >= 2:
            ctx.wait("sync", f"{tag}_ga", (i - 1) * NH)
            ctx.wait("sync", f"{tag}_sq", (i - 1) * NH)
        ctx.dma("sync", ot[b][:, :], o_dram[i * 128:(i + 1) * 128, :], f"{tag}_old")
        ctx.dma("sync", sgt[b][:, :], sg_dram[i * 128:(i + 1) * 128, :], f"{tag}_old")
        lv = ctx.v(f"{tag}_old")
        ctx.wait("scalar", f"{tag}_old", lv)
        if i >= 1:
            ctx.wait("scalar", f"{tag}_rc", i)
        for h in range(NH):
            ctx.op("scalar", lambda e, b=b, h=h: e.activation(
                scr[:, :], ot[b][:, h * 128:(h + 1) * 128], AF.Square,
                accum_out=ssq[:, h:h + 1]),
                inc=(f"{tag}_sq", 1))
        ctx.wait("scalar", f"{tag}_sq", ctx.v(f"{tag}_sq"))
        ctx.op("scalar", lambda e: e.activation(ssq[:, :], ssq[:, :], AF.Sqrt,
                                                scale=np.float32(1.0 / HD),
                                                bias=np.float32(1e-5)),
               inc=(f"{tag}_sqa", 1))
        ctx.wait("vector", f"{tag}_sqa", ctx.v(f"{tag}_sqa"))
        ctx.op("vector", lambda e: e.reciprocal(rstd[:, :], ssq[:, :]),
               inc=(f"{tag}_rc", 1))
        rcv = ctx.v(f"{tag}_rc")
        ctx.wait("vector", f"{tag}_rc", rcv)
        ctx.wait("vector", f"{tag}_old", lv)
        if i >= 2:
            ctx.wait("vector", f"{tag}_tp", (i - 1) * NDC)
        for h in range(NH):
            ctx.op("vector", lambda e, b=b, h=h: e.tensor_mul(
                nws[:, :], normw_sb[:, :], sgt[b][:, h * 128:(h + 1) * 128]),
                inc=(f"{tag}_nw", 1))
            ctx.wait("vector", f"{tag}_nw", ctx.v(f"{tag}_nw"))
            ctx.op("vector", lambda e, b=b, h=h: e.scalar_tensor_tensor(
                gat[b][:, h * 128:(h + 1) * 128], ot[b][:, h * 128:(h + 1) * 128],
                rstd[:, h:h + 1], nws[:, :], OP.mult, OP.mult),
                inc=(f"{tag}_ga", 1))
        gav = ctx.v(f"{tag}_ga")
        ctx.wait("tensor", f"{tag}_ga", gav)
        for dc in range(NDC):
            g = i * NDC + dc
            pb = g % 2
            if g >= 2:
                ctx.wait("tensor", f"{tag}_tev", g - 1)
            ctx.op("tensor", lambda e, b=b, dc=dc, pb=pb: e.transpose(
                tp[pb][:, :], gat[b][:, dc * 128:(dc + 1) * 128], ident[:, :]),
                inc=(f"{tag}_tp", 1))
            ctx.wait("vector", f"{tag}_tp", ctx.v(f"{tag}_tp"))
            ctx.op("vector", lambda e, dc=dc, i=i, pb=pb: e.tensor_copy(
                gatedT[:, dc * TPC + i * 128: dc * TPC + (i + 1) * 128], tp[pb][:, :]),
                inc=(f"{tag}_tev", 1))


def emit_out_proj(ctx, aT, woT_c_dram, x_dram, fg_dram, xout_dram, tag):
    wsb = ctx.sbuf(f"{tag}_w", [128, NDC * 512], BF16)
    xsb = [ctx.sbuf(f"{tag}_x{i}", [128, 512], F32) for i in range(2)]
    fsb = [ctx.sbuf(f"{tag}_f{i}", [128, 512], F32) for i in range(2)]
    t1 = ctx.sbuf(f"{tag}_t1", [128, 512], F32)
    stg = [ctx.sbuf(f"{tag}_stg{i}", [128, 512], F32) for i in range(2)]
    ps = [ctx.psum(f"{tag}_ps{i}", [128, 512]) for i in range(2)]
    for cb in range(4):
        if cb > 0:
            ctx.wait("gpsimd", f"{tag}_mm", cb * NT)
        srcw = woT_c_dram[cb].rearrange("(dc p) c -> p dc c", p=128)
        ctx.dma("gpsimd", wsb[:, :].rearrange("p (dc c) -> p dc c", dc=NDC), srcw, f"{tag}_wld")
        ctx.wait("tensor", f"{tag}_wld", ctx.v(f"{tag}_wld"))
        for i in range(NT):
            g = cb * NT + i
            b = g % 2
            if g >= 2:
                ctx.wait("sync", f"{tag}_ev", g - 1)
            ctx.dma("sync", xsb[b][:, :], x_dram[i * 128:(i + 1) * 128, cb * 512:(cb + 1) * 512],
                    f"{tag}_xld")
            ctx.dma("sync", fsb[b][:, :], fg_dram[i * 128:(i + 1) * 128, cb * 512:(cb + 1) * 512],
                    f"{tag}_xld")
            xlv = ctx.v(f"{tag}_xld")
            if g >= 2:
                ctx.wait("tensor", f"{tag}_ev", g - 1)
            for dc in range(NDC):
                ctx.op("tensor", lambda e, dc=dc, i=i, b=b: e.matmul(
                    ps[b][:, :], aT[:, dc * TPC + i * 128: dc * TPC + (i + 1) * 128],
                    wsb[:, dc * 512:(dc + 1) * 512],
                    start=(dc == 0), stop=(dc == NDC - 1)),
                    inc=(f"{tag}_mm", 1) if dc == NDC - 1 else None)
            ctx.wait("vector", f"{tag}_mm", ctx.v(f"{tag}_mm"))
            ctx.wait("vector", f"{tag}_xld", xlv)
            if g >= 2:
                ctx.wait("vector", f"{tag}_st", 16 * (g - 1))
            ctx.op("vector", lambda e, b=b: e.tensor_mul(t1[:, :], ps[b][:, :], fsb[b][:, :]),
                   inc=(f"{tag}_t1", 1))
            ctx.wait("vector", f"{tag}_t1", ctx.v(f"{tag}_t1"))
            ctx.op("vector", lambda e, b=b: e.tensor_add(stg[b][:, :], t1[:, :], xsb[b][:, :]),
                   inc=(f"{tag}_ev", 1))
            ctx.wait("sync", f"{tag}_ev", ctx.v(f"{tag}_ev"))
            ctx.dma("sync", xout_dram[i * 128:(i + 1) * 128, cb * 512:(cb + 1) * 512],
                    stg[b][:, :], f"{tag}_st")


def emit_q_norope(ctx, xnT, wT_h_dram, out_dram, scale, tag):
    wsb = ctx.sbuf(f"{tag}_w", [128, NDC * 128], BF16)
    rop = [ctx.sbuf(f"{tag}_rop{i}", [128, TPC], BF16) for i in range(2)]
    ps = [ctx.psum(f"{tag}_ps{i}", [128, 512]) for i in range(2)]
    NSQB = TPC // 512
    for h in range(NH):
        if h > 0:
            ctx.wait("gpsimd", f"{tag}_mm", h * NSQB)
        src = wT_h_dram[h].rearrange("(dc p) c -> p dc c", p=128)
        ctx.dma("gpsimd", wsb[:, :].rearrange("p (dc c) -> p dc c", dc=NDC), src, f"{tag}_wld")
        ctx.wait("tensor", f"{tag}_wld", ctx.v(f"{tag}_wld"))
        rb = h % 2
        if h >= 2:
            ctx.wait("scalar", f"{tag}_qst", 16 * (h - 1))
        for sqb in range(NSQB):
            g = h * NSQB + sqb
            pb = g % 2
            if g >= 2:
                ctx.wait("tensor", f"{tag}_evac", g - 1)
            for dc in range(NDC):
                ctx.op("tensor", lambda e, dc=dc, sqb=sqb, pb=pb: e.matmul(
                    ps[pb][:, :], wsb[:, dc * 128:(dc + 1) * 128],
                    xnT[:, dc * TPC + sqb * 512: dc * TPC + sqb * 512 + 512],
                    start=(dc == 0), stop=(dc == NDC - 1)),
                    inc=(f"{tag}_mm", 1) if dc == NDC - 1 else None)
            ctx.wait("scalar", f"{tag}_mm", ctx.v(f"{tag}_mm"))
            if scale == 1.0:
                ctx.op("scalar", lambda e, sqb=sqb, pb=pb, rb=rb: e.copy(
                    rop[rb][:, sqb * 512:sqb * 512 + 512], ps[pb][:, :]),
                    inc=(f"{tag}_evac", 1))
            else:
                ctx.op("scalar", lambda e, sqb=sqb, pb=pb, rb=rb: e.activation(
                    rop[rb][:, sqb * 512:sqb * 512 + 512], ps[pb][:, :], AF.Identity,
                    scale=np.float32(scale)),
                    inc=(f"{tag}_evac", 1))
        ctx.wait("sync", f"{tag}_evac", ctx.v(f"{tag}_evac"))
        ctx.dma("sync", out_dram[h * 128:(h + 1) * 128, :], rop[rb][:, :], f"{tag}_qst")


def emit_ffn1(ctx, xnT, wg_dram, wu_dram, wd_dram, sg_sc, su_sc, sd_bc, fg_out, tag="f1"):
    NFC = INTER // 128
    wg_sb = ctx.sbuf(f"{tag}_wg", [128, NDC * 512], BF16)
    wu_sb = ctx.sbuf(f"{tag}_wu", [128, NDC * 512], BF16)
    wd_sb = ctx.sbuf(f"{tag}_wd", [128, NDC * 512], BF16)
    fuse = ctx.sbuf(f"{tag}_fuse", [128, NFC * 512], BF16)
    sgs = ctx.sbuf(f"{tag}_sgs", [128, 512], BF16)
    t1 = ctx.sbuf(f"{tag}_t1", [128, 512], F32)
    stg = [ctx.sbuf(f"{tag}_stg{i}", [128, 512], F32) for i in range(2)]
    psg = ctx.psum(f"{tag}_psg", [128, 512])
    psu = ctx.psum(f"{tag}_psu", [128, 512])
    psd = [ctx.psum(f"{tag}_psd{i}", [128, 512]) for i in range(4)]

    def loadw(dram_slice, dest, nch):
        srcw = dram_slice.rearrange("(c p) f -> p c f", p=128)
        ctx.dma("gpsimd", dest[:, :].rearrange("p (c f) -> p c f", c=nch), srcw, f"{tag}_wld")
        return ctx.v(f"{tag}_wld")

    for sqb in range(TPC // 512):
        for fs in range(16):
            sidx = sqb * 16 + fs
            if sidx > 0:
                ctx.wait("gpsimd", f"{tag}_gmm", sidx * 4)
                ctx.wait("gpsimd", f"{tag}_umm", sidx * 4)
            loadw(wg_dram[fs], wg_sb, NDC)
            v2 = loadw(wu_dram[fs], wu_sb, NDC)
            ctx.wait("tensor", f"{tag}_wld", v2)
            for fc in range(4):
                fcg = fs * 4 + fc
                git = sidx * 4 + fc
                if git >= 1:
                    ctx.wait("tensor", f"{tag}_fuse", git)
                for dc in range(NDC):
                    ctx.op("tensor", lambda e, dc=dc, fc=fc, sqb=sqb: e.matmul(
                        psg[:, :], wg_sb[:, dc * 512 + fc * 128: dc * 512 + (fc + 1) * 128],
                        xnT[:, dc * TPC + sqb * 512: dc * TPC + sqb * 512 + 512],
                        start=(dc == 0), stop=(dc == NDC - 1)),
                        inc=(f"{tag}_gmm", 1) if dc == NDC - 1 else None)
                for dc in range(NDC):
                    ctx.op("tensor", lambda e, dc=dc, fc=fc, sqb=sqb: e.matmul(
                        psu[:, :], wu_sb[:, dc * 512 + fc * 128: dc * 512 + (fc + 1) * 128],
                        xnT[:, dc * TPC + sqb * 512: dc * TPC + sqb * 512 + 512],
                        start=(dc == 0), stop=(dc == NDC - 1)),
                        inc=(f"{tag}_umm", 1) if dc == NDC - 1 else None)
                ctx.wait("scalar", f"{tag}_gmm", ctx.v(f"{tag}_gmm"))
                if git >= 1:
                    ctx.wait("scalar", f"{tag}_fuse", git)
                ctx.op("scalar", lambda e, fcg=fcg: e.activation(
                    sgs[:, :], psg[:, :], AF.Silu, scale=sg_sc[:, fcg:fcg + 1]),
                    inc=(f"{tag}_silu", 1))
                ctx.wait("vector", f"{tag}_silu", ctx.v(f"{tag}_silu"))
                ctx.wait("vector", f"{tag}_umm", ctx.v(f"{tag}_umm"))
                ctx.op("vector", lambda e, fcg=fcg: e.scalar_tensor_tensor(
                    fuse[:, fcg * 512:(fcg + 1) * 512], psu[:, :],
                    su_sc[:, fcg:fcg + 1], sgs[:, :], OP.mult, OP.mult),
                    inc=(f"{tag}_fuse", 1))
        fusev = ctx.v(f"{tag}_fuse")
        for ds in range(4):
            for fs2 in range(4):
                lidx = (sqb * 4 + ds) * 4 + fs2
                if lidx > 0:
                    ctx.wait("gpsimd", f"{tag}_dmm", lidx * 4)
                v3 = loadw(wd_dram[ds][fs2 * 2048:(fs2 + 1) * 2048, :], wd_sb, NDC)
                ctx.wait("tensor", f"{tag}_wld", v3)
                if fs2 == 0:
                    if ds == 0:
                        ctx.wait("tensor", f"{tag}_fuse", fusev)
                    ctx.wait("tensor", f"{tag}_ev", ctx.v(f"{tag}_ev"))
                for sqt in range(4):
                    for fcl in range(NDC):
                        fcg = fs2 * NDC + fcl
                        ctx.op("tensor", lambda e, fcg=fcg, sqt=sqt, fs2=fs2, fcl=fcl: e.matmul(
                            psd[sqt][:, :],
                            fuse[:, fcg * 512 + sqt * 128: fcg * 512 + (sqt + 1) * 128],
                            wd_sb[:, fcl * 512:(fcl + 1) * 512],
                            start=(fs2 == 0 and fcl == 0), stop=(fs2 == 3 and fcl == NDC - 1)),
                            inc=(f"{tag}_dmm", 1) if fcl == NDC - 1 else None)
            ctx.wait("vector", f"{tag}_dmm", ctx.v(f"{tag}_dmm"))
            for sqt in range(4):
                g = (sqb * 4 + ds) * 4 + sqt
                b = g % 2
                ctx.wait("vector", f"{tag}_ev", ctx.v(f"{tag}_ev"))
                ctx.op("vector", lambda e, sqt=sqt, ds=ds: e.tensor_mul(
                    t1[:, :], psd[sqt][:, :], sd_bc[:, ds * 512:(ds + 1) * 512]),
                    inc=(f"{tag}_t1", 1))
                ctx.wait("scalar", f"{tag}_t1", ctx.v(f"{tag}_t1"))
                if g >= 2:
                    ctx.wait("scalar", f"{tag}_fgst", 16 * (g - 1))
                ctx.op("scalar", lambda e, b=b: e.activation(stg[b][:, :], t1[:, :], AF.Silu),
                       inc=(f"{tag}_ev", 1))
                ctx.wait("sync", f"{tag}_ev", ctx.v(f"{tag}_ev"))
                row = sqb * 512 + sqt * 128
                ctx.dma("sync", fg_out[row:row + 128, ds * 512:(ds + 1) * 512], stg[b][:, :],
                        f"{tag}_fgst")


def build_launch_C():
    nc = bass.Bass("TRN2", target_bir_lowering=False)
    o_d = nc.dram_tensor("o_d", [TPC, D], BF16, kind="ExternalInput").ap()
    sg_d = nc.dram_tensor("sg_d", [TPC, D], BF16, kind="ExternalInput").ap()
    x_d = nc.dram_tensor("x_d", [TPC, D], F32, kind="ExternalInput").ap()
    fg_d = nc.dram_tensor("fg_d", [TPC, D], F32, kind="ExternalInput").ap()
    normw_d = nc.dram_tensor("normw_d", [128, 128], F32, kind="ExternalInput").ap()
    wo_c = nc.dram_tensor("wo_c", [4, D, 512], F32, kind="ExternalInput").ap()
    wq_h = nc.dram_tensor("wq_h", [NH, D, HD], F32, kind="ExternalInput").ap()
    wk_h = nc.dram_tensor("wk_h", [NH, D, HD], F32, kind="ExternalInput").ap()
    wv_c = nc.dram_tensor("wv_c", [4, D, 512], F32, kind="ExternalInput").ap()
    wg1_d = nc.dram_tensor("wg1_d", [16, D, 512], I8, kind="ExternalInput").ap()
    wu1_d = nc.dram_tensor("wu1_d", [16, D, 512], I8, kind="ExternalInput").ap()
    wd1_d = nc.dram_tensor("wd1_d", [4, INTER, 512], I8, kind="ExternalInput").ap()
    sg_sc_d = nc.dram_tensor("sg_sc_d", [128, 64], F32, kind="ExternalInput").ap()
    su_sc_d = nc.dram_tensor("su_sc_d", [128, 64], F32, kind="ExternalInput").ap()
    sd_bc_d = nc.dram_tensor("sd_bc_d", [128, D], F32, kind="ExternalInput").ap()
    identd = nc.dram_tensor("identd", [128, 128], BF16, kind="ExternalInput").ap()

    x1_o = nc.dram_tensor("x1_o", [TPC, D], F32, kind="ExternalOutput").ap()
    qt1_o = nc.dram_tensor("qt1_o", [D, TPC], BF16, kind="ExternalOutput").ap()
    kt1_o = nc.dram_tensor("kt1_o", [D, TPC], BF16, kind="ExternalOutput").ap()
    v1_o = nc.dram_tensor("v1_o", [TPC, D], BF16, kind="ExternalOutput").ap()
    fg1_o = nc.dram_tensor("fg1_o", [TPC, D], F32, kind="ExternalOutput").ap()
    xn1_o = nc.dram_tensor("xn1_o", [TPC, D], BF16, kind="ExternalOutput").ap()

    with ExitStack() as stack:
        ctx = Ctx(nc, stack)
        ctx.ident = ctx.sbuf("identsb", [128, 128], BF16)
        gatedT = ctx.sbuf("gatedT", [128, NDC * TPC], BF16)
        xn1T = ctx.sbuf("xn1T", [128, NDC * TPC], BF16)
        normw_sb = ctx.sbuf("normw_sb", [128, 128], F32)
        sg_sc = ctx.sbuf("sg_sc", [128, 64], F32)
        su_sc = ctx.sbuf("su_sc", [128, 64], F32)
        sd_bc = ctx.sbuf("sd_bc", [128, D], F32)
        ctx.dma("sync", ctx.ident[:, :], identd, "cld")
        ctx.dma("sync", normw_sb[:, :], normw_d, "cld")
        ctx.dma("sync", sg_sc[:, :], sg_sc_d, "cld")
        ctx.dma("sync", su_sc[:, :], su_sc_d, "cld")
        ctx.dma("sync", sd_bc[:, :], sd_bc_d, "cld")

        def phase(emitfn):
            ctx.stage()
            with ExitStack() as ph:
                ctx.push_scope(ph)
                emitfn()
                ctx.pop_scope()

        phase(lambda: emit_l0_epi(ctx, o_d, sg_d, normw_sb, gatedT, tag="ep0"))
        phase(lambda: emit_out_proj(ctx, gatedT, wo_c, x_d, fg_d, x1_o, tag="op0"))
        phase(lambda: emit_ln_xnT(ctx, x1_o, xn1T, xn_out_dram=xn1_o, tag="ln1"))
        phase(lambda: emit_q_norope(ctx, xn1T, wq_h, qt1_o, float(SCALE), tag="q1"))
        phase(lambda: emit_q_norope(ctx, xn1T, wk_h, kt1_o, 1.0, tag="k1"))
        phase(lambda: emit_vg_proj(ctx, xn1T, wv_c, v1_o, tag="v1", silu=False))
        phase(lambda: emit_ffn1(ctx, xn1T, wg1_d, wu1_d, wd1_d, sg_sc, su_sc, sd_bc,
                                fg1_o, tag="f1"))
        ctx.stage()
        for s_ in ("op0_st", "ln1_xnst", "q1_qst", "k1_qst", "v1_st", "f1_fgst"):
            ctx.wait("sync", s_, ctx.v(s_))
        ctx.finalize()
    return nc


def inmaps_C(inputs, w, c, a_results, b_results):
    x = inputs["L0_attn_hidden_states"]
    maps = []
    for core in range(NCORE):
        b, t0 = core // 4, (core % 4) * TPC
        O_tok = np.zeros((TPC, D), dtype=NPBF16)
        for hc in range(4):
            src = b_results[b * 4 + hc]["o_d"]
            for j in range(4):
                h = hc * 4 + j
                O_tok[:, h * 128:(h + 1) * 128] = src[j][t0:t0 + TPC, :]
        maps.append({
            "o_d": O_tok,
            "sg_d": a_results[core]["sg_o"],
            "fg_d": a_results[core]["fg_o"],
            "x_d": np.ascontiguousarray(x[b, t0:t0 + TPC, :]),
            "normw_d": w["l0_normw_t"],
            "wo_c": w["l0_woT_c"],
            "wq_h": w["l1_wqT_h"], "wk_h": w["l1_wkT_h"], "wv_c": w["l1_wvT_c"],
            "wg1_d": w["l1_wg_t"], "wu1_d": w["l1_wu_t"], "wd1_d": w["l1_wd_t"],
            "sg_sc_d": w["l1_sg_t"], "su_sc_d": w["l1_su_t"], "sd_bc_d": w["l1_sd_t"],
            "identd": c["ident"],
        })
    return maps




def emit_lam(ctx, lam_d, ones_d, nlam_b, tag="lam"):
    lamv = ctx.sbuf(f"{tag}_v", [1, 128], F32)
    lscr = ctx.sbuf(f"{tag}_scr", [1, 128], F32)
    lamt = ctx.sbuf(f"{tag}_t", [1, 8], F32)
    onesb = ctx.sbuf(f"{tag}_ones1", [1, 128], F32)
    ps_l = ctx.psum(f"{tag}_ps", [128, 4])
    ctx.dma("sync", onesb[:, :], ones_d, f"{tag}_ld")
    for idx, (qrow, krow) in enumerate(((0, 2), (1, 3))):
        ctx.dma("sync", lamv[:, :], lam_d[qrow:qrow + 1, :], f"{tag}_ld")
        ctx.dma("sync", lscr[:, :], lam_d[krow:krow + 1, :], f"{tag}_ld")
        lv = ctx.v(f"{tag}_ld")
        ctx.wait("vector", f"{tag}_ld", lv)
        ctx.op("vector", lambda e: e.tensor_mul(lamv[:, :], lamv[:, :], lscr[:, :]),
               inc=(f"{tag}_ch", 1))
        ctx.wait("vector", f"{tag}_ch", ctx.v(f"{tag}_ch"))
        ctx.op("vector", lambda e, idx=idx: e.reduce_sum(lamt[:, idx:idx + 1], lamv[:, :], AX.X),
               inc=(f"{tag}_ch", 1))
        ctx.wait("vector", f"{tag}_ch", ctx.v(f"{tag}_ch"))
        ctx.wait("sync", f"{tag}_ch", ctx.v(f"{tag}_ch"))
    ctx.wait("scalar", f"{tag}_ch", ctx.v(f"{tag}_ch"))
    ctx.op("scalar", lambda e: e.activation(lamt[:, 2:4], lamt[:, 0:2], AF.Exp),
           inc=(f"{tag}_sc", 1))
    ctx.wait("vector", f"{tag}_sc", ctx.v(f"{tag}_sc"))
    ctx.op("vector", lambda e: e.tensor_sub(lamt[:, 4:5], lamt[:, 3:4], lamt[:, 2:3]),
           inc=(f"{tag}_ch", 1))
    ctx.wait("vector", f"{tag}_ch", ctx.v(f"{tag}_ch"))
    ctx.op("vector", lambda e: e.tensor_single_scalar(lamt[:, 5:6], lamt[:, 4:5], -0.8, OP.add),
           inc=(f"{tag}_ch", 1))
    ctx.wait("tensor", f"{tag}_ch", ctx.v(f"{tag}_ch"))
    ctx.op("tensor", lambda e: e.matmul(ps_l[:, 0:1], onesb[:, :], lamt[:, 5:6],
                                        start=True, stop=True),
           inc=(f"{tag}_mm", 1))
    ctx.wait("vector", f"{tag}_mm", ctx.v(f"{tag}_mm"))
    ctx.op("vector", lambda e: e.tensor_copy(nlam_b[:, :], ps_l[:, 0:1]),
           inc=(f"{tag}_bc", 1))


def emit_attn_l1(ctx, qt_d, kt_d, v_d, nlam_b, o_d, tag="at1"):
    NKC = S // 128
    SQB = 256
    NSQB = S // SQB
    qt = [ctx.sbuf(f"{tag}_qt{i}", [128, S], BF16) for i in range(2)]
    kt = [ctx.sbuf(f"{tag}_kt{i}", [128, S], BF16) for i in range(2)]
    ve = [ctx.sbuf(f"{tag}_ve{i}", [128, NKC * 129], BF16) for i in range(2)]
    pexp = [ctx.sbuf(f"{tag}_px{i}", [128, 512], BF16) for i in range(2)]
    onrm = [ctx.sbuf(f"{tag}_on{i}", [128, 128], BF16) for i in range(2)]
    rr = ctx.sbuf(f"{tag}_rr", [128, 8], F32)
    oscr = ctx.sbuf(f"{tag}_oscr", [128, 128], F32)
    sscr = ctx.sbuf(f"{tag}_sscr", [128, 128], F32)
    ps_s1 = [ctx.psum(f"{tag}_ps1{i}", [128, 256]) for i in range(2)]
    ps_s2 = [ctx.psum(f"{tag}_ps2{i}", [128, 256]) for i in range(2)]
    ps_o1 = [ctx.psum(f"{tag}_po1{i}", [128, 132]) for i in range(2)]
    ps_o2 = [ctx.psum(f"{tag}_po2{i}", [128, 132]) for i in range(2)]
    for i in range(2):
        ctx.op("vector", lambda e, i=i: e.memset(
            ve[i][:, :].rearrange("p (c x) -> p c x", x=129)[:, :, 128:129], 1.0))
    ctx.nopinc("vector", f"{tag}_ones", 1)

    for j in range(4):
        hb = j % 2
        if j >= 2:
            ctx.wait("sync", f"{tag}_smm", (j - 1) * NSQB * NKC * 2)
            ctx.wait("sync", f"{tag}_av", (j - 1) * NSQB * NKC)
        ctx.dma("sync", qt[hb][:, :], qt_d[j], f"{tag}_qld")
        ctx.dma("sync", kt[hb][:, :], kt_d[j], f"{tag}_kld")
        vsrc = v_d[j].rearrange("(c p) d -> p c d", p=128)
        vdst = ve[hb][:, :].rearrange("p (c x) -> p c x", x=129)[:, :, 0:128]
        ctx.dma("sync", vdst, vsrc, f"{tag}_vld")
        ldv = (ctx.v(f"{tag}_qld"), ctx.v(f"{tag}_kld"), ctx.v(f"{tag}_vld"))
        ctx.wait("tensor", f"{tag}_qld", ldv[0])
        ctx.wait("tensor", f"{tag}_kld", ldv[1])
        ctx.wait("tensor", f"{tag}_vld", ldv[2])
        if j == 0:
            ctx.wait("tensor", f"{tag}_ones", 1)
        for sqb in range(NSQB):
            base_nrm = ctx.v(f"{tag}_nrm")
            ctx.wait("tensor", f"{tag}_nrm", base_nrm)
            for kc in range(NKC):
                sb = kc % 2
                if kc >= 2:
                    ctx.wait("tensor", f"{tag}_exp", ctx.v(f"{tag}_exp") - 1)
                ctx.op("tensor", lambda e, sb=sb, kc=kc, sqb=sqb, hb=hb: e.matmul(
                    ps_s1[sb][:, :], kt[hb][0:64, kc * 128:(kc + 1) * 128],
                    qt[hb][0:64, sqb * SQB:(sqb + 1) * SQB], start=True, stop=True),
                    inc=(f"{tag}_smm", 1))
                ctx.op("tensor", lambda e, sb=sb, kc=kc, sqb=sqb, hb=hb: e.matmul(
                    ps_s2[sb][:, :], kt[hb][64:128, kc * 128:(kc + 1) * 128],
                    qt[hb][64:128, sqb * SQB:(sqb + 1) * SQB], start=True, stop=True),
                    inc=(f"{tag}_smm", 1))
                smmv = ctx.v(f"{tag}_smm")
                ctx.wait("scalar", f"{tag}_smm", smmv)
                if kc >= 2:
                    ctx.wait("scalar", f"{tag}_av", ctx.v(f"{tag}_av") - 1)
                elif sqb > 0 or j > 0:
                    ctx.wait("scalar", f"{tag}_av", ctx.v(f"{tag}_av"))
                ctx.op("scalar", lambda e, sb=sb: e.activation(pexp[sb][:, 0:SQB], ps_s1[sb][:, :],
                                                               AF.Exp))
                ctx.op("scalar", lambda e, sb=sb: e.activation(pexp[sb][:, SQB:2 * SQB],
                                                               ps_s2[sb][:, :], AF.Exp),
                       inc=(f"{tag}_exp", 1))
                ctx.wait("tensor", f"{tag}_exp", ctx.v(f"{tag}_exp"))
                for st in range(2):
                    ctx.op("tensor", lambda e, st=st, sb=sb, kc=kc, hb=hb: e.matmul(
                        ps_o1[st][:, 0:129], pexp[sb][:, st * 128:(st + 1) * 128],
                        ve[hb][:, kc * 129:(kc + 1) * 129],
                        start=(kc == 0), stop=(kc == NKC - 1)), inc=None)
                for st in range(2):
                    ctx.op("tensor", lambda e, st=st, sb=sb, kc=kc, hb=hb: e.matmul(
                        ps_o2[st][:, 0:129], pexp[sb][:, SQB + st * 128: SQB + (st + 1) * 128],
                        ve[hb][:, kc * 129:(kc + 1) * 129],
                        start=(kc == 0), stop=(kc == NKC - 1)),
                        inc=(f"{tag}_av", 1) if st == 1 else None)
            avv = ctx.v(f"{tag}_av")
            ctx.wait("vector", f"{tag}_av", avv)
            for st in range(2):
                ctx.op("vector", lambda e, st=st: e.reciprocal(rr[:, 0:1], ps_o1[st][:, 128:129]),
                       inc=(f"{tag}_rc", 1))
                ctx.op("vector", lambda e, st=st: e.reciprocal(rr[:, 1:2], ps_o2[st][:, 128:129]),
                       inc=(f"{tag}_rc", 1))
                ctx.wait("vector", f"{tag}_rc", ctx.v(f"{tag}_rc"))
                ctx.op("vector", lambda e: e.tensor_mul(rr[:, 2:3], rr[:, 1:2], nlam_b[:, :]),
                       inc=(f"{tag}_rc", 1))
                ctx.wait("vector", f"{tag}_rc", ctx.v(f"{tag}_rc"))
                ctx.op("vector", lambda e, st=st: e.tensor_scalar(
                    oscr[:, :], ps_o1[st][:, 0:128], rr[:, 0:1], None, OP.mult),
                    inc=(f"{tag}_rc", 1))
                ctx.wait("vector", f"{tag}_rc", ctx.v(f"{tag}_rc"))
                ctx.op("vector", lambda e, st=st: e.scalar_tensor_tensor(
                    oscr[:, :], ps_o2[st][:, 0:128], rr[:, 2:3], oscr[:, :], OP.mult, OP.add),
                    inc=(f"{tag}_oc", 1))
                ocv = ctx.v(f"{tag}_oc")
                ctx.wait("scalar", f"{tag}_oc", ocv)
                ctx.op("scalar", lambda e: e.activation(sscr[:, :], oscr[:, :], AF.Square,
                                                        accum_out=rr[:, 3:4]),
                       inc=(f"{tag}_sq", 1))
                ctx.wait("scalar", f"{tag}_sq", ctx.v(f"{tag}_sq"))
                ctx.op("scalar", lambda e: e.activation(rr[:, 4:5], rr[:, 3:4], AF.Sqrt,
                                                        scale=np.float32(1.0 / HD),
                                                        bias=np.float32(1.1920928955078125e-07)),
                       inc=(f"{tag}_sqa", 1))
                ctx.wait("vector", f"{tag}_sqa", ctx.v(f"{tag}_sqa"))
                ctx.op("vector", lambda e: e.reciprocal(rr[:, 5:6], rr[:, 4:5]),
                       inc=(f"{tag}_rc", 1))
                ctx.wait("vector", f"{tag}_rc", ctx.v(f"{tag}_rc"))
                ctx.op("vector", lambda e: e.tensor_single_scalar(rr[:, 6:7], rr[:, 5:6], 0.2, OP.mult),
                       inc=(f"{tag}_rc", 1))
                ctx.wait("vector", f"{tag}_rc", ctx.v(f"{tag}_rc"))
                ob = st % 2
                g = ctx.v(f"{tag}_nrm")
                if g >= 2:
                    ctx.wait("vector", f"{tag}_ost", 16 * (g - 1))
                ctx.op("vector", lambda e, ob=ob: e.tensor_scalar(
                    onrm[ob][:, :], oscr[:, :], rr[:, 6:7], None, OP.mult),
                    inc=(f"{tag}_nrm", 1))
                ctx.wait("sync", f"{tag}_nrm", ctx.v(f"{tag}_nrm"))
                row = sqb * SQB + st * 128
                ctx.dma("sync", o_d[j][row:row + 128, :], onrm[ob][:, :], f"{tag}_ost")


def build_launch_D():
    nc = bass.Bass("TRN2", target_bir_lowering=False)
    qt_d = nc.dram_tensor("qt_d", [4, 128, S], BF16, kind="ExternalInput").ap()
    kt_d = nc.dram_tensor("kt_d", [4, 128, S], BF16, kind="ExternalInput").ap()
    v_d = nc.dram_tensor("v_d", [4, S, HD], BF16, kind="ExternalInput").ap()
    lam_d = nc.dram_tensor("lam_d", [4, 128], F32, kind="ExternalInput").ap()
    ones_d = nc.dram_tensor("ones_d", [1, 128], F32, kind="ExternalInput").ap()
    o_d = nc.dram_tensor("o_d", [4, S, HD], BF16, kind="ExternalOutput").ap()
    with ExitStack() as stack:
        ctx = Ctx(nc, stack)
        nlam_b = ctx.sbuf("nlam_b", [128, 1], F32)
        with ExitStack() as ph:
            ctx.push_scope(ph)
            emit_lam(ctx, lam_d, ones_d, nlam_b, tag="lam")
            ctx.pop_scope()
        ctx.stage()
        with ExitStack() as ph:
            ctx.push_scope(ph)
            emit_attn_l1(ctx, qt_d, kt_d, v_d, nlam_b, o_d, tag="at1")
            ctx.pop_scope()
        ctx.stage()
        ctx.wait("sync", "at1_ost", ctx.v("at1_ost"))
        ctx.finalize()
    return nc


def inmaps_D(w, c_results):
    lam = np.stack([w["l1_lambda_q1"][0], w["l1_lambda_q2"][0],
                    w["l1_lambda_k1"][0], w["l1_lambda_k2"][0]]).astype(np.float32)
    ones = np.ones((1, 128), dtype=np.float32)
    maps = []
    for core in range(NCORE):
        b, h0 = core // 4, (core % 4) * 4
        qt = np.zeros((4, 128, S), dtype=NPBF16)
        kt = np.zeros((4, 128, S), dtype=NPBF16)
        v = np.zeros((4, S, HD), dtype=NPBF16)
        for r in range(4):
            src = c_results[b * 4 + r]
            sl = slice(r * TPC, (r + 1) * TPC)
            for j in range(4):
                h = h0 + j
                qt[j][:, sl] = src["qt1_o"][h * 128:(h + 1) * 128, :]
                kt[j][:, sl] = src["kt1_o"][h * 128:(h + 1) * 128, :]
                v[j][sl, :] = src["v1_o"][:, h * 128:(h + 1) * 128]
        maps.append({"qt_d": qt, "kt_d": kt, "v_d": v, "lam_d": lam, "ones_d": ones})
    return maps


def build_launch_E():
    nc = bass.Bass("TRN2", target_bir_lowering=False)
    tT_d = nc.dram_tensor("tT_d", [D, TPC], BF16, kind="ExternalInput").ap()
    wo_c = nc.dram_tensor("wo_c", [4, D, 512], F32, kind="ExternalInput").ap()
    x_d = nc.dram_tensor("x_d", [TPC, D], F32, kind="ExternalInput").ap()
    fg_d = nc.dram_tensor("fg_d", [TPC, D], F32, kind="ExternalInput").ap()
    x2_o = nc.dram_tensor("x2_o", [TPC, D], F32, kind="ExternalOutput").ap()
    with ExitStack() as stack:
        ctx = Ctx(nc, stack)
        tT = ctx.sbuf("tT", [128, NDC * TPC], BF16)
        srcw = tT_d.rearrange("(dc p) c -> p dc c", p=128)
        ctx.dma("sync", tT[:, :].rearrange("p (dc c) -> p dc c", dc=NDC), srcw, "tld")
        ctx.wait("tensor", "tld", 16)
        ctx.stage()
        with ExitStack() as ph:
            ctx.push_scope(ph)
            emit_out_proj(ctx, tT, wo_c, x_d, fg_d, x2_o, tag="op1")
            ctx.pop_scope()
        ctx.stage()
        ctx.wait("sync", "op1_st", ctx.v("op1_st"))
        ctx.finalize()
    return nc


def inmaps_E(inputs, w, d_results, c_results):
    normed = np.zeros((B, NH, S, HD), dtype=NPBF16)
    for core in range(NCORE):
        b, h0 = core // 4, (core % 4) * 4
        for j in range(4):
            normed[b, h0 + j] = d_results[core]["o_d"][j]
    t = normed.reshape(B, NH * S, HD).transpose(0, 2, 1).reshape(B, S, D)
    maps = []
    for core in range(NCORE):
        b, t0 = core // 4, (core % 4) * TPC
        tT = np.ascontiguousarray(t[b, t0:t0 + TPC, :].T)
        maps.append({
            "tT_d": tT,
            "wo_c": w["l1_woT_c"],
            "x_d": c_results[core]["x1_o"],
            "fg_d": c_results[core]["fg1_o"],
        })
    return maps


def kernel(**inputs):
    c = host_consts()
    w = prep_weights(inputs)

    nc = build_launch_A()
    res = run_bass_kernel_spmd(nc, inmaps_A(inputs, w, c), core_ids=list(range(NCORE)))
    a_results = res.results

    nc = build_launch_B()
    res = run_bass_kernel_spmd(nc, inmaps_B(a_results, c), core_ids=list(range(NCORE)))
    b_results = res.results

    nc = build_launch_C()
    res = run_bass_kernel_spmd(nc, inmaps_C(inputs, w, c, a_results, b_results),
                               core_ids=list(range(NCORE)))
    c_results = res.results

    nc = build_launch_D()
    res = run_bass_kernel_spmd(nc, inmaps_D(w, c_results), core_ids=list(range(NCORE)))
    d_results = res.results

    nc = build_launch_E()
    res = run_bass_kernel_spmd(nc, inmaps_E(inputs, w, d_results, c_results),
                               core_ids=list(range(NCORE)))
    e_results = res.results

    out = np.zeros((B, S, D), dtype=np.float32)
    for core in range(NCORE):
        b, t0 = core // 4, (core % 4) * TPC
        out[b, t0:t0 + TPC, :] = e_results[core]["x2_o"]
    return out


# version 02
# speedup vs baseline: 1.5448x; 1.5448x over previous
import numpy as np
import ml_dtypes
from contextlib import ExitStack
import concourse.bass as bass
import concourse.mybir as mybir
from concourse.bass_utils import run_bass_kernel_spmd

F32 = mybir.dt.float32
BF16 = mybir.dt.bfloat16
I8 = mybir.dt.int8
AF = mybir.ActivationFunctionType
OP = mybir.AluOpType
AX = mybir.AxisListType
NPBF16 = ml_dtypes.bfloat16

B, S, D, NH, HD = 2, 4096, 2048, 16, 128
E, F, INTER = 8, 2048, 8192
NCORE = 8
TPC = 1024
NT = TPC // 128
NDC = D // 128
SCALE = np.float32(HD ** (-0.5))
GC = np.float32(1.0 / np.e)
GL = np.float32(0.5 - np.log(4.0 / 3.0))
GU = np.float32(0.5)

ENGS = ("tensor", "vector", "scalar", "sync", "gpsimd")


class Ctx:
    def __init__(self, nc, stack):
        self.nc = nc
        self.scopes = [stack]
        self.stages = []
        self.cur = None
        self.semh = {}
        self.semv = {}
        self.stage()

    def stage(self):
        self.cur = {e: [] for e in ENGS}
        self.stages.append(self.cur)

    def push_scope(self, st):
        self.scopes.append(st)

    def pop_scope(self):
        self.scopes.pop()

    def finalize(self):
        for st in self.stages:
            with self.nc.Block() as blk:
                for eng in ENGS:
                    lst = st[eng]
                    if lst:
                        getattr(blk, eng)(lambda e, lst=lst: [f(e) for f in lst])

    def sbuf(self, name, shape, dtype):
        return self.scopes[-1].enter_context(self.nc.sbuf_tensor(name, shape, dtype))

    def psum(self, name, shape, dtype=F32):
        return self.scopes[-1].enter_context(self.nc.psum_tensor(name, shape, dtype))

    def sem(self, name):
        if name not in self.semh:
            self.semh[name] = self.scopes[0].enter_context(self.nc.semaphore(name))
            self.semv[name] = 0
        return self.semh[name]

    def bump(self, name, by):
        self.sem(name)
        self.semv[name] += by
        return self.semv[name]

    def v(self, name):
        return self.semv.get(name, 0)

    def emit(self, eng, fn):
        self.cur[eng].append(fn)

    def wait(self, eng, name, val):
        if val <= 0:
            return
        h = self.sem(name)
        self.emit(eng, lambda e, h=h, val=val: e.wait_ge(h, val))

    def op(self, eng, fn, inc=None):
        if inc is None:
            self.emit(eng, fn)
            return None
        name, by = inc
        h = self.sem(name)
        v = self.bump(name, by)
        self.emit(eng, lambda e, fn=fn, h=h, by=by: fn(e).then_inc(h, by))
        return v

    def nopinc(self, eng, name, by=1):
        h = self.sem(name)
        v = self.bump(name, by)
        if eng in ("vector",):
            self.emit(eng, lambda e, h=h, by=by: e.engine_nop().then_inc(h, by))
        else:
            self.emit(eng, lambda e, h=h, by=by: e.nop().then_inc(h, by))
        return v

    def dma(self, eng, out_ap, in_ap, sem):
        h = self.sem(sem)
        v = self.bump(sem, 16)
        self.emit(eng, lambda e, o=out_ap, i=in_ap, h=h: e.dma_start(out=o, in_=i).then_inc(h, 16))
        return v


def host_consts():
    c = {}
    c["ident"] = np.eye(128, dtype=NPBF16)
    R = np.zeros((128, 128), dtype=np.float32)
    for i in range(64):
        R[i, i + 64] = -1.0
        R[i + 64, i] = 1.0
    c["rotT"] = np.ascontiguousarray(R.T).astype(NPBF16)
    inv_freq = 1.0 / (10000.0 ** (np.arange(0, HD, 2, dtype=np.float32) / HD))
    t = np.arange(S, dtype=np.float32)
    freqs = np.einsum("j,t->jt", inv_freq, t)
    emb = np.concatenate([freqs, freqs], axis=0)
    c["cosF"] = np.cos(emb).astype(np.float32)
    c["sinF"] = np.sin(emb).astype(np.float32)
    m4 = np.zeros((4, 128, 512), dtype=NPBF16)
    p = np.arange(128)[:, None]
    f = np.arange(512)[None, :]
    for j in range(4):
        m4[j] = (p + 128 * j <= f).astype(NPBF16)
    c["mask4"] = m4
    return c


def prep_weights(inp):
    w = {}
    for nm in ("wq", "wk"):
        wT = np.ascontiguousarray(inp[f"L0_attn_{nm}"].T)
        w[f"l0_{nm}T_h"] = np.ascontiguousarray(wT.reshape(D, NH, HD).transpose(1, 0, 2))
    for nm in ("wv", "wg"):
        wT = np.ascontiguousarray(inp[f"L0_attn_{nm}"].T)
        w[f"l0_{nm}T_c"] = np.ascontiguousarray(wT.reshape(D, 4, 512).transpose(1, 0, 2))
    woT = np.ascontiguousarray(inp["L0_attn_wo"].T)
    w["l0_woT_c"] = np.ascontiguousarray(woT.reshape(D, 4, 512).transpose(1, 0, 2))
    w["l0_Wr_t"] = np.ascontiguousarray(
        inp["L0_ffn_Wr"].reshape(NDC, 128, E).transpose(1, 0, 2).reshape(128, NDC * E))
    w["l0_Wg_t"] = np.ascontiguousarray(inp["L0_ffn_Wg"].reshape(E, D, 4, 512).transpose(0, 2, 1, 3))
    w["l0_Wu_t"] = np.ascontiguousarray(inp["L0_ffn_Wu"].reshape(E, D, 4, 512).transpose(0, 2, 1, 3))
    w["l0_Wd_t"] = np.ascontiguousarray(inp["L0_ffn_Wd"].reshape(E, F, 4, 512).transpose(0, 2, 1, 3))
    for nm in ("wq", "wk"):
        wT = np.ascontiguousarray(inp[f"L1_attn_{nm}"].T)
        w[f"l1_{nm}T_h"] = np.ascontiguousarray(wT.reshape(D, NH, HD).transpose(1, 0, 2))
    wT = np.ascontiguousarray(inp["L1_attn_wv"].T)
    w["l1_wvT_c"] = np.ascontiguousarray(wT.reshape(D, 4, 512).transpose(1, 0, 2))
    woT1 = np.ascontiguousarray(inp["L1_attn_wo"].T)
    w["l1_woT_c"] = np.ascontiguousarray(woT1.reshape(D, 4, 512).transpose(1, 0, 2))
    w["l1_wg_t"] = np.ascontiguousarray(inp["L1_ffn_w_gate"].reshape(D, 16, 512).transpose(1, 0, 2))
    w["l1_wu_t"] = np.ascontiguousarray(inp["L1_ffn_w_up"].reshape(D, 16, 512).transpose(1, 0, 2))
    w["l1_wd_t"] = np.ascontiguousarray(inp["L1_ffn_w_down"].reshape(INTER, 4, 512).transpose(1, 0, 2))
    w["l1_sg_t"] = np.ascontiguousarray(inp["L1_ffn_s_gate"].reshape(64, 128).T)
    w["l1_su_t"] = np.ascontiguousarray(inp["L1_ffn_s_up"].reshape(64, 128).T)
    w["l1_sd_t"] = np.ascontiguousarray(np.tile(inp["L1_ffn_s_down"][None, :], (128, 1)))
    w["l0_normw_t"] = np.ascontiguousarray(np.tile(inp["L0_attn_norm_w"][None, :], (128, 1)))
    w["l0_temp_t"] = np.ascontiguousarray(
        np.tile(np.asarray(inp["L0_ffn_temperature"], dtype=np.float32).reshape(1, 1), (128, 1)))
    for nm in ("lambda_q1", "lambda_q2", "lambda_k1", "lambda_k2"):
        w[f"l1_{nm}"] = np.ascontiguousarray(inp[f"L1_attn_{nm}"].reshape(1, HD))
    return w


def emit_ln_xnT(ctx, x_dram, xnT, xn_out_dram=None, tag="ln"):
    xt = [ctx.sbuf(f"{tag}_xt{i}", [128, D], F32) for i in range(2)]
    xnb = [ctx.sbuf(f"{tag}_xnb{i}", [128, D], BF16) for i in range(2)]
    scr = ctx.sbuf(f"{tag}_scr", [128, D], F32)
    st = [ctx.sbuf(f"{tag}_st{i}", [128, 8], F32) for i in range(2)]
    tp = [ctx.psum(f"{tag}_tp{i}", [128, 128], BF16) for i in range(2)]
    ident = ctx.ident
    for i in range(NT):
        b = i % 2
        if i >= 2:
            ctx.wait("sync", f"{tag}_xn", i - 1)
            ctx.wait("sync", f"{tag}_sq", i - 1)
        ctx.dma("sync", xt[b][:, :], x_dram[i * 128:(i + 1) * 128, :], f"{tag}_xld")
        ctx.wait("scalar", f"{tag}_xld", 16 * (i + 1))
        sti = st[b]
        if i >= 2:
            ctx.wait("scalar", f"{tag}_rc", i - 1)
            ctx.wait("scalar", f"{tag}_xn", i - 1)
        ctx.wait("vector", f"{tag}_xld", 16 * (i + 1))
        ctx.op("vector", lambda e, b=b, sti=sti: e.reduce_sum(sti[:, 0:1], xt[b][:, :], AX.X),
               inc=(f"{tag}_rs", 1))
        ctx.op("scalar", lambda e, b=b, sti=sti: e.activation(scr[:, :], xt[b][:, :], AF.Square,
                                                              accum_out=sti[:, 1:2]),
               inc=(f"{tag}_sq", 1))
        ctx.wait("scalar", f"{tag}_sq", ctx.v(f"{tag}_sq"))
        ctx.wait("scalar", f"{tag}_rs", ctx.v(f"{tag}_rs"))
        ctx.op("scalar", lambda e, sti=sti: e.mul(sti[:, 2:3], sti[:, 0:1], 1.0 / D),
               inc=(f"{tag}_sch", 1))
        ctx.wait("scalar", f"{tag}_sch", ctx.v(f"{tag}_sch"))
        ctx.op("scalar", lambda e, sti=sti: e.mul(sti[:, 3:4], sti[:, 2:3], sti[:, 2:3]),
               inc=(f"{tag}_sch", 1))
        ctx.wait("scalar", f"{tag}_sch", ctx.v(f"{tag}_sch"))
        ctx.op("scalar", lambda e, sti=sti: e.activation(sti[:, 3:4], sti[:, 3:4], AF.Identity,
                                                         scale=np.float32(-1.0),
                                                         bias=np.float32(1e-5)),
               inc=(f"{tag}_sch", 1))
        ctx.wait("scalar", f"{tag}_sch", ctx.v(f"{tag}_sch"))
        ctx.op("scalar", lambda e, sti=sti: e.activation(sti[:, 4:5], sti[:, 1:2], AF.Sqrt,
                                                         scale=np.float32(1.0 / D),
                                                         bias=sti[:, 3:4]),
               inc=(f"{tag}_sqa", 1))
        ctx.wait("vector", f"{tag}_sqa", ctx.v(f"{tag}_sqa"))
        ctx.op("vector", lambda e, sti=sti: e.reciprocal(sti[:, 5:6], sti[:, 4:5]),
               inc=(f"{tag}_rc", 1))
        rcv = ctx.v(f"{tag}_rc")
        ctx.wait("scalar", f"{tag}_rc", rcv)
        ctx.op("scalar", lambda e, sti=sti: e.mul(sti[:, 6:7], sti[:, 2:3], sti[:, 5:6]),
               inc=(f"{tag}_mrs", 1))
        if i >= 2:
            ctx.wait("vector", f"{tag}_tp", (i - 1) * NDC)
            if xn_out_dram is not None:
                ctx.wait("vector", f"{tag}_xnst", 16 * (i - 1))
        ctx.wait("vector", f"{tag}_mrs", ctx.v(f"{tag}_mrs"))
        ctx.wait("vector", f"{tag}_rc", rcv)
        ctx.wait("vector", f"{tag}_xld", 16 * (i + 1))
        ctx.op("vector", lambda e, b=b, sti=sti: e.tensor_scalar(xnb[b][:, :], xt[b][:, :],
                                                                 sti[:, 5:6], sti[:, 6:7],
                                                                 OP.mult, OP.subtract),
               inc=(f"{tag}_xn", 1))
        xnv = ctx.v(f"{tag}_xn")
        if xn_out_dram is not None:
            ctx.wait("sync", f"{tag}_xn", xnv)
            ctx.dma("sync", xn_out_dram[i * 128:(i + 1) * 128, :], xnb[b][:, :], f"{tag}_xnst")
        ctx.wait("tensor", f"{tag}_xn", xnv)
        for dc in range(NDC):
            g = i * NDC + dc
            pb = g % 2
            if g >= 2:
                ctx.wait("tensor", f"{tag}_tev", g - 1)
            ctx.op("tensor", lambda e, b=b, dc=dc, pb=pb: e.transpose(
                tp[pb][:, :], xnb[b][:, dc * 128:(dc + 1) * 128], ident[:, :]),
                inc=(f"{tag}_tp", 1))
            ctx.wait("vector", f"{tag}_tp", ctx.v(f"{tag}_tp"))
            ctx.op("vector", lambda e, dc=dc, i=i, pb=pb: e.tensor_copy(
                xnT[:, dc * TPC + i * 128: dc * TPC + (i + 1) * 128], tp[pb][:, :]),
                inc=(f"{tag}_tev", 1))


def emit_qk_proj(ctx, xnT, wT_h_dram, out_dram, cos_sb, sin_sb, tag):
    wsb = [ctx.sbuf(f"{tag}_w{i}", [128, NDC * 128], BF16) for i in range(2)]
    qraw = ctx.sbuf(f"{tag}_qraw", [128, TPC], BF16)
    t1 = ctx.sbuf(f"{tag}_t1", [128, 512], F32)
    t2 = ctx.sbuf(f"{tag}_t2", [128, 512], F32)
    rop = [ctx.sbuf(f"{tag}_rop{i}", [128, TPC], BF16) for i in range(2)]
    ps = [ctx.psum(f"{tag}_ps{i}", [128, 512]) for i in range(2)]
    psr = ctx.psum(f"{tag}_psr", [128, 512])
    rotT = ctx.rotT
    NSQB = TPC // 512
    for h in range(NH):
        wb = h % 2
        if h >= 2:
            ctx.wait("gpsimd", f"{tag}_mm", (h - 1) * NSQB)
        src = wT_h_dram[h].rearrange("(dc p) c -> p dc c", p=128)
        ctx.dma("gpsimd", wsb[wb][:, :].rearrange("p (dc c) -> p dc c", dc=NDC), src, f"{tag}_wld")
        ctx.wait("tensor", f"{tag}_wld", 16 * (h + 1))
        for sqb in range(NSQB):
            g = h * NSQB + sqb
            pb = g % 2
            if g >= 2:
                ctx.wait("tensor", f"{tag}_evac", g - 1)
            for dc in range(NDC):
                ctx.op("tensor", lambda e, dc=dc, sqb=sqb, pb=pb, wb=wb: e.matmul(
                    ps[pb][:, :], wsb[wb][:, dc * 128:(dc + 1) * 128],
                    xnT[:, dc * TPC + sqb * 512: dc * TPC + sqb * 512 + 512],
                    start=(dc == 0), stop=(dc == NDC - 1)),
                    inc=(f"{tag}_mm", 1) if dc == NDC - 1 else None)
            mmv = ctx.v(f"{tag}_mm")
            ctx.wait("scalar", f"{tag}_mm", mmv)
            if h > 0:
                ctx.wait("scalar", f"{tag}_rope", (h - 1) * NSQB + sqb + 1)
            ctx.op("scalar", lambda e, sqb=sqb, pb=pb: e.copy(qraw[:, sqb * 512:sqb * 512 + 512],
                                                              ps[pb][:, :]),
                   inc=(f"{tag}_evac", 1))
            evv = ctx.v(f"{tag}_evac")
            ctx.wait("tensor", f"{tag}_evac", evv)
            if g >= 1:
                ctx.wait("tensor", f"{tag}_t2", g)
            ctx.op("tensor", lambda e, sqb=sqb: e.matmul(
                psr[:, :], rotT[:, :], qraw[:, sqb * 512:sqb * 512 + 512], start=True, stop=True),
                inc=(f"{tag}_rot", 1))
            rotv = ctx.v(f"{tag}_rot")
            ctx.wait("vector", f"{tag}_evac", evv)
            rb = h % 2
            if h >= 2 and sqb == 0:
                ctx.wait("vector", f"{tag}_qst", 16 * (h - 1))
            ctx.op("vector", lambda e, sqb=sqb: e.tensor_mul(
                t1[:, :], qraw[:, sqb * 512:sqb * 512 + 512], cos_sb[:, sqb * 512:sqb * 512 + 512]))
            ctx.wait("vector", f"{tag}_rot", rotv)
            ctx.op("vector", lambda e, sqb=sqb: e.tensor_mul(t2[:, :], psr[:, :],
                                                             sin_sb[:, sqb * 512:sqb * 512 + 512]),
                   inc=(f"{tag}_t2", 1))
            ctx.wait("vector", f"{tag}_t2", ctx.v(f"{tag}_t2"))
            ctx.op("vector", lambda e, rb=rb, sqb=sqb: e.tensor_add(
                rop[rb][:, sqb * 512:sqb * 512 + 512], t1[:, :], t2[:, :]),
                inc=(f"{tag}_rope", 1))
        ctx.wait("sync", f"{tag}_rope", ctx.v(f"{tag}_rope"))
        ctx.dma("sync", out_dram[h * 128:(h + 1) * 128, :], rop[h % 2][:, :], f"{tag}_qst")


def emit_vg_proj(ctx, xnT, wT_c_dram, out_dram, tag, silu=False):
    wsb = [ctx.sbuf(f"{tag}_w{i}", [128, NDC * 512], BF16) for i in range(2)]
    stg = [ctx.sbuf(f"{tag}_stg{i}", [128, 512], BF16) for i in range(2)]
    ps = [ctx.psum(f"{tag}_ps{i}", [128, 512]) for i in range(2)]
    for cb in range(4):
        wb = cb % 2
        if cb >= 2:
            ctx.wait("gpsimd", f"{tag}_mm", (cb - 1) * NT)
        src = wT_c_dram[cb].rearrange("(dc p) c -> p dc c", p=128)
        ctx.dma("gpsimd", wsb[wb][:, :].rearrange("p (dc c) -> p dc c", dc=NDC), src, f"{tag}_wld")
        ctx.wait("tensor", f"{tag}_wld", 16 * (cb + 1))
        for i in range(NT):
            g = cb * NT + i
            pb = g % 2
            if g >= 2:
                ctx.wait("tensor", f"{tag}_evac", g - 1)
            for dc in range(NDC):
                ctx.op("tensor", lambda e, dc=dc, i=i, pb=pb, wb=wb: e.matmul(
                    ps[pb][:, :], xnT[:, dc * TPC + i * 128: dc * TPC + (i + 1) * 128],
                    wsb[wb][:, dc * 512:(dc + 1) * 512],
                    start=(dc == 0), stop=(dc == NDC - 1)),
                    inc=(f"{tag}_mm", 1) if dc == NDC - 1 else None)
            ctx.wait("scalar", f"{tag}_mm", ctx.v(f"{tag}_mm"))
            if g >= 2:
                ctx.wait("scalar", f"{tag}_st", 16 * (g - 1))
            sb = g % 2
            if silu:
                ctx.op("scalar", lambda e, sb=sb, pb=pb: e.activation(stg[sb][:, :], ps[pb][:, :], AF.Silu),
                       inc=(f"{tag}_evac", 1))
            else:
                ctx.op("scalar", lambda e, sb=sb, pb=pb: e.copy(stg[sb][:, :], ps[pb][:, :]),
                       inc=(f"{tag}_evac", 1))
            ctx.wait("sync", f"{tag}_evac", ctx.v(f"{tag}_evac"))
            ctx.dma("sync", out_dram[i * 128:(i + 1) * 128, cb * 512:(cb + 1) * 512], stg[sb][:, :],
                    f"{tag}_st")


def emit_router(ctx, xnT, wr_sb, rtemp, wfinal, tag="rt", dbg=None):
    ps = ctx.psum(f"{tag}_ps", [128, 16])
    inh = ctx.sbuf(f"{tag}_inh", [128, 8], F32)
    dist = ctx.sbuf(f"{tag}_dist", [128, 8], F32)
    zone = ctx.sbuf(f"{tag}_zone", [128, 8], F32)
    z2 = ctx.sbuf(f"{tag}_z2", [128, 8], F32)
    wz = ctx.sbuf(f"{tag}_wz", [128, 8], F32)
    fb = ctx.sbuf(f"{tag}_fb", [128, 8], F32)
    fbx = ctx.sbuf(f"{tag}_fbx", [128, 8], F32)
    m = ctx.sbuf(f"{tag}_m", [128, 8], F32)
    mask1 = ctx.sbuf(f"{tag}_mask1", [128, 8], F32)
    mask2 = ctx.sbuf(f"{tag}_mask2", [128, 8], F32)
    ctx.wait("tensor", "cwld", 16)

    def vch(fn):
        v = ctx.op("vector", fn, inc=(f"{tag}_vch", 1))
        ctx.wait("vector", f"{tag}_vch", v)

    for i in range(NT):
        if i > 0:
            ctx.wait("tensor", f"{tag}_sg", i)
        for dc in range(NDC):
            ctx.op("tensor", lambda e, dc=dc, i=i: e.matmul(
                ps[:, 0:8], xnT[:, dc * TPC + i * 128: dc * TPC + (i + 1) * 128],
                wr_sb[:, dc * 8:(dc + 1) * 8],
                start=(dc == 0), stop=(dc == NDC - 1)),
                inc=(f"{tag}_mm", 1) if dc == NDC - 1 else None)
        ctx.wait("scalar", f"{tag}_mm", ctx.v(f"{tag}_mm"))
        if i > 0:
            ctx.wait("scalar", f"{tag}_done", i)
        ctx.op("scalar", lambda e: e.activation(inh[:, :], ps[:, 0:8], AF.Sigmoid, scale=rtemp[:, 0:1]),
               inc=(f"{tag}_sg", 1))
        ctx.wait("scalar", f"{tag}_sg", ctx.v(f"{tag}_sg"))
        if dbg is not None:
            ctx.op("scalar", lambda e, i=i: e.copy(dbg[:, i * 8:(i + 1) * 8], inh[:, :]))
        ctx.op("scalar", lambda e: e.activation(dist[:, :], inh[:, :], AF.Abs, bias=np.float32(-GC)))
        ctx.nopinc("scalar", f"{tag}_sch", 1)
        ctx.wait("scalar", f"{tag}_sch", ctx.v(f"{tag}_sch"))
        ctx.nopinc("scalar", f"{tag}_sch", 1)
        ctx.wait("scalar", f"{tag}_sch", ctx.v(f"{tag}_sch"))
        ctx.op("scalar", lambda e: e.activation(wz[:, :], dist[:, :], AF.Exp, scale=np.float32(-10.0)))
        ctx.op("scalar", lambda e: e.activation(fb[:, :], dist[:, :], AF.Exp,
                                                scale=np.float32(-1.0 / 0.3)),
               inc=(f"{tag}_sc", 1))
        ctx.wait("vector", f"{tag}_sc", ctx.v(f"{tag}_sc"))
        vch(lambda e: e.tensor_single_scalar(zone[:, :], inh[:, :], float(GL), OP.is_ge))
        vch(lambda e: e.tensor_single_scalar(z2[:, :], inh[:, :], float(GU), OP.is_le))
        vch(lambda e: e.tensor_mul(zone[:, :], zone[:, :], z2[:, :]))
        vch(lambda e: e.tensor_mul(wz[:, :], wz[:, :], zone[:, :]))
        vch(lambda e: e.reduce_sum(m[:, 0:1], wz[:, :], AX.X))
        vch(lambda e: e.tensor_single_scalar(m[:, 1:2], m[:, 0:1], 1e-8, OP.is_lt))
        vch(lambda e: e.reduce_max(m[:, 2:3], fb[:, :], AX.X))
        vch(lambda e: e.tensor_scalar(mask1[:, :], fb[:, :], m[:, 2:3], None, OP.is_equal))
        vch(lambda e: e.scalar_tensor_tensor(fbx[:, :], mask1[:, :], -1e30, fb[:, :],
                                             OP.mult, OP.add))
        vch(lambda e: e.reduce_max(m[:, 3:4], fbx[:, :], AX.X))
        vch(lambda e: e.tensor_scalar(mask2[:, :], fbx[:, :], m[:, 3:4], None, OP.is_equal))
        vch(lambda e: e.tensor_add(mask1[:, :], mask1[:, :], mask2[:, :]))
        vch(lambda e: e.tensor_mul(mask1[:, :], fb[:, :], mask1[:, :]))
        vch(lambda e: e.reduce_sum(m[:, 4:5], mask1[:, :], AX.X))
        vch(lambda e: e.tensor_single_scalar(m[:, 4:5], m[:, 4:5], 1e-8, OP.max))
        vch(lambda e: e.reciprocal(m[:, 5:6], m[:, 4:5]))
        vch(lambda e: e.tensor_scalar(mask1[:, :], mask1[:, :], m[:, 5:6], None, OP.mult))
        vch(lambda e: e.scalar_tensor_tensor(wz[:, :], mask1[:, :], m[:, 1:2], wz[:, :],
                                             OP.mult, OP.add))
        vch(lambda e: e.reduce_sum(m[:, 6:7], wz[:, :], AX.X))
        vch(lambda e: e.tensor_single_scalar(m[:, 6:7], m[:, 6:7], 1e-8, OP.max))
        vch(lambda e: e.reciprocal(m[:, 7:8], m[:, 6:7]))
        ctx.op("vector", lambda e, i=i: e.tensor_scalar(wfinal[:, i * 8:(i + 1) * 8], wz[:, :],
                                                        m[:, 7:8], None, OP.mult),
               inc=(f"{tag}_done", 1))


def emit_moe(ctx, xnT, wg_dram, wu_dram, wd_dram, wfinal, fg_out_dram, tag="moe"):
    wgp = [ctx.sbuf(f"{tag}_wg{i}", [128, NDC * 128], BF16) for i in range(2)]
    wup = [ctx.sbuf(f"{tag}_wu{i}", [128, NDC * 128], BF16) for i in range(2)]
    wd_sb = [ctx.sbuf(f"{tag}_wd{i}", [128, NDC * 512], BF16) for i in range(2)]
    fuseT = ctx.sbuf(f"{tag}_fuseT", [128, NDC * TPC], BF16)
    sg_scr = [ctx.sbuf(f"{tag}_sg{i}", [128, 512], BF16) for i in range(2)]
    acc = ctx.sbuf(f"{tag}_acc", [128, NT * D], F32)
    fg_stg = [ctx.sbuf(f"{tag}_fg{i}", [128, 512], F32) for i in range(2)]
    psg = [ctx.psum(f"{tag}_psg{i}", [128, 512]) for i in range(2)]
    psu = [ctx.psum(f"{tag}_psu{i}", [128, 512]) for i in range(2)]
    psd = [ctx.psum(f"{tag}_psd{i}", [128, 512]) for i in range(2)]
    NSQB = TPC // 512

    for e_i in range(E):
        for fs in range(4):
            for fc in range(4):
                fcg = fs * 4 + fc
                pidx = (e_i * 4 + fs) * 4 + fc
                pb = pidx % 2
                if pidx >= 2:
                    ctx.wait("gpsimd", f"{tag}_gmm", 2 * (pidx - 1))
                    ctx.wait("gpsimd", f"{tag}_umm", 2 * (pidx - 1))
                srcg = wg_dram[e_i, fs][:, fc * 128:(fc + 1) * 128].rearrange(
                    "(c p) f -> p c f", p=128)
                ctx.dma("gpsimd", wgp[pb][:, :].rearrange("p (c f) -> p c f", c=NDC),
                        srcg, f"{tag}_wgld")
                srcu = wu_dram[e_i, fs][:, fc * 128:(fc + 1) * 128].rearrange(
                    "(c p) f -> p c f", p=128)
                ctx.dma("gpsimd", wup[pb][:, :].rearrange("p (c f) -> p c f", c=NDC),
                        srcu, f"{tag}_wuld")
                ctx.wait("tensor", f"{tag}_wgld", 16 * (pidx + 1))
                ctx.wait("tensor", f"{tag}_wuld", 16 * (pidx + 1))
                for sqb in range(NSQB):
                    git = e_i * 16 * NSQB + fcg * NSQB + sqb
                    b = git % 2
                    if git >= 2:
                        ctx.wait("tensor", f"{tag}_silu", git - 1)
                        ctx.wait("tensor", f"{tag}_fuse", git - 1)
                    for dc in range(NDC):
                        ctx.op("tensor", lambda e, dc=dc, sqb=sqb, pb=pb, b=b: e.matmul(
                            psg[b][:, :], wgp[pb][:, dc * 128:(dc + 1) * 128],
                            xnT[:, dc * TPC + sqb * 512: dc * TPC + sqb * 512 + 512],
                            start=(dc == 0), stop=(dc == NDC - 1)),
                            inc=(f"{tag}_gmm", 1) if dc == NDC - 1 else None)
                    for dc in range(NDC):
                        ctx.op("tensor", lambda e, dc=dc, sqb=sqb, pb=pb, b=b: e.matmul(
                            psu[b][:, :], wup[pb][:, dc * 128:(dc + 1) * 128],
                            xnT[:, dc * TPC + sqb * 512: dc * TPC + sqb * 512 + 512],
                            start=(dc == 0), stop=(dc == NDC - 1)),
                            inc=(f"{tag}_umm", 1) if dc == NDC - 1 else None)
                    gv = ctx.v(f"{tag}_gmm")
                    uv = ctx.v(f"{tag}_umm")
                    ctx.wait("scalar", f"{tag}_gmm", gv)
                    if git >= 2:
                        ctx.wait("scalar", f"{tag}_fuse", git - 1)
                    ctx.op("scalar", lambda e, b=b: e.activation(sg_scr[b][:, :], psg[b][:, :],
                                                                 AF.Silu),
                           inc=(f"{tag}_silu", 1))
                    ctx.wait("vector", f"{tag}_silu", ctx.v(f"{tag}_silu"))
                    ctx.wait("vector", f"{tag}_umm", uv)
                    ctx.op("vector", lambda e, fcg=fcg, sqb=sqb, b=b: e.tensor_tensor(
                        fuseT[:, fcg * TPC + sqb * 512: fcg * TPC + sqb * 512 + 512],
                        psu[b][:, :], sg_scr[b][:, :], OP.mult),
                        inc=(f"{tag}_fuse", 1))
        fusev = ctx.v(f"{tag}_fuse")
        for ds in range(4):
            sidx = e_i * 4 + ds
            wb = sidx % 2
            if sidx >= 2:
                ctx.wait("gpsimd", f"{tag}_dmm", (sidx - 1) * NT)
            srcd = wd_dram[e_i, ds].rearrange("(c p) f -> p c f", p=128)
            ctx.dma("gpsimd", wd_sb[wb][:, :].rearrange("p (c f) -> p c f", c=NDC),
                    srcd, f"{tag}_wdld")
            ctx.wait("tensor", f"{tag}_wdld", 16 * (sidx + 1))
            if ds == 0:
                ctx.wait("tensor", f"{tag}_fuse", fusev)
            for i in range(NT):
                git = sidx * NT + i
                pb = git % 2
                if git >= 2:
                    ctx.wait("tensor", f"{tag}_dps", git - 1)
                for fcg in range(NDC):
                    ctx.op("tensor", lambda e, fcg=fcg, i=i, pb=pb, wb=wb: e.matmul(
                        psd[pb][:, :], fuseT[:, fcg * TPC + i * 128: fcg * TPC + (i + 1) * 128],
                        wd_sb[wb][:, fcg * 512:(fcg + 1) * 512],
                        start=(fcg == 0), stop=(fcg == NDC - 1)),
                        inc=(f"{tag}_dmm", 1) if fcg == NDC - 1 else None)
                ctx.wait("vector", f"{tag}_dmm", ctx.v(f"{tag}_dmm"))
                if e_i == 0:
                    ctx.op("vector", lambda e, i=i, ds=ds, pb=pb, e_i=e_i: e.tensor_scalar(
                        acc[:, i * D + ds * 512: i * D + (ds + 1) * 512], psd[pb][:, :],
                        wfinal[:, i * 8 + e_i: i * 8 + e_i + 1], None, OP.mult),
                        inc=(f"{tag}_dps", 1))
                else:
                    ctx.op("vector", lambda e, i=i, ds=ds, pb=pb, e_i=e_i: e.scalar_tensor_tensor(
                        acc[:, i * D + ds * 512: i * D + (ds + 1) * 512], psd[pb][:, :],
                        wfinal[:, i * 8 + e_i: i * 8 + e_i + 1],
                        acc[:, i * D + ds * 512: i * D + (ds + 1) * 512],
                        OP.mult, OP.add),
                        inc=(f"{tag}_dps", 1))
    ctx.wait("scalar", f"{tag}_dps", ctx.v(f"{tag}_dps"))
    for i in range(NT):
        for dsb in range(4):
            g = i * 4 + dsb
            b = g % 2
            if g >= 2:
                ctx.wait("scalar", f"{tag}_fgst", 16 * (g - 1))
            ctx.op("scalar", lambda e, i=i, dsb=dsb, b=b: e.activation(
                fg_stg[b][:, :], acc[:, i * D + dsb * 512: i * D + (dsb + 1) * 512], AF.Silu),
                inc=(f"{tag}_fgev", 1))
            ctx.wait("sync", f"{tag}_fgev", ctx.v(f"{tag}_fgev"))
            ctx.dma("sync", fg_out_dram[i * 128:(i + 1) * 128, dsb * 512:(dsb + 1) * 512],
                    fg_stg[b][:, :], f"{tag}_fgst")


def build_launch_A():
    nc = bass.Bass("TRN2", target_bir_lowering=False)
    x_tok = nc.dram_tensor("x_tok", [TPC, D], F32, kind="ExternalInput").ap()
    wq_h = nc.dram_tensor("wq_h", [NH, D, HD], F32, kind="ExternalInput").ap()
    wk_h = nc.dram_tensor("wk_h", [NH, D, HD], F32, kind="ExternalInput").ap()
    wv_c = nc.dram_tensor("wv_c", [4, D, 512], F32, kind="ExternalInput").ap()
    wg_c = nc.dram_tensor("wg_c", [4, D, 512], F32, kind="ExternalInput").ap()
    wr_d = nc.dram_tensor("wr_d", [128, NDC * E], F32, kind="ExternalInput").ap()
    temp_d = nc.dram_tensor("temp_d", [128, 1], F32, kind="ExternalInput").ap()
    mwg = nc.dram_tensor("mwg", [E, 4, D, 512], F32, kind="ExternalInput").ap()
    mwu = nc.dram_tensor("mwu", [E, 4, D, 512], F32, kind="ExternalInput").ap()
    mwd = nc.dram_tensor("mwd", [E, 4, F, 512], F32, kind="ExternalInput").ap()
    identd = nc.dram_tensor("identd", [128, 128], BF16, kind="ExternalInput").ap()
    rotTd = nc.dram_tensor("rotTd", [128, 128], BF16, kind="ExternalInput").ap()
    cosd = nc.dram_tensor("cosd", [128, TPC], F32, kind="ExternalInput").ap()
    sind = nc.dram_tensor("sind", [128, TPC], F32, kind="ExternalInput").ap()

    qt_o = nc.dram_tensor("qt_o", [D, TPC], BF16, kind="ExternalOutput").ap()
    kt_o = nc.dram_tensor("kt_o", [D, TPC], BF16, kind="ExternalOutput").ap()
    v_o = nc.dram_tensor("v_o", [TPC, D], BF16, kind="ExternalOutput").ap()
    sg_o = nc.dram_tensor("sg_o", [TPC, D], BF16, kind="ExternalOutput").ap()
    fg_o = nc.dram_tensor("fg_o", [TPC, D], F32, kind="ExternalOutput").ap()
    xn_o = nc.dram_tensor("xn_o", [TPC, D], BF16, kind="ExternalOutput").ap()
    wr_o = nc.dram_tensor("wr_o", [128, NT * E], F32, kind="ExternalOutput").ap()
    inh_o = nc.dram_tensor("inh_o", [128, 2 * NT * E], F32, kind="ExternalOutput").ap()

    with ExitStack() as stack:
        ctx = Ctx(nc, stack)
        xnT = ctx.sbuf("xnT", [128, NDC * TPC], BF16)
        ctx.ident = ctx.sbuf("identsb", [128, 128], BF16)
        ctx.rotT = ctx.sbuf("rotTsb", [128, 128], BF16)
        cosQ = ctx.sbuf("cosQ", [128, TPC], F32)
        sinQ = ctx.sbuf("sinQ", [128, TPC], F32)
        cosK = ctx.sbuf("cosK", [128, TPC], F32)
        sinK = ctx.sbuf("sinK", [128, TPC], F32)
        wr_sb = ctx.sbuf("wr_sb", [128, NDC * E], BF16)
        rtemp = ctx.sbuf("rtemp", [128, 1], F32)
        tempsb = ctx.sbuf("tempsb", [128, 1], F32)
        wfinal = ctx.sbuf("wfinal", [128, NT * E], F32)

        ctx.dma("sync", ctx.ident[:, :], identd, "cld")
        ctx.dma("sync", ctx.rotT[:, :], rotTd, "cld")
        ctx.dma("sync", cosK[:, :], cosd, "cld")
        ctx.dma("sync", sinK[:, :], sind, "cld")
        ctx.dma("sync", tempsb[:, :], temp_d, "cld")
        ctx.dma("gpsimd", wr_sb[:, :], wr_d, "cwld")
        ctx.wait("vector", "cld", ctx.v("cld"))
        ctx.op("vector", lambda e: e.tensor_single_scalar(cosQ[:, :], cosK[:, :], float(SCALE), OP.mult))
        ctx.op("vector", lambda e: e.tensor_single_scalar(sinQ[:, :], sinK[:, :], float(SCALE), OP.mult))
        ctx.op("vector", lambda e: e.reciprocal(rtemp[:, :], tempsb[:, :]))

        def phase(emitfn):
            ctx.stage()
            with ExitStack() as ph:
                ctx.push_scope(ph)
                emitfn()
                ctx.pop_scope()

        phase(lambda: emit_ln_xnT(ctx, x_tok, xnT, xn_out_dram=xn_o, tag="ln0"))
        phase(lambda: emit_qk_proj(ctx, xnT, wq_h, qt_o, cosQ, sinQ, tag="q0"))
        phase(lambda: emit_qk_proj(ctx, xnT, wk_h, kt_o, cosK, sinK, tag="k0"))
        phase(lambda: emit_vg_proj(ctx, xnT, wv_c, v_o, tag="v0", silu=False))
        phase(lambda: emit_vg_proj(ctx, xnT, wg_c, sg_o, tag="g0", silu=True))
        inhdbg = ctx.sbuf("inhdbg", [128, 2 * NT * E], F32)
        phase(lambda: emit_router(ctx, xnT, wr_sb, rtemp, wfinal, tag="rt", dbg=inhdbg))
        ctx.wait("sync", "rt_done", ctx.v("rt_done"))
        ctx.dma("sync", inh_o, inhdbg[:, :], "wr_st")
        ctx.wait("sync", "rt_done", ctx.v("rt_done"))
        ctx.dma("sync", wr_o, wfinal[:, :], "wr_st")
        phase(lambda: emit_moe(ctx, xnT, mwg, mwu, mwd, wfinal, fg_o, tag="moe"))

        ctx.stage()
        for s in ("ln0_xnst", "q0_qst", "k0_qst", "v0_st", "g0_st", "wr_st", "moe_fgst"):
            ctx.wait("sync", s, ctx.v(s))
        ctx.finalize()
    return nc


def inmaps_A(inputs, w, c):
    x = inputs["L0_attn_hidden_states"]
    maps = []
    for core in range(NCORE):
        b, t0 = core // 4, (core % 4) * TPC
        maps.append({
            "x_tok": np.ascontiguousarray(x[b, t0:t0 + TPC, :]),
            "wq_h": w["l0_wqT_h"], "wk_h": w["l0_wkT_h"],
            "wv_c": w["l0_wvT_c"], "wg_c": w["l0_wgT_c"],
            "wr_d": w["l0_Wr_t"], "temp_d": w["l0_temp_t"],
            "mwg": w["l0_Wg_t"], "mwu": w["l0_Wu_t"], "mwd": w["l0_Wd_t"],
            "identd": c["ident"], "rotTd": c["rotT"],
            "cosd": np.ascontiguousarray(c["cosF"][:, t0:t0 + TPC]),
            "sind": np.ascontiguousarray(c["sinF"][:, t0:t0 + TPC]),
        })
    return maps




def emit_attn_l0(ctx, qt_d, kt_d, v_d, mask_sb, o_d, tag="at0"):
    NKC = S // 128
    NSQB = S // 512
    GPH = sum(sqb + 1 for sqb in range(NSQB))
    qt = [ctx.sbuf(f"{tag}_qt{i}", [128, S], BF16) for i in range(2)]
    kt = [ctx.sbuf(f"{tag}_kt{i}", [128, S], BF16) for i in range(2)]
    ve = [ctx.sbuf(f"{tag}_ve{i}", [128, NKC * 129], BF16) for i in range(2)]
    pexp = [ctx.sbuf(f"{tag}_px{i}", [128, 2048], BF16) for i in range(2)]
    onrm = [ctx.sbuf(f"{tag}_on{i}", [128, 128], BF16) for i in range(2)]
    rden = ctx.sbuf(f"{tag}_rd", [128, 4], F32)
    ps_s = ctx.psum(f"{tag}_pss", [128, 2048])
    ps_o = [ctx.psum(f"{tag}_pso{i}", [128, 132]) for i in range(4)]

    def po_ap(par, st, a, b):
        return ps_o[st][:, a:b]
    for i in range(2):
        ctx.op("vector", lambda e, i=i: e.memset(
            ve[i][:, :].rearrange("p (c x) -> p c x", x=129)[:, :, 128:129], 1.0))
    ctx.nopinc("vector", f"{tag}_ones", 1)

    gtot = 0
    for j in range(4):
        hb = j % 2
        if j >= 2:
            ctx.wait("sync", f"{tag}_smm", (j - 1) * GPH)
            ctx.wait("sync", f"{tag}_av", (j - 1) * GPH)
        ctx.dma("sync", qt[hb][:, :], qt_d[j], f"{tag}_qld")
        ctx.dma("sync", kt[hb][:, :], kt_d[j], f"{tag}_kld")
        vsrc = v_d[j].rearrange("(c p) d -> p c d", p=128)
        vdst = ve[hb][:, :].rearrange("p (c x) -> p c x", x=129)[:, :, 0:128]
        ctx.dma("sync", vdst, vsrc, f"{tag}_vld")
        ldv = (ctx.v(f"{tag}_qld"), ctx.v(f"{tag}_kld"), ctx.v(f"{tag}_vld"))
        ctx.wait("tensor", f"{tag}_qld", ldv[0])
        ctx.wait("tensor", f"{tag}_kld", ldv[1])
        ctx.wait("tensor", f"{tag}_vld", ldv[2])
        if j == 0:
            ctx.wait("tensor", f"{tag}_ones", 1)
        for sqb in range(NSQB):
            ng = sqb + 1
            par = sqb % 2
            nrm_prev = ctx.v(f"{tag}_nrm")
            if nrm_prev > 0:
                ctx.wait("tensor", f"{tag}_nrm", nrm_prev)
            def emit_av(g, b):
                for kcl in range(4):
                    kc = 4 * g + kcl
                    for st in range(4):
                        ctx.op("tensor", lambda e, st=st, kcl=kcl, kc=kc, b=b,
                               hb=hb, g=g, ng=ng, par=par: e.matmul(
                            po_ap(par, st, 0, 129),
                            pexp[b][:, kcl * 512 + st * 128: kcl * 512 + (st + 1) * 128],
                            ve[hb][:, kc * 129:(kc + 1) * 129],
                            start=(g == 0 and kcl == 0), stop=(g == ng - 1 and kcl == 3)),
                            inc=(f"{tag}_av", 1) if (kcl == 3 and st == 3) else None)

            for g in range(ng):
                b = gtot % 2
                ctx.wait("tensor", f"{tag}_exp", ctx.v(f"{tag}_exp"))
                for kcl in range(4):
                    kc = 4 * g + kcl
                    ctx.op("tensor", lambda e, kcl=kcl, kc=kc, sqb=sqb, hb=hb: e.matmul(
                        ps_s[:, kcl * 512:(kcl + 1) * 512],
                        kt[hb][:, kc * 128:(kc + 1) * 128],
                        qt[hb][:, sqb * 512:sqb * 512 + 512], start=True, stop=True),
                        inc=(f"{tag}_smm", 1) if kcl == 3 else None)
                ctx.wait("scalar", f"{tag}_smm", ctx.v(f"{tag}_smm"))
                ctx.wait("scalar", f"{tag}_av", max(0, gtot - 1))
                for kcl in range(4):
                    ctx.op("scalar", lambda e, kcl=kcl, b=b: e.activation(
                        pexp[b][:, kcl * 512:(kcl + 1) * 512],
                        ps_s[:, kcl * 512:(kcl + 1) * 512], AF.Exp),
                        inc=(f"{tag}_exp", 1) if kcl == 3 else None)
                if g == sqb:
                    ctx.wait("vector", f"{tag}_exp", ctx.v(f"{tag}_exp"))
                    for kcl in range(4):
                        ctx.op("vector", lambda e, kcl=kcl, b=b: e.tensor_mul(
                            pexp[b][:, kcl * 512:(kcl + 1) * 512],
                            pexp[b][:, kcl * 512:(kcl + 1) * 512],
                            mask_sb[:, kcl * 512:(kcl + 1) * 512]),
                            inc=(f"{tag}_msk", 1) if kcl == 3 else None)
                if g > 0:
                    ctx.wait("tensor", f"{tag}_exp", ctx.v(f"{tag}_exp") - 1)
                    emit_av(g - 1, 1 - b)
                gtot += 1
            ctx.wait("tensor", f"{tag}_msk", ctx.v(f"{tag}_msk"))
            emit_av(ng - 1, (gtot - 1) % 2)
            avv = ctx.v(f"{tag}_av")
            ctx.wait("vector", f"{tag}_av", avv)
            for st in range(4):
                ctx.op("vector", lambda e, st=st, par=par: e.reciprocal(
                    rden[:, st:st + 1], po_ap(par, st, 128, 129)),
                    inc=(f"{tag}_rc", 1))
                ctx.wait("vector", f"{tag}_rc", ctx.v(f"{tag}_rc"))
                ob = st % 2
                if j > 0 or sqb > 0 or st >= 2:
                    ctx.wait("vector", f"{tag}_ost", max(0, 16 * (ctx.v(f"{tag}_nrm") - 1)))
                ctx.op("vector", lambda e, st=st, ob=ob, par=par: e.tensor_scalar(
                    onrm[ob][:, :], po_ap(par, st, 0, 128), rden[:, st:st + 1], None, OP.mult),
                    inc=(f"{tag}_nrm", 1))
                ctx.wait("sync", f"{tag}_nrm", ctx.v(f"{tag}_nrm"))
                ctx.dma("sync", o_d[j][sqb * 512 + st * 128: sqb * 512 + (st + 1) * 128, :],
                        onrm[ob][:, :], f"{tag}_ost")


def build_launch_B():
    nc = bass.Bass("TRN2", target_bir_lowering=False)
    qt_d = nc.dram_tensor("qt_d", [4, 128, S], BF16, kind="ExternalInput").ap()
    kt_d = nc.dram_tensor("kt_d", [4, 128, S], BF16, kind="ExternalInput").ap()
    v_d = nc.dram_tensor("v_d", [4, S, HD], BF16, kind="ExternalInput").ap()
    mask_d = nc.dram_tensor("mask_d", [128, 4 * 512], BF16, kind="ExternalInput").ap()
    o_d = nc.dram_tensor("o_d", [4, S, HD], BF16, kind="ExternalOutput").ap()
    with ExitStack() as stack:
        ctx = Ctx(nc, stack)
        mask_sb = ctx.sbuf("mask_sb", [128, 4 * 512], BF16)
        ctx.dma("sync", mask_sb[:, :], mask_d, "cld")
        ctx.wait("tensor", "cld", 16)
        ctx.wait("vector", "cld", 16)
        ctx.stage()
        emit_attn_l0(ctx, qt_d, kt_d, v_d, mask_sb, o_d, tag="at0")
        ctx.stage()
        ctx.wait("sync", "at0_ost", ctx.v("at0_ost"))
        ctx.finalize()
    return nc


def inmaps_B(a_results, c):
    maps = []
    m4 = np.ascontiguousarray(c["mask4"].transpose(1, 0, 2).reshape(128, 4 * 512))
    for core in range(NCORE):
        b, h0 = core // 4, (core % 4) * 4
        qt = np.zeros((4, 128, S), dtype=NPBF16)
        kt = np.zeros((4, 128, S), dtype=NPBF16)
        v = np.zeros((4, S, HD), dtype=NPBF16)
        for r in range(4):
            src = a_results[b * 4 + r]
            sl = slice(r * TPC, (r + 1) * TPC)
            for j in range(4):
                h = h0 + j
                qt[j][:, sl] = src["qt_o"][h * 128:(h + 1) * 128, :]
                kt[j][:, sl] = src["kt_o"][h * 128:(h + 1) * 128, :]
                v[j][sl, :] = src["v_o"][:, h * 128:(h + 1) * 128]
        maps.append({"qt_d": qt, "kt_d": kt, "v_d": v, "mask_d": m4})
    return maps




def emit_l0_epi(ctx, o_dram, sg_dram, normw_sb, gatedT, tag="ep0"):
    ot = [ctx.sbuf(f"{tag}_ot{i}", [128, D], BF16) for i in range(2)]
    sgt = [ctx.sbuf(f"{tag}_sg{i}", [128, D], BF16) for i in range(2)]
    gat = [ctx.sbuf(f"{tag}_ga{i}", [128, D], BF16) for i in range(2)]
    scr = ctx.sbuf(f"{tag}_scr", [128, 128], F32)
    nws = ctx.sbuf(f"{tag}_nws", [128, 128], F32)
    ssq = ctx.sbuf(f"{tag}_ssq", [128, NH], F32)
    rstd = ctx.sbuf(f"{tag}_rstd", [128, NH], F32)
    tp = [ctx.psum(f"{tag}_tp{i}", [128, 128], BF16) for i in range(2)]
    ident = ctx.ident
    for i in range(NT):
        b = i % 2
        if i >= 2:
            ctx.wait("sync", f"{tag}_ga", (i - 1) * NH)
            ctx.wait("sync", f"{tag}_sq", (i - 1) * NH)
        ctx.dma("sync", ot[b][:, :], o_dram[i * 128:(i + 1) * 128, :], f"{tag}_old")
        ctx.dma("sync", sgt[b][:, :], sg_dram[i * 128:(i + 1) * 128, :], f"{tag}_old")
        lv = ctx.v(f"{tag}_old")
        ctx.wait("scalar", f"{tag}_old", lv)
        if i >= 1:
            ctx.wait("scalar", f"{tag}_rc", i)
        for h in range(NH):
            ctx.op("scalar", lambda e, b=b, h=h: e.activation(
                scr[:, :], ot[b][:, h * 128:(h + 1) * 128], AF.Square,
                accum_out=ssq[:, h:h + 1]),
                inc=(f"{tag}_sq", 1))
        ctx.wait("scalar", f"{tag}_sq", ctx.v(f"{tag}_sq"))
        ctx.op("scalar", lambda e: e.activation(ssq[:, :], ssq[:, :], AF.Sqrt,
                                                scale=np.float32(1.0 / HD),
                                                bias=np.float32(1e-5)),
               inc=(f"{tag}_sqa", 1))
        ctx.wait("vector", f"{tag}_sqa", ctx.v(f"{tag}_sqa"))
        ctx.op("vector", lambda e: e.reciprocal(rstd[:, :], ssq[:, :]),
               inc=(f"{tag}_rc", 1))
        rcv = ctx.v(f"{tag}_rc")
        ctx.wait("vector", f"{tag}_rc", rcv)
        ctx.wait("vector", f"{tag}_old", lv)
        if i >= 2:
            ctx.wait("vector", f"{tag}_tp", (i - 1) * NDC)
        for h in range(NH):
            ctx.op("vector", lambda e, b=b, h=h: e.tensor_mul(
                nws[:, :], normw_sb[:, :], sgt[b][:, h * 128:(h + 1) * 128]),
                inc=(f"{tag}_nw", 1))
            ctx.wait("vector", f"{tag}_nw", ctx.v(f"{tag}_nw"))
            ctx.op("vector", lambda e, b=b, h=h: e.scalar_tensor_tensor(
                gat[b][:, h * 128:(h + 1) * 128], ot[b][:, h * 128:(h + 1) * 128],
                rstd[:, h:h + 1], nws[:, :], OP.mult, OP.mult),
                inc=(f"{tag}_ga", 1))
        gav = ctx.v(f"{tag}_ga")
        ctx.wait("tensor", f"{tag}_ga", gav)
        for dc in range(NDC):
            g = i * NDC + dc
            pb = g % 2
            if g >= 2:
                ctx.wait("tensor", f"{tag}_tev", g - 1)
            ctx.op("tensor", lambda e, b=b, dc=dc, pb=pb: e.transpose(
                tp[pb][:, :], gat[b][:, dc * 128:(dc + 1) * 128], ident[:, :]),
                inc=(f"{tag}_tp", 1))
            ctx.wait("vector", f"{tag}_tp", ctx.v(f"{tag}_tp"))
            ctx.op("vector", lambda e, dc=dc, i=i, pb=pb: e.tensor_copy(
                gatedT[:, dc * TPC + i * 128: dc * TPC + (i + 1) * 128], tp[pb][:, :]),
                inc=(f"{tag}_tev", 1))


def emit_out_proj(ctx, aT, woT_c_dram, x_dram, fg_dram, xout_dram, tag):
    wsb = ctx.sbuf(f"{tag}_w", [128, NDC * 512], BF16)
    xsb = [ctx.sbuf(f"{tag}_x{i}", [128, 512], F32) for i in range(2)]
    fsb = [ctx.sbuf(f"{tag}_f{i}", [128, 512], F32) for i in range(2)]
    t1 = ctx.sbuf(f"{tag}_t1", [128, 512], F32)
    stg = [ctx.sbuf(f"{tag}_stg{i}", [128, 512], F32) for i in range(2)]
    ps = [ctx.psum(f"{tag}_ps{i}", [128, 512]) for i in range(2)]
    for cb in range(4):
        if cb > 0:
            ctx.wait("gpsimd", f"{tag}_mm", cb * NT)
        srcw = woT_c_dram[cb].rearrange("(dc p) c -> p dc c", p=128)
        ctx.dma("gpsimd", wsb[:, :].rearrange("p (dc c) -> p dc c", dc=NDC), srcw, f"{tag}_wld")
        ctx.wait("tensor", f"{tag}_wld", ctx.v(f"{tag}_wld"))
        for i in range(NT):
            g = cb * NT + i
            b = g % 2
            if g >= 2:
                ctx.wait("sync", f"{tag}_ev", g - 1)
            ctx.dma("sync", xsb[b][:, :], x_dram[i * 128:(i + 1) * 128, cb * 512:(cb + 1) * 512],
                    f"{tag}_xld")
            ctx.dma("sync", fsb[b][:, :], fg_dram[i * 128:(i + 1) * 128, cb * 512:(cb + 1) * 512],
                    f"{tag}_xld")
            xlv = ctx.v(f"{tag}_xld")
            if g >= 2:
                ctx.wait("tensor", f"{tag}_ev", g - 1)
            for dc in range(NDC):
                ctx.op("tensor", lambda e, dc=dc, i=i, b=b: e.matmul(
                    ps[b][:, :], aT[:, dc * TPC + i * 128: dc * TPC + (i + 1) * 128],
                    wsb[:, dc * 512:(dc + 1) * 512],
                    start=(dc == 0), stop=(dc == NDC - 1)),
                    inc=(f"{tag}_mm", 1) if dc == NDC - 1 else None)
            ctx.wait("vector", f"{tag}_mm", ctx.v(f"{tag}_mm"))
            ctx.wait("vector", f"{tag}_xld", xlv)
            if g >= 2:
                ctx.wait("vector", f"{tag}_st", 16 * (g - 1))
            ctx.op("vector", lambda e, b=b: e.tensor_mul(t1[:, :], ps[b][:, :], fsb[b][:, :]),
                   inc=(f"{tag}_t1", 1))
            ctx.wait("vector", f"{tag}_t1", ctx.v(f"{tag}_t1"))
            ctx.op("vector", lambda e, b=b: e.tensor_add(stg[b][:, :], t1[:, :], xsb[b][:, :]),
                   inc=(f"{tag}_ev", 1))
            ctx.wait("sync", f"{tag}_ev", ctx.v(f"{tag}_ev"))
            ctx.dma("sync", xout_dram[i * 128:(i + 1) * 128, cb * 512:(cb + 1) * 512],
                    stg[b][:, :], f"{tag}_st")


def emit_q_norope(ctx, xnT, wT_h_dram, out_dram, scale, tag):
    wsb = [ctx.sbuf(f"{tag}_w{i}", [128, NDC * 128], BF16) for i in range(2)]
    rop = [ctx.sbuf(f"{tag}_rop{i}", [128, TPC], BF16) for i in range(2)]
    ps = [ctx.psum(f"{tag}_ps{i}", [128, 512]) for i in range(2)]
    NSQB = TPC // 512
    for h in range(NH):
        wb = h % 2
        if h >= 2:
            ctx.wait("gpsimd", f"{tag}_mm", (h - 1) * NSQB)
        src = wT_h_dram[h].rearrange("(dc p) c -> p dc c", p=128)
        ctx.dma("gpsimd", wsb[wb][:, :].rearrange("p (dc c) -> p dc c", dc=NDC), src, f"{tag}_wld")
        ctx.wait("tensor", f"{tag}_wld", 16 * (h + 1))
        rb = h % 2
        if h >= 2:
            ctx.wait("scalar", f"{tag}_qst", 16 * (h - 1))
        for sqb in range(NSQB):
            g = h * NSQB + sqb
            pb = g % 2
            if g >= 2:
                ctx.wait("tensor", f"{tag}_evac", g - 1)
            for dc in range(NDC):
                ctx.op("tensor", lambda e, dc=dc, sqb=sqb, pb=pb, wb=wb: e.matmul(
                    ps[pb][:, :], wsb[wb][:, dc * 128:(dc + 1) * 128],
                    xnT[:, dc * TPC + sqb * 512: dc * TPC + sqb * 512 + 512],
                    start=(dc == 0), stop=(dc == NDC - 1)),
                    inc=(f"{tag}_mm", 1) if dc == NDC - 1 else None)
            ctx.wait("scalar", f"{tag}_mm", ctx.v(f"{tag}_mm"))
            if scale == 1.0:
                ctx.op("scalar", lambda e, sqb=sqb, pb=pb, rb=rb: e.copy(
                    rop[rb][:, sqb * 512:sqb * 512 + 512], ps[pb][:, :]),
                    inc=(f"{tag}_evac", 1))
            else:
                ctx.op("scalar", lambda e, sqb=sqb, pb=pb, rb=rb: e.activation(
                    rop[rb][:, sqb * 512:sqb * 512 + 512], ps[pb][:, :], AF.Identity,
                    scale=np.float32(scale)),
                    inc=(f"{tag}_evac", 1))
        ctx.wait("sync", f"{tag}_evac", ctx.v(f"{tag}_evac"))
        ctx.dma("sync", out_dram[h * 128:(h + 1) * 128, :], rop[rb][:, :], f"{tag}_qst")


def emit_ffn1(ctx, xnT, wg_dram, wu_dram, wd_dram, sg_sc, su_sc, sd_bc, fg_out, tag="f1"):
    NFC = INTER // 128
    wgp = [ctx.sbuf(f"{tag}_wg{i}", [128, NDC * 128], BF16) for i in range(2)]
    wup = [ctx.sbuf(f"{tag}_wu{i}", [128, NDC * 128], BF16) for i in range(2)]
    wd_sb = [ctx.sbuf(f"{tag}_wd{i}", [128, NDC * 512], BF16) for i in range(2)]
    fuse = ctx.sbuf(f"{tag}_fuse", [128, NFC * 512], BF16)
    sgs = [ctx.sbuf(f"{tag}_sgs{i}", [128, 512], BF16) for i in range(2)]
    t1 = ctx.sbuf(f"{tag}_t1", [128, 512], F32)
    stg = [ctx.sbuf(f"{tag}_stg{i}", [128, 512], F32) for i in range(2)]
    psg = [ctx.psum(f"{tag}_psg{i}", [128, 512]) for i in range(2)]
    psu = [ctx.psum(f"{tag}_psu{i}", [128, 512]) for i in range(2)]
    psd = [ctx.psum(f"{tag}_psd{i}", [128, 512]) for i in range(4)]

    for sqb in range(TPC // 512):
        for fs in range(16):
            for fc in range(4):
                fcg = fs * 4 + fc
                git = (sqb * 16 + fs) * 4 + fc
                pb = git % 2
                if git >= 2:
                    ctx.wait("gpsimd", f"{tag}_gmm", git - 1)
                    ctx.wait("gpsimd", f"{tag}_umm", git - 1)
                srcg = wg_dram[fs][:, fc * 128:(fc + 1) * 128].rearrange(
                    "(c p) f -> p c f", p=128)
                ctx.dma("gpsimd", wgp[pb][:, :].rearrange("p (c f) -> p c f", c=NDC),
                        srcg, f"{tag}_wgld")
                srcu = wu_dram[fs][:, fc * 128:(fc + 1) * 128].rearrange(
                    "(c p) f -> p c f", p=128)
                ctx.dma("gpsimd", wup[pb][:, :].rearrange("p (c f) -> p c f", c=NDC),
                        srcu, f"{tag}_wuld")
                ctx.wait("tensor", f"{tag}_wgld", 16 * (git + 1))
                ctx.wait("tensor", f"{tag}_wuld", 16 * (git + 1))
                if git >= 2:
                    ctx.wait("tensor", f"{tag}_silu", git - 1)
                    ctx.wait("tensor", f"{tag}_fuse", git - 1)
                for dc in range(NDC):
                    ctx.op("tensor", lambda e, dc=dc, sqb=sqb, pb=pb: e.matmul(
                        psg[pb][:, :], wgp[pb][:, dc * 128:(dc + 1) * 128],
                        xnT[:, dc * TPC + sqb * 512: dc * TPC + sqb * 512 + 512],
                        start=(dc == 0), stop=(dc == NDC - 1)),
                        inc=(f"{tag}_gmm", 1) if dc == NDC - 1 else None)
                for dc in range(NDC):
                    ctx.op("tensor", lambda e, dc=dc, sqb=sqb, pb=pb: e.matmul(
                        psu[pb][:, :], wup[pb][:, dc * 128:(dc + 1) * 128],
                        xnT[:, dc * TPC + sqb * 512: dc * TPC + sqb * 512 + 512],
                        start=(dc == 0), stop=(dc == NDC - 1)),
                        inc=(f"{tag}_umm", 1) if dc == NDC - 1 else None)
                ctx.wait("scalar", f"{tag}_gmm", ctx.v(f"{tag}_gmm"))
                if git >= 2:
                    ctx.wait("scalar", f"{tag}_fuse", git - 1)
                ctx.op("scalar", lambda e, fcg=fcg, pb=pb: e.activation(
                    sgs[pb][:, :], psg[pb][:, :], AF.Silu, scale=sg_sc[:, fcg:fcg + 1]),
                    inc=(f"{tag}_silu", 1))
                ctx.wait("vector", f"{tag}_silu", ctx.v(f"{tag}_silu"))
                ctx.wait("vector", f"{tag}_umm", ctx.v(f"{tag}_umm"))
                ctx.op("vector", lambda e, fcg=fcg, pb=pb: e.scalar_tensor_tensor(
                    fuse[:, fcg * 512:(fcg + 1) * 512], psu[pb][:, :],
                    su_sc[:, fcg:fcg + 1], sgs[pb][:, :], OP.mult, OP.mult),
                    inc=(f"{tag}_fuse", 1))
        fusev = ctx.v(f"{tag}_fuse")
        for ds in range(4):
            for fs2 in range(4):
                lidx = (sqb * 4 + ds) * 4 + fs2
                wb = lidx % 2
                if lidx >= 2:
                    ctx.wait("gpsimd", f"{tag}_dmm", (lidx - 1) * 4)
                srcd = wd_dram[ds][fs2 * 2048:(fs2 + 1) * 2048, :].rearrange(
                    "(c p) f -> p c f", p=128)
                ctx.dma("gpsimd", wd_sb[wb][:, :].rearrange("p (c f) -> p c f", c=NDC),
                        srcd, f"{tag}_wdld")
                ctx.wait("tensor", f"{tag}_wdld", 16 * (lidx + 1))
                if fs2 == 0:
                    if ds == 0:
                        ctx.wait("tensor", f"{tag}_fuse", fusev)
                    ctx.wait("tensor", f"{tag}_ev", ctx.v(f"{tag}_ev"))
                for sqt in range(4):
                    for fcl in range(NDC):
                        fcg = fs2 * NDC + fcl
                        ctx.op("tensor", lambda e, fcg=fcg, sqt=sqt, fs2=fs2, fcl=fcl,
                               wb=wb: e.matmul(
                            psd[sqt][:, :],
                            fuse[:, fcg * 512 + sqt * 128: fcg * 512 + (sqt + 1) * 128],
                            wd_sb[wb][:, fcl * 512:(fcl + 1) * 512],
                            start=(fs2 == 0 and fcl == 0), stop=(fs2 == 3 and fcl == NDC - 1)),
                            inc=(f"{tag}_dmm", 1) if fcl == NDC - 1 else None)
            ctx.wait("vector", f"{tag}_dmm", ctx.v(f"{tag}_dmm"))
            for sqt in range(4):
                g = (sqb * 4 + ds) * 4 + sqt
                b = g % 2
                ctx.wait("vector", f"{tag}_ev", ctx.v(f"{tag}_ev"))
                ctx.op("vector", lambda e, sqt=sqt, ds=ds: e.tensor_mul(
                    t1[:, :], psd[sqt][:, :], sd_bc[:, ds * 512:(ds + 1) * 512]),
                    inc=(f"{tag}_t1", 1))
                ctx.wait("scalar", f"{tag}_t1", ctx.v(f"{tag}_t1"))
                if g >= 2:
                    ctx.wait("scalar", f"{tag}_fgst", 16 * (g - 1))
                ctx.op("scalar", lambda e, b=b: e.activation(stg[b][:, :], t1[:, :], AF.Silu),
                       inc=(f"{tag}_ev", 1))
                ctx.wait("sync", f"{tag}_ev", ctx.v(f"{tag}_ev"))
                row = sqb * 512 + sqt * 128
                ctx.dma("sync", fg_out[row:row + 128, ds * 512:(ds + 1) * 512], stg[b][:, :],
                        f"{tag}_fgst")


def build_launch_C():
    nc = bass.Bass("TRN2", target_bir_lowering=False)
    o_d = nc.dram_tensor("o_d", [TPC, D], BF16, kind="ExternalInput").ap()
    sg_d = nc.dram_tensor("sg_d", [TPC, D], BF16, kind="ExternalInput").ap()
    x_d = nc.dram_tensor("x_d", [TPC, D], F32, kind="ExternalInput").ap()
    fg_d = nc.dram_tensor("fg_d", [TPC, D], F32, kind="ExternalInput").ap()
    normw_d = nc.dram_tensor("normw_d", [128, 128], F32, kind="ExternalInput").ap()
    wo_c = nc.dram_tensor("wo_c", [4, D, 512], F32, kind="ExternalInput").ap()
    wq_h = nc.dram_tensor("wq_h", [NH, D, HD], F32, kind="ExternalInput").ap()
    wk_h = nc.dram_tensor("wk_h", [NH, D, HD], F32, kind="ExternalInput").ap()
    wv_c = nc.dram_tensor("wv_c", [4, D, 512], F32, kind="ExternalInput").ap()
    wg1_d = nc.dram_tensor("wg1_d", [16, D, 512], I8, kind="ExternalInput").ap()
    wu1_d = nc.dram_tensor("wu1_d", [16, D, 512], I8, kind="ExternalInput").ap()
    wd1_d = nc.dram_tensor("wd1_d", [4, INTER, 512], I8, kind="ExternalInput").ap()
    sg_sc_d = nc.dram_tensor("sg_sc_d", [128, 64], F32, kind="ExternalInput").ap()
    su_sc_d = nc.dram_tensor("su_sc_d", [128, 64], F32, kind="ExternalInput").ap()
    sd_bc_d = nc.dram_tensor("sd_bc_d", [128, D], F32, kind="ExternalInput").ap()
    identd = nc.dram_tensor("identd", [128, 128], BF16, kind="ExternalInput").ap()

    x1_o = nc.dram_tensor("x1_o", [TPC, D], F32, kind="ExternalOutput").ap()
    qt1_o = nc.dram_tensor("qt1_o", [D, TPC], BF16, kind="ExternalOutput").ap()
    kt1_o = nc.dram_tensor("kt1_o", [D, TPC], BF16, kind="ExternalOutput").ap()
    v1_o = nc.dram_tensor("v1_o", [TPC, D], BF16, kind="ExternalOutput").ap()
    fg1_o = nc.dram_tensor("fg1_o", [TPC, D], F32, kind="ExternalOutput").ap()
    xn1_o = nc.dram_tensor("xn1_o", [TPC, D], BF16, kind="ExternalOutput").ap()

    with ExitStack() as stack:
        ctx = Ctx(nc, stack)
        ctx.ident = ctx.sbuf("identsb", [128, 128], BF16)
        gatedT = ctx.sbuf("gatedT", [128, NDC * TPC], BF16)
        xn1T = ctx.sbuf("xn1T", [128, NDC * TPC], BF16)
        normw_sb = ctx.sbuf("normw_sb", [128, 128], F32)
        sg_sc = ctx.sbuf("sg_sc", [128, 64], F32)
        su_sc = ctx.sbuf("su_sc", [128, 64], F32)
        sd_bc = ctx.sbuf("sd_bc", [128, D], F32)
        ctx.dma("sync", ctx.ident[:, :], identd, "cld")
        ctx.dma("sync", normw_sb[:, :], normw_d, "cld")
        ctx.dma("sync", sg_sc[:, :], sg_sc_d, "cld")
        ctx.dma("sync", su_sc[:, :], su_sc_d, "cld")
        ctx.dma("sync", sd_bc[:, :], sd_bc_d, "cld")

        def phase(emitfn):
            ctx.stage()
            with ExitStack() as ph:
                ctx.push_scope(ph)
                emitfn()
                ctx.pop_scope()

        phase(lambda: emit_l0_epi(ctx, o_d, sg_d, normw_sb, gatedT, tag="ep0"))
        phase(lambda: emit_out_proj(ctx, gatedT, wo_c, x_d, fg_d, x1_o, tag="op0"))
        phase(lambda: emit_ln_xnT(ctx, x1_o, xn1T, xn_out_dram=xn1_o, tag="ln1"))
        phase(lambda: emit_q_norope(ctx, xn1T, wq_h, qt1_o, float(SCALE), tag="q1"))
        phase(lambda: emit_q_norope(ctx, xn1T, wk_h, kt1_o, 1.0, tag="k1"))
        phase(lambda: emit_vg_proj(ctx, xn1T, wv_c, v1_o, tag="v1", silu=False))
        phase(lambda: emit_ffn1(ctx, xn1T, wg1_d, wu1_d, wd1_d, sg_sc, su_sc, sd_bc,
                                fg1_o, tag="f1"))
        ctx.stage()
        for s_ in ("op0_st", "ln1_xnst", "q1_qst", "k1_qst", "v1_st", "f1_fgst"):
            ctx.wait("sync", s_, ctx.v(s_))
        ctx.finalize()
    return nc


def inmaps_C(inputs, w, c, a_results, b_results):
    x = inputs["L0_attn_hidden_states"]
    maps = []
    for core in range(NCORE):
        b, t0 = core // 4, (core % 4) * TPC
        O_tok = np.zeros((TPC, D), dtype=NPBF16)
        for hc in range(4):
            src = b_results[b * 4 + hc]["o_d"]
            for j in range(4):
                h = hc * 4 + j
                O_tok[:, h * 128:(h + 1) * 128] = src[j][t0:t0 + TPC, :]
        maps.append({
            "o_d": O_tok,
            "sg_d": a_results[core]["sg_o"],
            "fg_d": a_results[core]["fg_o"],
            "x_d": np.ascontiguousarray(x[b, t0:t0 + TPC, :]),
            "normw_d": w["l0_normw_t"],
            "wo_c": w["l0_woT_c"],
            "wq_h": w["l1_wqT_h"], "wk_h": w["l1_wkT_h"], "wv_c": w["l1_wvT_c"],
            "wg1_d": w["l1_wg_t"], "wu1_d": w["l1_wu_t"], "wd1_d": w["l1_wd_t"],
            "sg_sc_d": w["l1_sg_t"], "su_sc_d": w["l1_su_t"], "sd_bc_d": w["l1_sd_t"],
            "identd": c["ident"],
        })
    return maps




def emit_lam(ctx, lam_d, ones_d, nlam_b, tag="lam"):
    lamv = ctx.sbuf(f"{tag}_v", [1, 128], F32)
    lscr = ctx.sbuf(f"{tag}_scr", [1, 128], F32)
    lamt = ctx.sbuf(f"{tag}_t", [1, 8], F32)
    onesb = ctx.sbuf(f"{tag}_ones1", [1, 128], F32)
    ps_l = ctx.psum(f"{tag}_ps", [128, 4])
    ctx.dma("sync", onesb[:, :], ones_d, f"{tag}_ld")
    for idx, (qrow, krow) in enumerate(((0, 2), (1, 3))):
        ctx.dma("sync", lamv[:, :], lam_d[qrow:qrow + 1, :], f"{tag}_ld")
        ctx.dma("sync", lscr[:, :], lam_d[krow:krow + 1, :], f"{tag}_ld")
        lv = ctx.v(f"{tag}_ld")
        ctx.wait("vector", f"{tag}_ld", lv)
        ctx.op("vector", lambda e: e.tensor_mul(lamv[:, :], lamv[:, :], lscr[:, :]),
               inc=(f"{tag}_ch", 1))
        ctx.wait("vector", f"{tag}_ch", ctx.v(f"{tag}_ch"))
        ctx.op("vector", lambda e, idx=idx: e.reduce_sum(lamt[:, idx:idx + 1], lamv[:, :], AX.X),
               inc=(f"{tag}_ch", 1))
        ctx.wait("vector", f"{tag}_ch", ctx.v(f"{tag}_ch"))
        ctx.wait("sync", f"{tag}_ch", ctx.v(f"{tag}_ch"))
    ctx.wait("scalar", f"{tag}_ch", ctx.v(f"{tag}_ch"))
    ctx.op("scalar", lambda e: e.activation(lamt[:, 2:4], lamt[:, 0:2], AF.Exp),
           inc=(f"{tag}_sc", 1))
    ctx.wait("vector", f"{tag}_sc", ctx.v(f"{tag}_sc"))
    ctx.op("vector", lambda e: e.tensor_sub(lamt[:, 4:5], lamt[:, 3:4], lamt[:, 2:3]),
           inc=(f"{tag}_ch", 1))
    ctx.wait("vector", f"{tag}_ch", ctx.v(f"{tag}_ch"))
    ctx.op("vector", lambda e: e.tensor_single_scalar(lamt[:, 5:6], lamt[:, 4:5], -0.8, OP.add),
           inc=(f"{tag}_ch", 1))
    ctx.wait("tensor", f"{tag}_ch", ctx.v(f"{tag}_ch"))
    ctx.op("tensor", lambda e: e.matmul(ps_l[:, 0:1], onesb[:, :], lamt[:, 5:6],
                                        start=True, stop=True),
           inc=(f"{tag}_mm", 1))
    ctx.wait("vector", f"{tag}_mm", ctx.v(f"{tag}_mm"))
    ctx.op("vector", lambda e: e.tensor_copy(nlam_b[:, :], ps_l[:, 0:1]),
           inc=(f"{tag}_bc", 1))


def emit_attn_l1(ctx, qt_d, kt_d, v_d, nlam_b, o_d, tag="at1"):
    NKC = S // 128
    SQB = 256
    NSQB = S // SQB
    GPH = NSQB * NKC
    qt = [ctx.sbuf(f"{tag}_qt{i}", [128, S], BF16) for i in range(2)]
    kt = [ctx.sbuf(f"{tag}_kt{i}", [128, S], BF16) for i in range(2)]
    ve = [ctx.sbuf(f"{tag}_ve{i}", [128, NKC * 129], BF16) for i in range(2)]
    pexp = [ctx.sbuf(f"{tag}_px{i}", [128, 512], BF16) for i in range(2)]
    onrm = [ctx.sbuf(f"{tag}_on{i}", [128, 128], BF16) for i in range(2)]
    rr = ctx.sbuf(f"{tag}_rr", [128, 8], F32)
    oscr = ctx.sbuf(f"{tag}_oscr", [128, 128], F32)
    sscr = ctx.sbuf(f"{tag}_sscr", [128, 128], F32)
    ps_s = [ctx.psum(f"{tag}_pss{i}", [128, 1024]) for i in range(2)]
    ps_o = [ctx.psum(f"{tag}_pso{i}", [128, 132]) for i in range(4)]

    def po_ap(par, idx, a, b):
        return ps_o[idx][:, a:b]
    for i in range(2):
        ctx.op("vector", lambda e, i=i: e.memset(
            ve[i][:, :].rearrange("p (c x) -> p c x", x=129)[:, :, 128:129], 1.0))
    ctx.nopinc("vector", f"{tag}_ones", 1)

    gtot = 0
    for j in range(4):
        hb = j % 2
        if j >= 2:
            ctx.wait("sync", f"{tag}_smm", (j - 1) * GPH)
            ctx.wait("sync", f"{tag}_av", (j - 1) * GPH)
        ctx.dma("sync", qt[hb][:, :], qt_d[j], f"{tag}_qld")
        ctx.dma("sync", kt[hb][:, :], kt_d[j], f"{tag}_kld")
        vsrc = v_d[j].rearrange("(c p) d -> p c d", p=128)
        vdst = ve[hb][:, :].rearrange("p (c x) -> p c x", x=129)[:, :, 0:128]
        ctx.dma("sync", vdst, vsrc, f"{tag}_vld")
        ldv = (ctx.v(f"{tag}_qld"), ctx.v(f"{tag}_kld"), ctx.v(f"{tag}_vld"))
        ctx.wait("tensor", f"{tag}_qld", ldv[0])
        ctx.wait("tensor", f"{tag}_kld", ldv[1])
        ctx.wait("tensor", f"{tag}_vld", ldv[2])
        if j == 0:
            ctx.wait("tensor", f"{tag}_ones", 1)
        for sqb in range(NSQB):
            par = sqb % 2
            nrm_prev = ctx.v(f"{tag}_nrm")
            if nrm_prev > 0:
                ctx.wait("tensor", f"{tag}_nrm", nrm_prev)
            def emit_av(kc, b):
                for st in range(2):
                    ctx.op("tensor", lambda e, st=st, kc=kc, b=b, hb=hb, par=par: e.matmul(
                        po_ap(par, st, 0, 129),
                        pexp[b][:, st * 128:(st + 1) * 128],
                        ve[hb][:, kc * 129:(kc + 1) * 129],
                        start=(kc == 0), stop=(kc == NKC - 1)))
                for st in range(2):
                    ctx.op("tensor", lambda e, st=st, kc=kc, b=b, hb=hb, par=par: e.matmul(
                        po_ap(par, 2 + st, 0, 129),
                        pexp[b][:, 256 + st * 128: 256 + (st + 1) * 128],
                        ve[hb][:, kc * 129:(kc + 1) * 129],
                        start=(kc == 0), stop=(kc == NKC - 1)),
                        inc=(f"{tag}_av", 1) if st == 1 else None)

            for kc in range(NKC):
                b = gtot % 2
                ctx.wait("tensor", f"{tag}_exp", max(0, ctx.v(f"{tag}_exp") - 1))
                ctx.op("tensor", lambda e, kc=kc, sqb=sqb, hb=hb, b=b: e.matmul(
                    ps_s[b][:, 0:256],
                    kt[hb][0:64, kc * 128:(kc + 1) * 128],
                    qt[hb][0:64, sqb * SQB:(sqb + 1) * SQB], start=True, stop=True))
                ctx.op("tensor", lambda e, kc=kc, sqb=sqb, hb=hb, b=b: e.matmul(
                    ps_s[b][:, 512:768],
                    kt[hb][64:128, kc * 128:(kc + 1) * 128],
                    qt[hb][64:128, sqb * SQB:(sqb + 1) * SQB], start=True, stop=True),
                    inc=(f"{tag}_smm", 1))
                ctx.wait("scalar", f"{tag}_smm", ctx.v(f"{tag}_smm"))
                ctx.wait("scalar", f"{tag}_av", max(0, gtot - 1))
                ctx.op("scalar", lambda e, b=b: e.activation(
                    pexp[b][:, 0:256], ps_s[b][:, 0:256], AF.Exp))
                ctx.op("scalar", lambda e, b=b: e.activation(
                    pexp[b][:, 256:512], ps_s[b][:, 512:768], AF.Exp),
                    inc=(f"{tag}_exp", 1))
                if kc > 0:
                    ctx.wait("tensor", f"{tag}_exp", ctx.v(f"{tag}_exp") - 1)
                    emit_av(kc - 1, 1 - b)
                gtot += 1
            ctx.wait("tensor", f"{tag}_exp", ctx.v(f"{tag}_exp"))
            emit_av(NKC - 1, (gtot - 1) % 2)
            avv = ctx.v(f"{tag}_av")
            ctx.wait("vector", f"{tag}_av", avv)
            for st in range(2):
                ctx.op("vector", lambda e, st=st, par=par: e.reciprocal(
                    rr[:, 0:1], po_ap(par, st, 128, 129)),
                    inc=(f"{tag}_rc", 1))
                ctx.op("vector", lambda e, st=st, par=par: e.reciprocal(
                    rr[:, 1:2], po_ap(par, 2 + st, 128, 129)),
                    inc=(f"{tag}_rc", 1))
                ctx.wait("vector", f"{tag}_rc", ctx.v(f"{tag}_rc"))
                ctx.op("vector", lambda e: e.tensor_mul(rr[:, 2:3], rr[:, 1:2], nlam_b[:, :]),
                       inc=(f"{tag}_rc", 1))
                ctx.wait("vector", f"{tag}_rc", ctx.v(f"{tag}_rc"))
                ctx.op("vector", lambda e, st=st, par=par: e.tensor_scalar(
                    oscr[:, :], po_ap(par, st, 0, 128), rr[:, 0:1], None, OP.mult),
                    inc=(f"{tag}_rc", 1))
                ctx.wait("vector", f"{tag}_rc", ctx.v(f"{tag}_rc"))
                ctx.op("vector", lambda e, st=st, par=par: e.scalar_tensor_tensor(
                    oscr[:, :], po_ap(par, 2 + st, 0, 128), rr[:, 2:3], oscr[:, :], OP.mult, OP.add),
                    inc=(f"{tag}_oc", 1))
                ocv = ctx.v(f"{tag}_oc")
                ctx.wait("scalar", f"{tag}_oc", ocv)
                ctx.op("scalar", lambda e: e.activation(sscr[:, :], oscr[:, :], AF.Square,
                                                        accum_out=rr[:, 3:4]),
                       inc=(f"{tag}_sq", 1))
                ctx.wait("scalar", f"{tag}_sq", ctx.v(f"{tag}_sq"))
                ctx.op("scalar", lambda e: e.activation(rr[:, 4:5], rr[:, 3:4], AF.Sqrt,
                                                        scale=np.float32(1.0 / HD),
                                                        bias=np.float32(1.1920928955078125e-07)),
                       inc=(f"{tag}_sqa", 1))
                ctx.wait("vector", f"{tag}_sqa", ctx.v(f"{tag}_sqa"))
                ctx.op("vector", lambda e: e.reciprocal(rr[:, 5:6], rr[:, 4:5]),
                       inc=(f"{tag}_rc", 1))
                ctx.wait("vector", f"{tag}_rc", ctx.v(f"{tag}_rc"))
                ctx.op("vector", lambda e: e.tensor_single_scalar(rr[:, 6:7], rr[:, 5:6], 0.2, OP.mult),
                       inc=(f"{tag}_rc", 1))
                ctx.wait("vector", f"{tag}_rc", ctx.v(f"{tag}_rc"))
                ob = st % 2
                g = ctx.v(f"{tag}_nrm")
                if g >= 2:
                    ctx.wait("vector", f"{tag}_ost", 16 * (g - 1))
                ctx.op("vector", lambda e, ob=ob: e.tensor_scalar(
                    onrm[ob][:, :], oscr[:, :], rr[:, 6:7], None, OP.mult),
                    inc=(f"{tag}_nrm", 1))
                ctx.wait("sync", f"{tag}_nrm", ctx.v(f"{tag}_nrm"))
                row = sqb * SQB + st * 128
                ctx.dma("sync", o_d[j][row:row + 128, :], onrm[ob][:, :], f"{tag}_ost")


def build_launch_D():
    nc = bass.Bass("TRN2", target_bir_lowering=False)
    qt_d = nc.dram_tensor("qt_d", [4, 128, S], BF16, kind="ExternalInput").ap()
    kt_d = nc.dram_tensor("kt_d", [4, 128, S], BF16, kind="ExternalInput").ap()
    v_d = nc.dram_tensor("v_d", [4, S, HD], BF16, kind="ExternalInput").ap()
    lam_d = nc.dram_tensor("lam_d", [4, 128], F32, kind="ExternalInput").ap()
    ones_d = nc.dram_tensor("ones_d", [1, 128], F32, kind="ExternalInput").ap()
    o_d = nc.dram_tensor("o_d", [4, S, HD], BF16, kind="ExternalOutput").ap()
    with ExitStack() as stack:
        ctx = Ctx(nc, stack)
        nlam_b = ctx.sbuf("nlam_b", [128, 1], F32)
        with ExitStack() as ph:
            ctx.push_scope(ph)
            emit_lam(ctx, lam_d, ones_d, nlam_b, tag="lam")
            ctx.pop_scope()
        ctx.stage()
        with ExitStack() as ph:
            ctx.push_scope(ph)
            emit_attn_l1(ctx, qt_d, kt_d, v_d, nlam_b, o_d, tag="at1")
            ctx.pop_scope()
        ctx.stage()
        ctx.wait("sync", "at1_ost", ctx.v("at1_ost"))
        ctx.finalize()
    return nc


def inmaps_D(w, c_results):
    lam = np.stack([w["l1_lambda_q1"][0], w["l1_lambda_q2"][0],
                    w["l1_lambda_k1"][0], w["l1_lambda_k2"][0]]).astype(np.float32)
    ones = np.ones((1, 128), dtype=np.float32)
    maps = []
    for core in range(NCORE):
        b, h0 = core // 4, (core % 4) * 4
        qt = np.zeros((4, 128, S), dtype=NPBF16)
        kt = np.zeros((4, 128, S), dtype=NPBF16)
        v = np.zeros((4, S, HD), dtype=NPBF16)
        for r in range(4):
            src = c_results[b * 4 + r]
            sl = slice(r * TPC, (r + 1) * TPC)
            for j in range(4):
                h = h0 + j
                qt[j][:, sl] = src["qt1_o"][h * 128:(h + 1) * 128, :]
                kt[j][:, sl] = src["kt1_o"][h * 128:(h + 1) * 128, :]
                v[j][sl, :] = src["v1_o"][:, h * 128:(h + 1) * 128]
        maps.append({"qt_d": qt, "kt_d": kt, "v_d": v, "lam_d": lam, "ones_d": ones})
    return maps


def build_launch_E():
    nc = bass.Bass("TRN2", target_bir_lowering=False)
    tT_d = nc.dram_tensor("tT_d", [D, TPC], BF16, kind="ExternalInput").ap()
    wo_c = nc.dram_tensor("wo_c", [4, D, 512], F32, kind="ExternalInput").ap()
    x_d = nc.dram_tensor("x_d", [TPC, D], F32, kind="ExternalInput").ap()
    fg_d = nc.dram_tensor("fg_d", [TPC, D], F32, kind="ExternalInput").ap()
    x2_o = nc.dram_tensor("x2_o", [TPC, D], F32, kind="ExternalOutput").ap()
    with ExitStack() as stack:
        ctx = Ctx(nc, stack)
        tT = ctx.sbuf("tT", [128, NDC * TPC], BF16)
        srcw = tT_d.rearrange("(dc p) c -> p dc c", p=128)
        ctx.dma("sync", tT[:, :].rearrange("p (dc c) -> p dc c", dc=NDC), srcw, "tld")
        ctx.wait("tensor", "tld", 16)
        ctx.stage()
        with ExitStack() as ph:
            ctx.push_scope(ph)
            emit_out_proj(ctx, tT, wo_c, x_d, fg_d, x2_o, tag="op1")
            ctx.pop_scope()
        ctx.stage()
        ctx.wait("sync", "op1_st", ctx.v("op1_st"))
        ctx.finalize()
    return nc


def inmaps_E(inputs, w, d_results, c_results):
    normed = np.zeros((B, NH, S, HD), dtype=NPBF16)
    for core in range(NCORE):
        b, h0 = core // 4, (core % 4) * 4
        for j in range(4):
            normed[b, h0 + j] = d_results[core]["o_d"][j]
    t = normed.reshape(B, NH * S, HD).transpose(0, 2, 1).reshape(B, S, D)
    maps = []
    for core in range(NCORE):
        b, t0 = core // 4, (core % 4) * TPC
        tT = np.ascontiguousarray(t[b, t0:t0 + TPC, :].T)
        maps.append({
            "tT_d": tT,
            "wo_c": w["l1_woT_c"],
            "x_d": c_results[core]["x1_o"],
            "fg_d": c_results[core]["fg1_o"],
        })
    return maps


def kernel(**inputs):
    c = host_consts()
    w = prep_weights(inputs)

    nc = build_launch_A()
    res = run_bass_kernel_spmd(nc, inmaps_A(inputs, w, c), core_ids=list(range(NCORE)))
    a_results = res.results

    nc = build_launch_B()
    res = run_bass_kernel_spmd(nc, inmaps_B(a_results, c), core_ids=list(range(NCORE)))
    b_results = res.results

    nc = build_launch_C()
    res = run_bass_kernel_spmd(nc, inmaps_C(inputs, w, c, a_results, b_results),
                               core_ids=list(range(NCORE)))
    c_results = res.results

    nc = build_launch_D()
    res = run_bass_kernel_spmd(nc, inmaps_D(w, c_results), core_ids=list(range(NCORE)))
    d_results = res.results

    nc = build_launch_E()
    res = run_bass_kernel_spmd(nc, inmaps_E(inputs, w, d_results, c_results),
                               core_ids=list(range(NCORE)))
    e_results = res.results

    out = np.zeros((B, S, D), dtype=np.float32)
    for core in range(NCORE):
        b, t0 = core // 4, (core % 4) * TPC
        out[b, t0:t0 + TPC, :] = e_results[core]["x2_o"]
    return out


# version 03
# speedup vs baseline: 1.5858x; 1.0266x over previous
import numpy as np
import ml_dtypes
from contextlib import ExitStack
import concourse.bass as bass
import concourse.mybir as mybir
from concourse.bass_utils import run_bass_kernel_spmd

F32 = mybir.dt.float32
BF16 = mybir.dt.bfloat16
I8 = mybir.dt.int8
AF = mybir.ActivationFunctionType
OP = mybir.AluOpType
AX = mybir.AxisListType
NPBF16 = ml_dtypes.bfloat16

B, S, D, NH, HD = 2, 4096, 2048, 16, 128
E, F, INTER = 8, 2048, 8192
NCORE = 8
TPC = 1024
NT = TPC // 128
NDC = D // 128
SCALE = np.float32(HD ** (-0.5))
GC = np.float32(1.0 / np.e)
GL = np.float32(0.5 - np.log(4.0 / 3.0))
GU = np.float32(0.5)

ENGS = ("tensor", "vector", "scalar", "sync", "gpsimd")


class Ctx:
    def __init__(self, nc, stack):
        self.nc = nc
        self.scopes = [stack]
        self.stages = []
        self.cur = None
        self.semh = {}
        self.semv = {}
        self.stage()

    def stage(self):
        self.cur = {e: [] for e in ENGS}
        self.stages.append(self.cur)

    def push_scope(self, st):
        self.scopes.append(st)

    def pop_scope(self):
        self.scopes.pop()

    def finalize(self):
        for st in self.stages:
            with self.nc.Block() as blk:
                for eng in ENGS:
                    lst = st[eng]
                    if lst:
                        getattr(blk, eng)(lambda e, lst=lst: [f(e) for f in lst])

    def sbuf(self, name, shape, dtype):
        return self.scopes[-1].enter_context(self.nc.sbuf_tensor(name, shape, dtype))

    def psum(self, name, shape, dtype=F32):
        return self.scopes[-1].enter_context(self.nc.psum_tensor(name, shape, dtype))

    def sem(self, name):
        if name not in self.semh:
            self.semh[name] = self.scopes[0].enter_context(self.nc.semaphore(name))
            self.semv[name] = 0
        return self.semh[name]

    def bump(self, name, by):
        self.sem(name)
        self.semv[name] += by
        return self.semv[name]

    def v(self, name):
        return self.semv.get(name, 0)

    def emit(self, eng, fn):
        self.cur[eng].append(fn)

    def wait(self, eng, name, val):
        if val <= 0:
            return
        h = self.sem(name)
        self.emit(eng, lambda e, h=h, val=val: e.wait_ge(h, val))

    def op(self, eng, fn, inc=None):
        if inc is None:
            self.emit(eng, fn)
            return None
        name, by = inc
        h = self.sem(name)
        v = self.bump(name, by)
        self.emit(eng, lambda e, fn=fn, h=h, by=by: fn(e).then_inc(h, by))
        return v

    def nopinc(self, eng, name, by=1):
        h = self.sem(name)
        v = self.bump(name, by)
        if eng in ("vector",):
            self.emit(eng, lambda e, h=h, by=by: e.engine_nop().then_inc(h, by))
        else:
            self.emit(eng, lambda e, h=h, by=by: e.nop().then_inc(h, by))
        return v

    def dma(self, eng, out_ap, in_ap, sem):
        h = self.sem(sem)
        v = self.bump(sem, 16)
        self.emit(eng, lambda e, o=out_ap, i=in_ap, h=h: e.dma_start(out=o, in_=i).then_inc(h, 16))
        return v


def host_consts():
    c = {}
    c["ident"] = np.eye(128, dtype=NPBF16)
    R = np.zeros((128, 128), dtype=np.float32)
    for i in range(64):
        R[i, i + 64] = -1.0
        R[i + 64, i] = 1.0
    c["rotT"] = np.ascontiguousarray(R.T).astype(NPBF16)
    inv_freq = 1.0 / (10000.0 ** (np.arange(0, HD, 2, dtype=np.float32) / HD))
    t = np.arange(S, dtype=np.float32)
    freqs = np.einsum("j,t->jt", inv_freq, t)
    emb = np.concatenate([freqs, freqs], axis=0)
    c["cosF"] = np.cos(emb).astype(np.float32)
    c["sinF"] = np.sin(emb).astype(np.float32)
    m4 = np.zeros((4, 128, 512), dtype=NPBF16)
    p = np.arange(128)[:, None]
    f = np.arange(512)[None, :]
    for j in range(4):
        m4[j] = (p + 128 * j <= f).astype(NPBF16)
    c["mask4"] = m4
    return c


def prep_weights(inp):
    w = {}
    for nm in ("wq", "wk"):
        wT = np.ascontiguousarray(inp[f"L0_attn_{nm}"].T)
        w[f"l0_{nm}T_h"] = np.ascontiguousarray(wT.reshape(D, NH, HD).transpose(1, 0, 2))
    for nm in ("wv", "wg"):
        wT = np.ascontiguousarray(inp[f"L0_attn_{nm}"].T)
        w[f"l0_{nm}T_c"] = np.ascontiguousarray(wT.reshape(D, 4, 512).transpose(1, 0, 2))
    woT = np.ascontiguousarray(inp["L0_attn_wo"].T)
    w["l0_woT_c"] = np.ascontiguousarray(woT.reshape(D, 4, 512).transpose(1, 0, 2))
    w["l0_Wr_t"] = np.ascontiguousarray(
        inp["L0_ffn_Wr"].reshape(NDC, 128, E).transpose(1, 0, 2).reshape(128, NDC * E))
    w["l0_Wg_t"] = np.ascontiguousarray(inp["L0_ffn_Wg"].reshape(E, D, 4, 512).transpose(0, 2, 1, 3))
    w["l0_Wu_t"] = np.ascontiguousarray(inp["L0_ffn_Wu"].reshape(E, D, 4, 512).transpose(0, 2, 1, 3))
    w["l0_Wd_t"] = np.ascontiguousarray(inp["L0_ffn_Wd"].reshape(E, F, 4, 512).transpose(0, 2, 1, 3))
    for nm in ("wq", "wk"):
        wT = np.ascontiguousarray(inp[f"L1_attn_{nm}"].T)
        w[f"l1_{nm}T_h"] = np.ascontiguousarray(wT.reshape(D, NH, HD).transpose(1, 0, 2))
    wT = np.ascontiguousarray(inp["L1_attn_wv"].T)
    w["l1_wvT_c"] = np.ascontiguousarray(wT.reshape(D, 4, 512).transpose(1, 0, 2))
    woT1 = np.ascontiguousarray(inp["L1_attn_wo"].T)
    w["l1_woT_c"] = np.ascontiguousarray(woT1.reshape(D, 4, 512).transpose(1, 0, 2))
    w["l1_wg_t"] = np.ascontiguousarray(inp["L1_ffn_w_gate"].reshape(D, 16, 512).transpose(1, 0, 2))
    w["l1_wu_t"] = np.ascontiguousarray(inp["L1_ffn_w_up"].reshape(D, 16, 512).transpose(1, 0, 2))
    w["l1_wd_t"] = np.ascontiguousarray(inp["L1_ffn_w_down"].reshape(INTER, 4, 512).transpose(1, 0, 2))
    w["l1_sg_t"] = np.ascontiguousarray(inp["L1_ffn_s_gate"].reshape(64, 128).T)
    w["l1_su_t"] = np.ascontiguousarray(inp["L1_ffn_s_up"].reshape(64, 128).T)
    w["l1_sd_t"] = np.ascontiguousarray(np.tile(inp["L1_ffn_s_down"][None, :], (128, 1)))
    w["l0_normw_t"] = np.ascontiguousarray(np.tile(inp["L0_attn_norm_w"][None, :], (128, 1)))
    w["l0_temp_t"] = np.ascontiguousarray(
        np.tile(np.asarray(inp["L0_ffn_temperature"], dtype=np.float32).reshape(1, 1), (128, 1)))
    for nm in ("lambda_q1", "lambda_q2", "lambda_k1", "lambda_k2"):
        w[f"l1_{nm}"] = np.ascontiguousarray(inp[f"L1_attn_{nm}"].reshape(1, HD))
    return w


def emit_ln_xnT(ctx, x_dram, xnT, xn_out_dram=None, tag="ln"):
    xt = [ctx.sbuf(f"{tag}_xt{i}", [128, D], F32) for i in range(2)]
    xnb = [ctx.sbuf(f"{tag}_xnb{i}", [128, D], BF16) for i in range(2)]
    scr = ctx.sbuf(f"{tag}_scr", [128, D], F32)
    st = [ctx.sbuf(f"{tag}_st{i}", [128, 8], F32) for i in range(2)]
    tp = [ctx.psum(f"{tag}_tp{i}", [128, 128], BF16) for i in range(2)]
    ident = ctx.ident
    for i in range(NT):
        b = i % 2
        if i >= 2:
            ctx.wait("sync", f"{tag}_xn", i - 1)
            ctx.wait("sync", f"{tag}_sq", i - 1)
        ctx.dma("sync", xt[b][:, :], x_dram[i * 128:(i + 1) * 128, :], f"{tag}_xld")
        ctx.wait("scalar", f"{tag}_xld", 16 * (i + 1))
        sti = st[b]
        if i >= 2:
            ctx.wait("scalar", f"{tag}_rc", i - 1)
            ctx.wait("scalar", f"{tag}_xn", i - 1)
        ctx.wait("vector", f"{tag}_xld", 16 * (i + 1))
        ctx.op("vector", lambda e, b=b, sti=sti: e.reduce_sum(sti[:, 0:1], xt[b][:, :], AX.X),
               inc=(f"{tag}_rs", 1))
        ctx.op("scalar", lambda e, b=b, sti=sti: e.activation(scr[:, :], xt[b][:, :], AF.Square,
                                                              accum_out=sti[:, 1:2]),
               inc=(f"{tag}_sq", 1))
        ctx.wait("scalar", f"{tag}_sq", ctx.v(f"{tag}_sq"))
        ctx.wait("scalar", f"{tag}_rs", ctx.v(f"{tag}_rs"))
        ctx.op("scalar", lambda e, sti=sti: e.mul(sti[:, 2:3], sti[:, 0:1], 1.0 / D),
               inc=(f"{tag}_sch", 1))
        ctx.wait("scalar", f"{tag}_sch", ctx.v(f"{tag}_sch"))
        ctx.op("scalar", lambda e, sti=sti: e.mul(sti[:, 3:4], sti[:, 2:3], sti[:, 2:3]),
               inc=(f"{tag}_sch", 1))
        ctx.wait("scalar", f"{tag}_sch", ctx.v(f"{tag}_sch"))
        ctx.op("scalar", lambda e, sti=sti: e.activation(sti[:, 3:4], sti[:, 3:4], AF.Identity,
                                                         scale=np.float32(-1.0),
                                                         bias=np.float32(1e-5)),
               inc=(f"{tag}_sch", 1))
        ctx.wait("scalar", f"{tag}_sch", ctx.v(f"{tag}_sch"))
        ctx.op("scalar", lambda e, sti=sti: e.activation(sti[:, 4:5], sti[:, 1:2], AF.Sqrt,
                                                         scale=np.float32(1.0 / D),
                                                         bias=sti[:, 3:4]),
               inc=(f"{tag}_sqa", 1))
        ctx.wait("vector", f"{tag}_sqa", ctx.v(f"{tag}_sqa"))
        ctx.op("vector", lambda e, sti=sti: e.reciprocal(sti[:, 5:6], sti[:, 4:5]),
               inc=(f"{tag}_rc", 1))
        rcv = ctx.v(f"{tag}_rc")
        ctx.wait("scalar", f"{tag}_rc", rcv)
        ctx.op("scalar", lambda e, sti=sti: e.mul(sti[:, 6:7], sti[:, 2:3], sti[:, 5:6]),
               inc=(f"{tag}_mrs", 1))
        if i >= 2:
            ctx.wait("vector", f"{tag}_tp", (i - 1) * NDC)
            if xn_out_dram is not None:
                ctx.wait("vector", f"{tag}_xnst", 16 * (i - 1))
        ctx.wait("vector", f"{tag}_mrs", ctx.v(f"{tag}_mrs"))
        ctx.wait("vector", f"{tag}_rc", rcv)
        ctx.wait("vector", f"{tag}_xld", 16 * (i + 1))
        ctx.op("vector", lambda e, b=b, sti=sti: e.tensor_scalar(xnb[b][:, :], xt[b][:, :],
                                                                 sti[:, 5:6], sti[:, 6:7],
                                                                 OP.mult, OP.subtract),
               inc=(f"{tag}_xn", 1))
        xnv = ctx.v(f"{tag}_xn")
        if xn_out_dram is not None:
            ctx.wait("sync", f"{tag}_xn", xnv)
            ctx.dma("sync", xn_out_dram[i * 128:(i + 1) * 128, :], xnb[b][:, :], f"{tag}_xnst")
        ctx.wait("tensor", f"{tag}_xn", xnv)
        for dc in range(NDC):
            g = i * NDC + dc
            pb = g % 2
            if g >= 2:
                ctx.wait("tensor", f"{tag}_tev", g - 1)
            ctx.op("tensor", lambda e, b=b, dc=dc, pb=pb: e.transpose(
                tp[pb][:, :], xnb[b][:, dc * 128:(dc + 1) * 128], ident[:, :]),
                inc=(f"{tag}_tp", 1))
            ctx.wait("vector", f"{tag}_tp", ctx.v(f"{tag}_tp"))
            ctx.op("vector", lambda e, dc=dc, i=i, pb=pb: e.tensor_copy(
                xnT[:, dc * TPC + i * 128: dc * TPC + (i + 1) * 128], tp[pb][:, :]),
                inc=(f"{tag}_tev", 1))


def emit_qk_proj(ctx, xnT, wT_h_dram, out_dram, cos_sb, sin_sb, tag):
    wsb = [ctx.sbuf(f"{tag}_w{i}", [128, NDC * 128], BF16) for i in range(2)]
    qraw = ctx.sbuf(f"{tag}_qraw", [128, TPC], BF16)
    t1 = ctx.sbuf(f"{tag}_t1", [128, 512], F32)
    t2 = ctx.sbuf(f"{tag}_t2", [128, 512], F32)
    rop = [ctx.sbuf(f"{tag}_rop{i}", [128, TPC], BF16) for i in range(2)]
    ps = [ctx.psum(f"{tag}_ps{i}", [128, 512]) for i in range(2)]
    psr = ctx.psum(f"{tag}_psr", [128, 512])
    rotT = ctx.rotT
    NSQB = TPC // 512
    for h in range(NH):
        wb = h % 2
        if h >= 2:
            ctx.wait("gpsimd", f"{tag}_mm", (h - 1) * NSQB)
        src = wT_h_dram[h].rearrange("(dc p) c -> p dc c", p=128)
        ctx.dma("gpsimd", wsb[wb][:, :].rearrange("p (dc c) -> p dc c", dc=NDC), src, f"{tag}_wld")
        ctx.wait("tensor", f"{tag}_wld", 16 * (h + 1))
        for sqb in range(NSQB):
            g = h * NSQB + sqb
            pb = g % 2
            if g >= 2:
                ctx.wait("tensor", f"{tag}_evac", g - 1)
            for dc in range(NDC):
                ctx.op("tensor", lambda e, dc=dc, sqb=sqb, pb=pb, wb=wb: e.matmul(
                    ps[pb][:, :], wsb[wb][:, dc * 128:(dc + 1) * 128],
                    xnT[:, dc * TPC + sqb * 512: dc * TPC + sqb * 512 + 512],
                    start=(dc == 0), stop=(dc == NDC - 1)),
                    inc=(f"{tag}_mm", 1) if dc == NDC - 1 else None)
            mmv = ctx.v(f"{tag}_mm")
            ctx.wait("scalar", f"{tag}_mm", mmv)
            if h > 0:
                ctx.wait("scalar", f"{tag}_rope", (h - 1) * NSQB + sqb + 1)
            ctx.op("scalar", lambda e, sqb=sqb, pb=pb: e.copy(qraw[:, sqb * 512:sqb * 512 + 512],
                                                              ps[pb][:, :]),
                   inc=(f"{tag}_evac", 1))
            evv = ctx.v(f"{tag}_evac")
            ctx.wait("tensor", f"{tag}_evac", evv)
            if g >= 1:
                ctx.wait("tensor", f"{tag}_t2", g)
            ctx.op("tensor", lambda e, sqb=sqb: e.matmul(
                psr[:, :], rotT[:, :], qraw[:, sqb * 512:sqb * 512 + 512], start=True, stop=True),
                inc=(f"{tag}_rot", 1))
            rotv = ctx.v(f"{tag}_rot")
            ctx.wait("vector", f"{tag}_evac", evv)
            rb = h % 2
            if h >= 2 and sqb == 0:
                ctx.wait("vector", f"{tag}_qst", 16 * (h - 1))
            ctx.op("vector", lambda e, sqb=sqb: e.tensor_mul(
                t1[:, :], qraw[:, sqb * 512:sqb * 512 + 512], cos_sb[:, sqb * 512:sqb * 512 + 512]))
            ctx.wait("vector", f"{tag}_rot", rotv)
            ctx.op("vector", lambda e, sqb=sqb: e.tensor_mul(t2[:, :], psr[:, :],
                                                             sin_sb[:, sqb * 512:sqb * 512 + 512]),
                   inc=(f"{tag}_t2", 1))
            ctx.wait("vector", f"{tag}_t2", ctx.v(f"{tag}_t2"))
            ctx.op("vector", lambda e, rb=rb, sqb=sqb: e.tensor_add(
                rop[rb][:, sqb * 512:sqb * 512 + 512], t1[:, :], t2[:, :]),
                inc=(f"{tag}_rope", 1))
        ctx.wait("sync", f"{tag}_rope", ctx.v(f"{tag}_rope"))
        ctx.dma("sync", out_dram[h * 128:(h + 1) * 128, :], rop[h % 2][:, :], f"{tag}_qst")


def emit_vg_proj(ctx, xnT, wT_c_dram, out_dram, tag, silu=False):
    wsb = [ctx.sbuf(f"{tag}_w{i}", [128, NDC * 512], BF16) for i in range(2)]
    stg = [ctx.sbuf(f"{tag}_stg{i}", [128, 512], BF16) for i in range(2)]
    ps = [ctx.psum(f"{tag}_ps{i}", [128, 512]) for i in range(2)]
    for cb in range(4):
        wb = cb % 2
        if cb >= 2:
            ctx.wait("gpsimd", f"{tag}_mm", (cb - 1) * NT)
        src = wT_c_dram[cb].rearrange("(dc p) c -> p dc c", p=128)
        ctx.dma("gpsimd", wsb[wb][:, :].rearrange("p (dc c) -> p dc c", dc=NDC), src, f"{tag}_wld")
        ctx.wait("tensor", f"{tag}_wld", 16 * (cb + 1))
        for i in range(NT):
            g = cb * NT + i
            pb = g % 2
            if g >= 2:
                ctx.wait("tensor", f"{tag}_evac", g - 1)
            for dc in range(NDC):
                ctx.op("tensor", lambda e, dc=dc, i=i, pb=pb, wb=wb: e.matmul(
                    ps[pb][:, :], xnT[:, dc * TPC + i * 128: dc * TPC + (i + 1) * 128],
                    wsb[wb][:, dc * 512:(dc + 1) * 512],
                    start=(dc == 0), stop=(dc == NDC - 1)),
                    inc=(f"{tag}_mm", 1) if dc == NDC - 1 else None)
            ctx.wait("scalar", f"{tag}_mm", ctx.v(f"{tag}_mm"))
            if g >= 2:
                ctx.wait("scalar", f"{tag}_st", 16 * (g - 1))
            sb = g % 2
            if silu:
                ctx.op("scalar", lambda e, sb=sb, pb=pb: e.activation(stg[sb][:, :], ps[pb][:, :], AF.Silu),
                       inc=(f"{tag}_evac", 1))
            else:
                ctx.op("scalar", lambda e, sb=sb, pb=pb: e.copy(stg[sb][:, :], ps[pb][:, :]),
                       inc=(f"{tag}_evac", 1))
            ctx.wait("sync", f"{tag}_evac", ctx.v(f"{tag}_evac"))
            ctx.dma("sync", out_dram[i * 128:(i + 1) * 128, cb * 512:(cb + 1) * 512], stg[sb][:, :],
                    f"{tag}_st")


def emit_router(ctx, xnT, wr_sb, rtemp, wfinal, tag="rt", dbg=None):
    ps = ctx.psum(f"{tag}_ps", [128, 16])
    inh = ctx.sbuf(f"{tag}_inh", [128, 8], F32)
    dist = ctx.sbuf(f"{tag}_dist", [128, 8], F32)
    zone = ctx.sbuf(f"{tag}_zone", [128, 8], F32)
    z2 = ctx.sbuf(f"{tag}_z2", [128, 8], F32)
    wz = ctx.sbuf(f"{tag}_wz", [128, 8], F32)
    fb = ctx.sbuf(f"{tag}_fb", [128, 8], F32)
    fbx = ctx.sbuf(f"{tag}_fbx", [128, 8], F32)
    m = ctx.sbuf(f"{tag}_m", [128, 8], F32)
    mask1 = ctx.sbuf(f"{tag}_mask1", [128, 8], F32)
    mask2 = ctx.sbuf(f"{tag}_mask2", [128, 8], F32)
    ctx.wait("tensor", "cwld", 16)

    def vch(fn):
        v = ctx.op("vector", fn, inc=(f"{tag}_vch", 1))
        ctx.wait("vector", f"{tag}_vch", v)

    for i in range(NT):
        if i > 0:
            ctx.wait("tensor", f"{tag}_sg", i)
        for dc in range(NDC):
            ctx.op("tensor", lambda e, dc=dc, i=i: e.matmul(
                ps[:, 0:8], xnT[:, dc * TPC + i * 128: dc * TPC + (i + 1) * 128],
                wr_sb[:, dc * 8:(dc + 1) * 8],
                start=(dc == 0), stop=(dc == NDC - 1)),
                inc=(f"{tag}_mm", 1) if dc == NDC - 1 else None)
        ctx.wait("scalar", f"{tag}_mm", ctx.v(f"{tag}_mm"))
        if i > 0:
            ctx.wait("scalar", f"{tag}_done", i)
        ctx.op("scalar", lambda e: e.activation(inh[:, :], ps[:, 0:8], AF.Sigmoid, scale=rtemp[:, 0:1]),
               inc=(f"{tag}_sg", 1))
        ctx.wait("scalar", f"{tag}_sg", ctx.v(f"{tag}_sg"))
        if dbg is not None:
            ctx.op("scalar", lambda e, i=i: e.copy(dbg[:, i * 8:(i + 1) * 8], inh[:, :]))
        ctx.op("scalar", lambda e: e.activation(dist[:, :], inh[:, :], AF.Abs, bias=np.float32(-GC)))
        ctx.nopinc("scalar", f"{tag}_sch", 1)
        ctx.wait("scalar", f"{tag}_sch", ctx.v(f"{tag}_sch"))
        ctx.nopinc("scalar", f"{tag}_sch", 1)
        ctx.wait("scalar", f"{tag}_sch", ctx.v(f"{tag}_sch"))
        ctx.op("scalar", lambda e: e.activation(wz[:, :], dist[:, :], AF.Exp, scale=np.float32(-10.0)))
        ctx.op("scalar", lambda e: e.activation(fb[:, :], dist[:, :], AF.Exp,
                                                scale=np.float32(-1.0 / 0.3)),
               inc=(f"{tag}_sc", 1))
        ctx.wait("vector", f"{tag}_sc", ctx.v(f"{tag}_sc"))
        vch(lambda e: e.tensor_single_scalar(zone[:, :], inh[:, :], float(GL), OP.is_ge))
        vch(lambda e: e.tensor_single_scalar(z2[:, :], inh[:, :], float(GU), OP.is_le))
        vch(lambda e: e.tensor_mul(zone[:, :], zone[:, :], z2[:, :]))
        vch(lambda e: e.tensor_mul(wz[:, :], wz[:, :], zone[:, :]))
        vch(lambda e: e.reduce_sum(m[:, 0:1], wz[:, :], AX.X))
        vch(lambda e: e.tensor_single_scalar(m[:, 1:2], m[:, 0:1], 1e-8, OP.is_lt))
        vch(lambda e: e.reduce_max(m[:, 2:3], fb[:, :], AX.X))
        vch(lambda e: e.tensor_scalar(mask1[:, :], fb[:, :], m[:, 2:3], None, OP.is_equal))
        vch(lambda e: e.scalar_tensor_tensor(fbx[:, :], mask1[:, :], -1e30, fb[:, :],
                                             OP.mult, OP.add))
        vch(lambda e: e.reduce_max(m[:, 3:4], fbx[:, :], AX.X))
        vch(lambda e: e.tensor_scalar(mask2[:, :], fbx[:, :], m[:, 3:4], None, OP.is_equal))
        vch(lambda e: e.tensor_add(mask1[:, :], mask1[:, :], mask2[:, :]))
        vch(lambda e: e.tensor_mul(mask1[:, :], fb[:, :], mask1[:, :]))
        vch(lambda e: e.reduce_sum(m[:, 4:5], mask1[:, :], AX.X))
        vch(lambda e: e.tensor_single_scalar(m[:, 4:5], m[:, 4:5], 1e-8, OP.max))
        vch(lambda e: e.reciprocal(m[:, 5:6], m[:, 4:5]))
        vch(lambda e: e.tensor_scalar(mask1[:, :], mask1[:, :], m[:, 5:6], None, OP.mult))
        vch(lambda e: e.scalar_tensor_tensor(wz[:, :], mask1[:, :], m[:, 1:2], wz[:, :],
                                             OP.mult, OP.add))
        vch(lambda e: e.reduce_sum(m[:, 6:7], wz[:, :], AX.X))
        vch(lambda e: e.tensor_single_scalar(m[:, 6:7], m[:, 6:7], 1e-8, OP.max))
        vch(lambda e: e.reciprocal(m[:, 7:8], m[:, 6:7]))
        ctx.op("vector", lambda e, i=i: e.tensor_scalar(wfinal[:, i * 8:(i + 1) * 8], wz[:, :],
                                                        m[:, 7:8], None, OP.mult),
               inc=(f"{tag}_done", 1))


def emit_moe(ctx, xnT, wg_dram, wu_dram, wd_dram, wfinal, fg_out_dram, tag="moe"):
    wgp = [ctx.sbuf(f"{tag}_wg{i}", [128, NDC * 128], BF16) for i in range(2)]
    wup = [ctx.sbuf(f"{tag}_wu{i}", [128, NDC * 128], BF16) for i in range(2)]
    wd_sb = [ctx.sbuf(f"{tag}_wd{i}", [128, NDC * 512], BF16) for i in range(2)]
    fuseT = ctx.sbuf(f"{tag}_fuseT", [128, NDC * TPC], BF16)
    sg_scr = [ctx.sbuf(f"{tag}_sg{i}", [128, 512], BF16) for i in range(2)]
    acc = ctx.sbuf(f"{tag}_acc", [128, NT * D], F32)
    fg_stg = [ctx.sbuf(f"{tag}_fg{i}", [128, 512], F32) for i in range(2)]
    psg = [ctx.psum(f"{tag}_psg{i}", [128, 512]) for i in range(2)]
    psu = [ctx.psum(f"{tag}_psu{i}", [128, 512]) for i in range(2)]
    psd = [ctx.psum(f"{tag}_psd{i}", [128, 512]) for i in range(2)]
    NSQB = TPC // 512

    for e_i in range(E):
        for fs in range(4):
            for fc in range(4):
                fcg = fs * 4 + fc
                pidx = (e_i * 4 + fs) * 4 + fc
                pb = pidx % 2
                if pidx >= 2:
                    ctx.wait("gpsimd", f"{tag}_gmm", 2 * (pidx - 1))
                    ctx.wait("gpsimd", f"{tag}_umm", 2 * (pidx - 1))
                srcg = wg_dram[e_i, fs][:, fc * 128:(fc + 1) * 128].rearrange(
                    "(c p) f -> p c f", p=128)
                ctx.dma("gpsimd", wgp[pb][:, :].rearrange("p (c f) -> p c f", c=NDC),
                        srcg, f"{tag}_wgld")
                srcu = wu_dram[e_i, fs][:, fc * 128:(fc + 1) * 128].rearrange(
                    "(c p) f -> p c f", p=128)
                ctx.dma("gpsimd", wup[pb][:, :].rearrange("p (c f) -> p c f", c=NDC),
                        srcu, f"{tag}_wuld")
                ctx.wait("tensor", f"{tag}_wgld", 16 * (pidx + 1))
                ctx.wait("tensor", f"{tag}_wuld", 16 * (pidx + 1))
                for sqb in range(NSQB):
                    git = e_i * 16 * NSQB + fcg * NSQB + sqb
                    b = git % 2
                    if git >= 2:
                        ctx.wait("tensor", f"{tag}_silu", git - 1)
                        ctx.wait("tensor", f"{tag}_fuse", git - 1)
                    for dc in range(NDC):
                        ctx.op("tensor", lambda e, dc=dc, sqb=sqb, pb=pb, b=b: e.matmul(
                            psg[b][:, :], wgp[pb][:, dc * 128:(dc + 1) * 128],
                            xnT[:, dc * TPC + sqb * 512: dc * TPC + sqb * 512 + 512],
                            start=(dc == 0), stop=(dc == NDC - 1)),
                            inc=(f"{tag}_gmm", 1) if dc == NDC - 1 else None)
                    for dc in range(NDC):
                        ctx.op("tensor", lambda e, dc=dc, sqb=sqb, pb=pb, b=b: e.matmul(
                            psu[b][:, :], wup[pb][:, dc * 128:(dc + 1) * 128],
                            xnT[:, dc * TPC + sqb * 512: dc * TPC + sqb * 512 + 512],
                            start=(dc == 0), stop=(dc == NDC - 1)),
                            inc=(f"{tag}_umm", 1) if dc == NDC - 1 else None)
                    gv = ctx.v(f"{tag}_gmm")
                    uv = ctx.v(f"{tag}_umm")
                    ctx.wait("scalar", f"{tag}_gmm", gv)
                    if git >= 2:
                        ctx.wait("scalar", f"{tag}_fuse", git - 1)
                    ctx.op("scalar", lambda e, b=b: e.activation(sg_scr[b][:, :], psg[b][:, :],
                                                                 AF.Silu),
                           inc=(f"{tag}_silu", 1))
                    ctx.wait("vector", f"{tag}_silu", ctx.v(f"{tag}_silu"))
                    ctx.wait("vector", f"{tag}_umm", uv)
                    ctx.op("vector", lambda e, fcg=fcg, sqb=sqb, b=b: e.tensor_tensor(
                        fuseT[:, fcg * TPC + sqb * 512: fcg * TPC + sqb * 512 + 512],
                        psu[b][:, :], sg_scr[b][:, :], OP.mult),
                        inc=(f"{tag}_fuse", 1))
        fusev = ctx.v(f"{tag}_fuse")
        for ds in range(4):
            sidx = e_i * 4 + ds
            wb = sidx % 2
            if sidx >= 2:
                ctx.wait("gpsimd", f"{tag}_dmm", (sidx - 1) * NT)
            srcd = wd_dram[e_i, ds].rearrange("(c p) f -> p c f", p=128)
            ctx.dma("gpsimd", wd_sb[wb][:, :].rearrange("p (c f) -> p c f", c=NDC),
                    srcd, f"{tag}_wdld")
            ctx.wait("tensor", f"{tag}_wdld", 16 * (sidx + 1))
            if ds == 0:
                ctx.wait("tensor", f"{tag}_fuse", fusev)
            for i in range(NT):
                git = sidx * NT + i
                pb = git % 2
                if git >= 2:
                    ctx.wait("tensor", f"{tag}_dps", git - 1)
                for fcg in range(NDC):
                    ctx.op("tensor", lambda e, fcg=fcg, i=i, pb=pb, wb=wb: e.matmul(
                        psd[pb][:, :], fuseT[:, fcg * TPC + i * 128: fcg * TPC + (i + 1) * 128],
                        wd_sb[wb][:, fcg * 512:(fcg + 1) * 512],
                        start=(fcg == 0), stop=(fcg == NDC - 1)),
                        inc=(f"{tag}_dmm", 1) if fcg == NDC - 1 else None)
                ctx.wait("vector", f"{tag}_dmm", ctx.v(f"{tag}_dmm"))
                if e_i == 0:
                    ctx.op("vector", lambda e, i=i, ds=ds, pb=pb, e_i=e_i: e.tensor_scalar(
                        acc[:, i * D + ds * 512: i * D + (ds + 1) * 512], psd[pb][:, :],
                        wfinal[:, i * 8 + e_i: i * 8 + e_i + 1], None, OP.mult),
                        inc=(f"{tag}_dps", 1))
                else:
                    ctx.op("vector", lambda e, i=i, ds=ds, pb=pb, e_i=e_i: e.scalar_tensor_tensor(
                        acc[:, i * D + ds * 512: i * D + (ds + 1) * 512], psd[pb][:, :],
                        wfinal[:, i * 8 + e_i: i * 8 + e_i + 1],
                        acc[:, i * D + ds * 512: i * D + (ds + 1) * 512],
                        OP.mult, OP.add),
                        inc=(f"{tag}_dps", 1))
    ctx.wait("scalar", f"{tag}_dps", ctx.v(f"{tag}_dps"))
    for i in range(NT):
        for dsb in range(4):
            g = i * 4 + dsb
            b = g % 2
            if g >= 2:
                ctx.wait("scalar", f"{tag}_fgst", 16 * (g - 1))
            ctx.op("scalar", lambda e, i=i, dsb=dsb, b=b: e.activation(
                fg_stg[b][:, :], acc[:, i * D + dsb * 512: i * D + (dsb + 1) * 512], AF.Silu),
                inc=(f"{tag}_fgev", 1))
            ctx.wait("sync", f"{tag}_fgev", ctx.v(f"{tag}_fgev"))
            ctx.dma("sync", fg_out_dram[i * 128:(i + 1) * 128, dsb * 512:(dsb + 1) * 512],
                    fg_stg[b][:, :], f"{tag}_fgst")


def build_launch_A():
    nc = bass.Bass("TRN2", target_bir_lowering=False)
    x_tok = nc.dram_tensor("x_tok", [TPC, D], F32, kind="ExternalInput").ap()
    wq_h = nc.dram_tensor("wq_h", [NH, D, HD], F32, kind="ExternalInput").ap()
    wk_h = nc.dram_tensor("wk_h", [NH, D, HD], F32, kind="ExternalInput").ap()
    wv_c = nc.dram_tensor("wv_c", [4, D, 512], F32, kind="ExternalInput").ap()
    wg_c = nc.dram_tensor("wg_c", [4, D, 512], F32, kind="ExternalInput").ap()
    wr_d = nc.dram_tensor("wr_d", [128, NDC * E], F32, kind="ExternalInput").ap()
    temp_d = nc.dram_tensor("temp_d", [128, 1], F32, kind="ExternalInput").ap()
    mwg = nc.dram_tensor("mwg", [E, 4, D, 512], F32, kind="ExternalInput").ap()
    mwu = nc.dram_tensor("mwu", [E, 4, D, 512], F32, kind="ExternalInput").ap()
    mwd = nc.dram_tensor("mwd", [E, 4, F, 512], F32, kind="ExternalInput").ap()
    identd = nc.dram_tensor("identd", [128, 128], BF16, kind="ExternalInput").ap()
    rotTd = nc.dram_tensor("rotTd", [128, 128], BF16, kind="ExternalInput").ap()
    cosd = nc.dram_tensor("cosd", [128, TPC], F32, kind="ExternalInput").ap()
    sind = nc.dram_tensor("sind", [128, TPC], F32, kind="ExternalInput").ap()

    qt_o = nc.dram_tensor("qt_o", [D, TPC], BF16, kind="ExternalOutput").ap()
    kt_o = nc.dram_tensor("kt_o", [D, TPC], BF16, kind="ExternalOutput").ap()
    v_o = nc.dram_tensor("v_o", [TPC, D], BF16, kind="ExternalOutput").ap()
    sg_o = nc.dram_tensor("sg_o", [TPC, D], BF16, kind="ExternalOutput").ap()
    fg_o = nc.dram_tensor("fg_o", [TPC, D], F32, kind="ExternalOutput").ap()
    xn_o = nc.dram_tensor("xn_o", [TPC, D], BF16, kind="ExternalOutput").ap()
    wr_o = nc.dram_tensor("wr_o", [128, NT * E], F32, kind="ExternalOutput").ap()
    inh_o = nc.dram_tensor("inh_o", [128, 2 * NT * E], F32, kind="ExternalOutput").ap()

    with ExitStack() as stack:
        ctx = Ctx(nc, stack)
        xnT = ctx.sbuf("xnT", [128, NDC * TPC], BF16)
        ctx.ident = ctx.sbuf("identsb", [128, 128], BF16)
        ctx.rotT = ctx.sbuf("rotTsb", [128, 128], BF16)
        cosQ = ctx.sbuf("cosQ", [128, TPC], F32)
        sinQ = ctx.sbuf("sinQ", [128, TPC], F32)
        cosK = ctx.sbuf("cosK", [128, TPC], F32)
        sinK = ctx.sbuf("sinK", [128, TPC], F32)
        wr_sb = ctx.sbuf("wr_sb", [128, NDC * E], BF16)
        rtemp = ctx.sbuf("rtemp", [128, 1], F32)
        tempsb = ctx.sbuf("tempsb", [128, 1], F32)
        wfinal = ctx.sbuf("wfinal", [128, NT * E], F32)

        ctx.dma("sync", ctx.ident[:, :], identd, "cld")
        ctx.dma("sync", ctx.rotT[:, :], rotTd, "cld")
        ctx.dma("sync", cosK[:, :], cosd, "cld")
        ctx.dma("sync", sinK[:, :], sind, "cld")
        ctx.dma("sync", tempsb[:, :], temp_d, "cld")
        ctx.dma("gpsimd", wr_sb[:, :], wr_d, "cwld")
        ctx.wait("vector", "cld", ctx.v("cld"))
        ctx.op("vector", lambda e: e.tensor_single_scalar(cosQ[:, :], cosK[:, :], float(SCALE), OP.mult))
        ctx.op("vector", lambda e: e.tensor_single_scalar(sinQ[:, :], sinK[:, :], float(SCALE), OP.mult))
        ctx.op("vector", lambda e: e.reciprocal(rtemp[:, :], tempsb[:, :]))

        def phase(*emitfns):
            ctx.stage()
            with ExitStack() as ph:
                ctx.push_scope(ph)
                for fn in emitfns:
                    fn()
                ctx.pop_scope()

        inhdbg = ctx.sbuf("inhdbg", [128, 2 * NT * E], F32)
        phase(lambda: emit_ln_xnT(ctx, x_tok, xnT, xn_out_dram=xn_o, tag="ln0"))
        phase(lambda: emit_qk_proj(ctx, xnT, wq_h, qt_o, cosQ, sinQ, tag="q0"),
              lambda: emit_qk_proj(ctx, xnT, wk_h, kt_o, cosK, sinK, tag="k0"),
              lambda: emit_router(ctx, xnT, wr_sb, rtemp, wfinal, tag="rt", dbg=inhdbg))
        phase(lambda: emit_vg_proj(ctx, xnT, wv_c, v_o, tag="v0", silu=False),
              lambda: emit_vg_proj(ctx, xnT, wg_c, sg_o, tag="g0", silu=True))
        ctx.wait("sync", "rt_done", ctx.v("rt_done"))
        ctx.dma("sync", inh_o, inhdbg[:, :], "wr_st")
        ctx.wait("sync", "rt_done", ctx.v("rt_done"))
        ctx.dma("sync", wr_o, wfinal[:, :], "wr_st")
        phase(lambda: emit_moe(ctx, xnT, mwg, mwu, mwd, wfinal, fg_o, tag="moe"))

        ctx.stage()
        for s in ("ln0_xnst", "q0_qst", "k0_qst", "v0_st", "g0_st", "wr_st", "moe_fgst"):
            ctx.wait("sync", s, ctx.v(s))
        ctx.finalize()
    return nc


def inmaps_A(inputs, w, c):
    x = inputs["L0_attn_hidden_states"]
    maps = []
    for core in range(NCORE):
        b, t0 = core // 4, (core % 4) * TPC
        maps.append({
            "x_tok": np.ascontiguousarray(x[b, t0:t0 + TPC, :]),
            "wq_h": w["l0_wqT_h"], "wk_h": w["l0_wkT_h"],
            "wv_c": w["l0_wvT_c"], "wg_c": w["l0_wgT_c"],
            "wr_d": w["l0_Wr_t"], "temp_d": w["l0_temp_t"],
            "mwg": w["l0_Wg_t"], "mwu": w["l0_Wu_t"], "mwd": w["l0_Wd_t"],
            "identd": c["ident"], "rotTd": c["rotT"],
            "cosd": np.ascontiguousarray(c["cosF"][:, t0:t0 + TPC]),
            "sind": np.ascontiguousarray(c["sinF"][:, t0:t0 + TPC]),
        })
    return maps




def emit_attn_l0(ctx, qt_d, kt_d, v_d, mask_sb, o_d, tag="at0"):
    NKC = S // 128
    NSQB = S // 512
    GPH = sum(sqb + 1 for sqb in range(NSQB))
    qt = [ctx.sbuf(f"{tag}_qt{i}", [128, S], BF16) for i in range(2)]
    kt = [ctx.sbuf(f"{tag}_kt{i}", [128, S], BF16) for i in range(2)]
    ve = [ctx.sbuf(f"{tag}_ve{i}", [128, NKC * 129], BF16) for i in range(2)]
    pexp = [ctx.sbuf(f"{tag}_px{i}", [128, 2048], BF16) for i in range(2)]
    onrm = [ctx.sbuf(f"{tag}_on{i}", [128, 128], BF16) for i in range(2)]
    rden = ctx.sbuf(f"{tag}_rd", [128, 4], F32)
    ps_s = ctx.psum(f"{tag}_pss", [128, 2048])
    ps_o = [ctx.psum(f"{tag}_pso{i}", [128, 132]) for i in range(4)]

    def po_ap(par, st, a, b):
        return ps_o[st][:, a:b]
    for i in range(2):
        ctx.op("vector", lambda e, i=i: e.memset(
            ve[i][:, :].rearrange("p (c x) -> p c x", x=129)[:, :, 128:129], 1.0))
    ctx.nopinc("vector", f"{tag}_ones", 1)

    gtot = 0
    for j in range(4):
        hb = j % 2
        if j >= 2:
            ctx.wait("sync", f"{tag}_smm", (j - 1) * GPH)
            ctx.wait("sync", f"{tag}_av", (j - 1) * GPH)
        ctx.dma("sync", qt[hb][:, :], qt_d[j], f"{tag}_qld")
        ctx.dma("sync", kt[hb][:, :], kt_d[j], f"{tag}_kld")
        vsrc = v_d[j].rearrange("(c p) d -> p c d", p=128)
        vdst = ve[hb][:, :].rearrange("p (c x) -> p c x", x=129)[:, :, 0:128]
        ctx.dma("sync", vdst, vsrc, f"{tag}_vld")
        ldv = (ctx.v(f"{tag}_qld"), ctx.v(f"{tag}_kld"), ctx.v(f"{tag}_vld"))
        ctx.wait("tensor", f"{tag}_qld", ldv[0])
        ctx.wait("tensor", f"{tag}_kld", ldv[1])
        ctx.wait("tensor", f"{tag}_vld", ldv[2])
        if j == 0:
            ctx.wait("tensor", f"{tag}_ones", 1)
        for sqb in range(NSQB):
            ng = sqb + 1
            par = sqb % 2
            nrm_prev = ctx.v(f"{tag}_nrm")
            if nrm_prev > 0:
                ctx.wait("tensor", f"{tag}_nrm", nrm_prev)
            def emit_av(g, b):
                for kcl in range(4):
                    kc = 4 * g + kcl
                    for st in range(4):
                        ctx.op("tensor", lambda e, st=st, kcl=kcl, kc=kc, b=b,
                               hb=hb, g=g, ng=ng, par=par: e.matmul(
                            po_ap(par, st, 0, 129),
                            pexp[b][:, kcl * 512 + st * 128: kcl * 512 + (st + 1) * 128],
                            ve[hb][:, kc * 129:(kc + 1) * 129],
                            start=(g == 0 and kcl == 0), stop=(g == ng - 1 and kcl == 3)),
                            inc=(f"{tag}_av", 1) if (kcl == 3 and st == 3) else None)

            for g in range(ng):
                b = gtot % 2
                ctx.wait("tensor", f"{tag}_exp", ctx.v(f"{tag}_exp"))
                for kcl in range(4):
                    kc = 4 * g + kcl
                    ctx.op("tensor", lambda e, kcl=kcl, kc=kc, sqb=sqb, hb=hb: e.matmul(
                        ps_s[:, kcl * 512:(kcl + 1) * 512],
                        kt[hb][:, kc * 128:(kc + 1) * 128],
                        qt[hb][:, sqb * 512:sqb * 512 + 512], start=True, stop=True),
                        inc=(f"{tag}_smm", 1) if kcl == 3 else None)
                ctx.wait("scalar", f"{tag}_smm", ctx.v(f"{tag}_smm"))
                ctx.wait("scalar", f"{tag}_av", max(0, gtot - 1))
                ctx.op("scalar", lambda e, b=b: e.activation(
                    pexp[b][:, :], ps_s[:, :], AF.Exp),
                    inc=(f"{tag}_exp", 1))
                if g == sqb:
                    ctx.wait("vector", f"{tag}_exp", ctx.v(f"{tag}_exp"))
                    for kcl in range(4):
                        ctx.op("vector", lambda e, kcl=kcl, b=b: e.tensor_mul(
                            pexp[b][:, kcl * 512:(kcl + 1) * 512],
                            pexp[b][:, kcl * 512:(kcl + 1) * 512],
                            mask_sb[:, kcl * 512:(kcl + 1) * 512]),
                            inc=(f"{tag}_msk", 1) if kcl == 3 else None)
                if g > 0:
                    ctx.wait("tensor", f"{tag}_exp", ctx.v(f"{tag}_exp") - 1)
                    emit_av(g - 1, 1 - b)
                gtot += 1
            ctx.wait("tensor", f"{tag}_msk", ctx.v(f"{tag}_msk"))
            emit_av(ng - 1, (gtot - 1) % 2)
            avv = ctx.v(f"{tag}_av")
            ctx.wait("vector", f"{tag}_av", avv)
            for st in range(4):
                ctx.op("vector", lambda e, st=st, par=par: e.reciprocal(
                    rden[:, st:st + 1], po_ap(par, st, 128, 129)),
                    inc=(f"{tag}_rc", 1))
                ctx.wait("vector", f"{tag}_rc", ctx.v(f"{tag}_rc"))
                ob = st % 2
                if j > 0 or sqb > 0 or st >= 2:
                    ctx.wait("vector", f"{tag}_ost", max(0, 16 * (ctx.v(f"{tag}_nrm") - 1)))
                ctx.op("vector", lambda e, st=st, ob=ob, par=par: e.tensor_scalar(
                    onrm[ob][:, :], po_ap(par, st, 0, 128), rden[:, st:st + 1], None, OP.mult),
                    inc=(f"{tag}_nrm", 1))
                ctx.wait("sync", f"{tag}_nrm", ctx.v(f"{tag}_nrm"))
                ctx.dma("sync", o_d[j][sqb * 512 + st * 128: sqb * 512 + (st + 1) * 128, :],
                        onrm[ob][:, :], f"{tag}_ost")


def build_launch_B():
    nc = bass.Bass("TRN2", target_bir_lowering=False)
    qt_d = nc.dram_tensor("qt_d", [4, 128, S], BF16, kind="ExternalInput").ap()
    kt_d = nc.dram_tensor("kt_d", [4, 128, S], BF16, kind="ExternalInput").ap()
    v_d = nc.dram_tensor("v_d", [4, S, HD], BF16, kind="ExternalInput").ap()
    mask_d = nc.dram_tensor("mask_d", [128, 4 * 512], BF16, kind="ExternalInput").ap()
    o_d = nc.dram_tensor("o_d", [4, S, HD], BF16, kind="ExternalOutput").ap()
    with ExitStack() as stack:
        ctx = Ctx(nc, stack)
        mask_sb = ctx.sbuf("mask_sb", [128, 4 * 512], BF16)
        ctx.dma("sync", mask_sb[:, :], mask_d, "cld")
        ctx.wait("tensor", "cld", 16)
        ctx.wait("vector", "cld", 16)
        ctx.stage()
        emit_attn_l0(ctx, qt_d, kt_d, v_d, mask_sb, o_d, tag="at0")
        ctx.stage()
        ctx.wait("sync", "at0_ost", ctx.v("at0_ost"))
        ctx.finalize()
    return nc


def inmaps_B(a_results, c):
    maps = []
    m4 = np.ascontiguousarray(c["mask4"].transpose(1, 0, 2).reshape(128, 4 * 512))
    for core in range(NCORE):
        b, h0 = core // 4, (core % 4) * 4
        qt = np.zeros((4, 128, S), dtype=NPBF16)
        kt = np.zeros((4, 128, S), dtype=NPBF16)
        v = np.zeros((4, S, HD), dtype=NPBF16)
        for r in range(4):
            src = a_results[b * 4 + r]
            sl = slice(r * TPC, (r + 1) * TPC)
            for j in range(4):
                h = h0 + j
                qt[j][:, sl] = src["qt_o"][h * 128:(h + 1) * 128, :]
                kt[j][:, sl] = src["kt_o"][h * 128:(h + 1) * 128, :]
                v[j][sl, :] = src["v_o"][:, h * 128:(h + 1) * 128]
        maps.append({"qt_d": qt, "kt_d": kt, "v_d": v, "mask_d": m4})
    return maps




def emit_l0_epi(ctx, o_dram, sg_dram, normw_sb, gatedT, tag="ep0"):
    ot = [ctx.sbuf(f"{tag}_ot{i}", [128, D], BF16) for i in range(2)]
    sgt = [ctx.sbuf(f"{tag}_sg{i}", [128, D], BF16) for i in range(2)]
    gat = [ctx.sbuf(f"{tag}_ga{i}", [128, D], BF16) for i in range(2)]
    scr = ctx.sbuf(f"{tag}_scr", [128, 128], F32)
    nws = ctx.sbuf(f"{tag}_nws", [128, 128], F32)
    ssq = ctx.sbuf(f"{tag}_ssq", [128, NH], F32)
    rstd = ctx.sbuf(f"{tag}_rstd", [128, NH], F32)
    tp = [ctx.psum(f"{tag}_tp{i}", [128, 128], BF16) for i in range(2)]
    ident = ctx.ident
    for i in range(NT):
        b = i % 2
        if i >= 2:
            ctx.wait("sync", f"{tag}_ga", (i - 1) * NH)
            ctx.wait("sync", f"{tag}_sq", (i - 1) * NH)
        ctx.dma("sync", ot[b][:, :], o_dram[i * 128:(i + 1) * 128, :], f"{tag}_old")
        ctx.dma("sync", sgt[b][:, :], sg_dram[i * 128:(i + 1) * 128, :], f"{tag}_old")
        lv = ctx.v(f"{tag}_old")
        ctx.wait("scalar", f"{tag}_old", lv)
        if i >= 1:
            ctx.wait("scalar", f"{tag}_rc", i)
        for h in range(NH):
            ctx.op("scalar", lambda e, b=b, h=h: e.activation(
                scr[:, :], ot[b][:, h * 128:(h + 1) * 128], AF.Square,
                accum_out=ssq[:, h:h + 1]),
                inc=(f"{tag}_sq", 1))
        ctx.wait("scalar", f"{tag}_sq", ctx.v(f"{tag}_sq"))
        ctx.op("scalar", lambda e: e.activation(ssq[:, :], ssq[:, :], AF.Sqrt,
                                                scale=np.float32(1.0 / HD),
                                                bias=np.float32(1e-5)),
               inc=(f"{tag}_sqa", 1))
        ctx.wait("vector", f"{tag}_sqa", ctx.v(f"{tag}_sqa"))
        ctx.op("vector", lambda e: e.reciprocal(rstd[:, :], ssq[:, :]),
               inc=(f"{tag}_rc", 1))
        rcv = ctx.v(f"{tag}_rc")
        ctx.wait("vector", f"{tag}_rc", rcv)
        ctx.wait("vector", f"{tag}_old", lv)
        if i >= 2:
            ctx.wait("vector", f"{tag}_tp", (i - 1) * NDC)
        for h in range(NH):
            ctx.op("vector", lambda e, b=b, h=h: e.tensor_mul(
                nws[:, :], normw_sb[:, :], sgt[b][:, h * 128:(h + 1) * 128]),
                inc=(f"{tag}_nw", 1))
            ctx.wait("vector", f"{tag}_nw", ctx.v(f"{tag}_nw"))
            ctx.op("vector", lambda e, b=b, h=h: e.scalar_tensor_tensor(
                gat[b][:, h * 128:(h + 1) * 128], ot[b][:, h * 128:(h + 1) * 128],
                rstd[:, h:h + 1], nws[:, :], OP.mult, OP.mult),
                inc=(f"{tag}_ga", 1))
        gav = ctx.v(f"{tag}_ga")
        ctx.wait("tensor", f"{tag}_ga", gav)
        for dc in range(NDC):
            g = i * NDC + dc
            pb = g % 2
            if g >= 2:
                ctx.wait("tensor", f"{tag}_tev", g - 1)
            ctx.op("tensor", lambda e, b=b, dc=dc, pb=pb: e.transpose(
                tp[pb][:, :], gat[b][:, dc * 128:(dc + 1) * 128], ident[:, :]),
                inc=(f"{tag}_tp", 1))
            ctx.wait("vector", f"{tag}_tp", ctx.v(f"{tag}_tp"))
            ctx.op("vector", lambda e, dc=dc, i=i, pb=pb: e.tensor_copy(
                gatedT[:, dc * TPC + i * 128: dc * TPC + (i + 1) * 128], tp[pb][:, :]),
                inc=(f"{tag}_tev", 1))


def emit_out_proj(ctx, aT, woT_c_dram, x_dram, fg_dram, xout_dram, tag):
    wsb = ctx.sbuf(f"{tag}_w", [128, NDC * 512], BF16)
    xsb = [ctx.sbuf(f"{tag}_x{i}", [128, 512], F32) for i in range(2)]
    fsb = [ctx.sbuf(f"{tag}_f{i}", [128, 512], F32) for i in range(2)]
    t1 = ctx.sbuf(f"{tag}_t1", [128, 512], F32)
    stg = [ctx.sbuf(f"{tag}_stg{i}", [128, 512], F32) for i in range(2)]
    ps = [ctx.psum(f"{tag}_ps{i}", [128, 512]) for i in range(2)]
    for cb in range(4):
        if cb > 0:
            ctx.wait("gpsimd", f"{tag}_mm", cb * NT)
        srcw = woT_c_dram[cb].rearrange("(dc p) c -> p dc c", p=128)
        ctx.dma("gpsimd", wsb[:, :].rearrange("p (dc c) -> p dc c", dc=NDC), srcw, f"{tag}_wld")
        ctx.wait("tensor", f"{tag}_wld", ctx.v(f"{tag}_wld"))
        for i in range(NT):
            g = cb * NT + i
            b = g % 2
            if g >= 2:
                ctx.wait("sync", f"{tag}_ev", g - 1)
            ctx.dma("sync", xsb[b][:, :], x_dram[i * 128:(i + 1) * 128, cb * 512:(cb + 1) * 512],
                    f"{tag}_xld")
            ctx.dma("sync", fsb[b][:, :], fg_dram[i * 128:(i + 1) * 128, cb * 512:(cb + 1) * 512],
                    f"{tag}_xld")
            xlv = ctx.v(f"{tag}_xld")
            if g >= 2:
                ctx.wait("tensor", f"{tag}_ev", g - 1)
            for dc in range(NDC):
                ctx.op("tensor", lambda e, dc=dc, i=i, b=b: e.matmul(
                    ps[b][:, :], aT[:, dc * TPC + i * 128: dc * TPC + (i + 1) * 128],
                    wsb[:, dc * 512:(dc + 1) * 512],
                    start=(dc == 0), stop=(dc == NDC - 1)),
                    inc=(f"{tag}_mm", 1) if dc == NDC - 1 else None)
            ctx.wait("vector", f"{tag}_mm", ctx.v(f"{tag}_mm"))
            ctx.wait("vector", f"{tag}_xld", xlv)
            if g >= 2:
                ctx.wait("vector", f"{tag}_st", 16 * (g - 1))
            ctx.op("vector", lambda e, b=b: e.tensor_mul(t1[:, :], ps[b][:, :], fsb[b][:, :]),
                   inc=(f"{tag}_t1", 1))
            ctx.wait("vector", f"{tag}_t1", ctx.v(f"{tag}_t1"))
            ctx.op("vector", lambda e, b=b: e.tensor_add(stg[b][:, :], t1[:, :], xsb[b][:, :]),
                   inc=(f"{tag}_ev", 1))
            ctx.wait("sync", f"{tag}_ev", ctx.v(f"{tag}_ev"))
            ctx.dma("sync", xout_dram[i * 128:(i + 1) * 128, cb * 512:(cb + 1) * 512],
                    stg[b][:, :], f"{tag}_st")


def emit_q_norope(ctx, xnT, wT_h_dram, out_dram, scale, tag):
    wsb = [ctx.sbuf(f"{tag}_w{i}", [128, NDC * 128], BF16) for i in range(2)]
    rop = [ctx.sbuf(f"{tag}_rop{i}", [128, TPC], BF16) for i in range(2)]
    ps = [ctx.psum(f"{tag}_ps{i}", [128, 512]) for i in range(2)]
    NSQB = TPC // 512
    for h in range(NH):
        wb = h % 2
        if h >= 2:
            ctx.wait("gpsimd", f"{tag}_mm", (h - 1) * NSQB)
        src = wT_h_dram[h].rearrange("(dc p) c -> p dc c", p=128)
        ctx.dma("gpsimd", wsb[wb][:, :].rearrange("p (dc c) -> p dc c", dc=NDC), src, f"{tag}_wld")
        ctx.wait("tensor", f"{tag}_wld", 16 * (h + 1))
        rb = h % 2
        if h >= 2:
            ctx.wait("scalar", f"{tag}_qst", 16 * (h - 1))
        for sqb in range(NSQB):
            g = h * NSQB + sqb
            pb = g % 2
            if g >= 2:
                ctx.wait("tensor", f"{tag}_evac", g - 1)
            for dc in range(NDC):
                ctx.op("tensor", lambda e, dc=dc, sqb=sqb, pb=pb, wb=wb: e.matmul(
                    ps[pb][:, :], wsb[wb][:, dc * 128:(dc + 1) * 128],
                    xnT[:, dc * TPC + sqb * 512: dc * TPC + sqb * 512 + 512],
                    start=(dc == 0), stop=(dc == NDC - 1)),
                    inc=(f"{tag}_mm", 1) if dc == NDC - 1 else None)
            ctx.wait("scalar", f"{tag}_mm", ctx.v(f"{tag}_mm"))
            if scale == 1.0:
                ctx.op("scalar", lambda e, sqb=sqb, pb=pb, rb=rb: e.copy(
                    rop[rb][:, sqb * 512:sqb * 512 + 512], ps[pb][:, :]),
                    inc=(f"{tag}_evac", 1))
            else:
                ctx.op("scalar", lambda e, sqb=sqb, pb=pb, rb=rb: e.activation(
                    rop[rb][:, sqb * 512:sqb * 512 + 512], ps[pb][:, :], AF.Identity,
                    scale=np.float32(scale)),
                    inc=(f"{tag}_evac", 1))
        ctx.wait("sync", f"{tag}_evac", ctx.v(f"{tag}_evac"))
        ctx.dma("sync", out_dram[h * 128:(h + 1) * 128, :], rop[rb][:, :], f"{tag}_qst")


def emit_ffn1(ctx, xnT, wg_dram, wu_dram, wd_dram, sg_sc, su_sc, sd_bc, fg_out, tag="f1"):
    NFC = INTER // 128
    wgp = [ctx.sbuf(f"{tag}_wg{i}", [128, NDC * 128], BF16) for i in range(2)]
    wup = [ctx.sbuf(f"{tag}_wu{i}", [128, NDC * 128], BF16) for i in range(2)]
    wd_sb = [ctx.sbuf(f"{tag}_wd{i}", [128, NDC * 512], BF16) for i in range(2)]
    fuse = ctx.sbuf(f"{tag}_fuse", [128, NFC * 512], BF16)
    sgs = [ctx.sbuf(f"{tag}_sgs{i}", [128, 512], BF16) for i in range(2)]
    t1 = ctx.sbuf(f"{tag}_t1", [128, 512], F32)
    stg = [ctx.sbuf(f"{tag}_stg{i}", [128, 512], F32) for i in range(2)]
    psg = [ctx.psum(f"{tag}_psg{i}", [128, 512]) for i in range(2)]
    psu = [ctx.psum(f"{tag}_psu{i}", [128, 512]) for i in range(2)]
    psd = [ctx.psum(f"{tag}_psd{i}", [128, 512]) for i in range(4)]

    for sqb in range(TPC // 512):
        for fs in range(16):
            for fc in range(4):
                fcg = fs * 4 + fc
                git = (sqb * 16 + fs) * 4 + fc
                pb = git % 2
                if git >= 2:
                    ctx.wait("gpsimd", f"{tag}_gmm", git - 1)
                    ctx.wait("gpsimd", f"{tag}_umm", git - 1)
                srcg = wg_dram[fs][:, fc * 128:(fc + 1) * 128].rearrange(
                    "(c p) f -> p c f", p=128)
                ctx.dma("gpsimd", wgp[pb][:, :].rearrange("p (c f) -> p c f", c=NDC),
                        srcg, f"{tag}_wgld")
                srcu = wu_dram[fs][:, fc * 128:(fc + 1) * 128].rearrange(
                    "(c p) f -> p c f", p=128)
                ctx.dma("gpsimd", wup[pb][:, :].rearrange("p (c f) -> p c f", c=NDC),
                        srcu, f"{tag}_wuld")
                ctx.wait("tensor", f"{tag}_wgld", 16 * (git + 1))
                ctx.wait("tensor", f"{tag}_wuld", 16 * (git + 1))
                if git >= 2:
                    ctx.wait("tensor", f"{tag}_silu", git - 1)
                    ctx.wait("tensor", f"{tag}_fuse", git - 1)
                for dc in range(NDC):
                    ctx.op("tensor", lambda e, dc=dc, sqb=sqb, pb=pb: e.matmul(
                        psg[pb][:, :], wgp[pb][:, dc * 128:(dc + 1) * 128],
                        xnT[:, dc * TPC + sqb * 512: dc * TPC + sqb * 512 + 512],
                        start=(dc == 0), stop=(dc == NDC - 1)),
                        inc=(f"{tag}_gmm", 1) if dc == NDC - 1 else None)
                for dc in range(NDC):
                    ctx.op("tensor", lambda e, dc=dc, sqb=sqb, pb=pb: e.matmul(
                        psu[pb][:, :], wup[pb][:, dc * 128:(dc + 1) * 128],
                        xnT[:, dc * TPC + sqb * 512: dc * TPC + sqb * 512 + 512],
                        start=(dc == 0), stop=(dc == NDC - 1)),
                        inc=(f"{tag}_umm", 1) if dc == NDC - 1 else None)
                ctx.wait("scalar", f"{tag}_gmm", ctx.v(f"{tag}_gmm"))
                if git >= 2:
                    ctx.wait("scalar", f"{tag}_fuse", git - 1)
                ctx.op("scalar", lambda e, fcg=fcg, pb=pb: e.activation(
                    sgs[pb][:, :], psg[pb][:, :], AF.Silu, scale=sg_sc[:, fcg:fcg + 1]),
                    inc=(f"{tag}_silu", 1))
                ctx.wait("vector", f"{tag}_silu", ctx.v(f"{tag}_silu"))
                ctx.wait("vector", f"{tag}_umm", ctx.v(f"{tag}_umm"))
                ctx.op("vector", lambda e, fcg=fcg, pb=pb: e.scalar_tensor_tensor(
                    fuse[:, fcg * 512:(fcg + 1) * 512], psu[pb][:, :],
                    su_sc[:, fcg:fcg + 1], sgs[pb][:, :], OP.mult, OP.mult),
                    inc=(f"{tag}_fuse", 1))
        fusev = ctx.v(f"{tag}_fuse")
        for ds in range(4):
            for fs2 in range(4):
                lidx = (sqb * 4 + ds) * 4 + fs2
                wb = lidx % 2
                if lidx >= 2:
                    ctx.wait("gpsimd", f"{tag}_dmm", (lidx - 1) * 4)
                srcd = wd_dram[ds][fs2 * 2048:(fs2 + 1) * 2048, :].rearrange(
                    "(c p) f -> p c f", p=128)
                ctx.dma("gpsimd", wd_sb[wb][:, :].rearrange("p (c f) -> p c f", c=NDC),
                        srcd, f"{tag}_wdld")
                ctx.wait("tensor", f"{tag}_wdld", 16 * (lidx + 1))
                if fs2 == 0:
                    if ds == 0:
                        ctx.wait("tensor", f"{tag}_fuse", fusev)
                    ctx.wait("tensor", f"{tag}_ev", ctx.v(f"{tag}_ev"))
                for sqt in range(4):
                    for fcl in range(NDC):
                        fcg = fs2 * NDC + fcl
                        ctx.op("tensor", lambda e, fcg=fcg, sqt=sqt, fs2=fs2, fcl=fcl,
                               wb=wb: e.matmul(
                            psd[sqt][:, :],
                            fuse[:, fcg * 512 + sqt * 128: fcg * 512 + (sqt + 1) * 128],
                            wd_sb[wb][:, fcl * 512:(fcl + 1) * 512],
                            start=(fs2 == 0 and fcl == 0), stop=(fs2 == 3 and fcl == NDC - 1)),
                            inc=(f"{tag}_dmm", 1) if fcl == NDC - 1 else None)
            ctx.wait("vector", f"{tag}_dmm", ctx.v(f"{tag}_dmm"))
            for sqt in range(4):
                g = (sqb * 4 + ds) * 4 + sqt
                b = g % 2
                ctx.wait("vector", f"{tag}_ev", ctx.v(f"{tag}_ev"))
                ctx.op("vector", lambda e, sqt=sqt, ds=ds: e.tensor_mul(
                    t1[:, :], psd[sqt][:, :], sd_bc[:, ds * 512:(ds + 1) * 512]),
                    inc=(f"{tag}_t1", 1))
                ctx.wait("scalar", f"{tag}_t1", ctx.v(f"{tag}_t1"))
                if g >= 2:
                    ctx.wait("scalar", f"{tag}_fgst", 16 * (g - 1))
                ctx.op("scalar", lambda e, b=b: e.activation(stg[b][:, :], t1[:, :], AF.Silu),
                       inc=(f"{tag}_ev", 1))
                ctx.wait("sync", f"{tag}_ev", ctx.v(f"{tag}_ev"))
                row = sqb * 512 + sqt * 128
                ctx.dma("sync", fg_out[row:row + 128, ds * 512:(ds + 1) * 512], stg[b][:, :],
                        f"{tag}_fgst")


def build_launch_C():
    nc = bass.Bass("TRN2", target_bir_lowering=False)
    o_d = nc.dram_tensor("o_d", [TPC, D], BF16, kind="ExternalInput").ap()
    sg_d = nc.dram_tensor("sg_d", [TPC, D], BF16, kind="ExternalInput").ap()
    x_d = nc.dram_tensor("x_d", [TPC, D], F32, kind="ExternalInput").ap()
    fg_d = nc.dram_tensor("fg_d", [TPC, D], F32, kind="ExternalInput").ap()
    normw_d = nc.dram_tensor("normw_d", [128, 128], F32, kind="ExternalInput").ap()
    wo_c = nc.dram_tensor("wo_c", [4, D, 512], F32, kind="ExternalInput").ap()
    wq_h = nc.dram_tensor("wq_h", [NH, D, HD], F32, kind="ExternalInput").ap()
    wk_h = nc.dram_tensor("wk_h", [NH, D, HD], F32, kind="ExternalInput").ap()
    wv_c = nc.dram_tensor("wv_c", [4, D, 512], F32, kind="ExternalInput").ap()
    wg1_d = nc.dram_tensor("wg1_d", [16, D, 512], I8, kind="ExternalInput").ap()
    wu1_d = nc.dram_tensor("wu1_d", [16, D, 512], I8, kind="ExternalInput").ap()
    wd1_d = nc.dram_tensor("wd1_d", [4, INTER, 512], I8, kind="ExternalInput").ap()
    sg_sc_d = nc.dram_tensor("sg_sc_d", [128, 64], F32, kind="ExternalInput").ap()
    su_sc_d = nc.dram_tensor("su_sc_d", [128, 64], F32, kind="ExternalInput").ap()
    sd_bc_d = nc.dram_tensor("sd_bc_d", [128, D], F32, kind="ExternalInput").ap()
    identd = nc.dram_tensor("identd", [128, 128], BF16, kind="ExternalInput").ap()

    x1_o = nc.dram_tensor("x1_o", [TPC, D], F32, kind="ExternalOutput").ap()
    qt1_o = nc.dram_tensor("qt1_o", [D, TPC], BF16, kind="ExternalOutput").ap()
    kt1_o = nc.dram_tensor("kt1_o", [D, TPC], BF16, kind="ExternalOutput").ap()
    v1_o = nc.dram_tensor("v1_o", [TPC, D], BF16, kind="ExternalOutput").ap()
    fg1_o = nc.dram_tensor("fg1_o", [TPC, D], F32, kind="ExternalOutput").ap()
    xn1_o = nc.dram_tensor("xn1_o", [TPC, D], BF16, kind="ExternalOutput").ap()

    with ExitStack() as stack:
        ctx = Ctx(nc, stack)
        ctx.ident = ctx.sbuf("identsb", [128, 128], BF16)
        gatedT = ctx.sbuf("gatedT", [128, NDC * TPC], BF16)
        xn1T = ctx.sbuf("xn1T", [128, NDC * TPC], BF16)
        normw_sb = ctx.sbuf("normw_sb", [128, 128], F32)
        sg_sc = ctx.sbuf("sg_sc", [128, 64], F32)
        su_sc = ctx.sbuf("su_sc", [128, 64], F32)
        sd_bc = ctx.sbuf("sd_bc", [128, D], F32)
        ctx.dma("sync", ctx.ident[:, :], identd, "cld")
        ctx.dma("sync", normw_sb[:, :], normw_d, "cld")
        ctx.dma("sync", sg_sc[:, :], sg_sc_d, "cld")
        ctx.dma("sync", su_sc[:, :], su_sc_d, "cld")
        ctx.dma("sync", sd_bc[:, :], sd_bc_d, "cld")

        def phase(*emitfns):
            ctx.stage()
            with ExitStack() as ph:
                ctx.push_scope(ph)
                for fn in emitfns:
                    fn()
                ctx.pop_scope()

        phase(lambda: emit_l0_epi(ctx, o_d, sg_d, normw_sb, gatedT, tag="ep0"))
        phase(lambda: emit_out_proj(ctx, gatedT, wo_c, x_d, fg_d, x1_o, tag="op0"))
        phase(lambda: emit_ln_xnT(ctx, x1_o, xn1T, xn_out_dram=xn1_o, tag="ln1"))
        phase(lambda: emit_q_norope(ctx, xn1T, wq_h, qt1_o, float(SCALE), tag="q1"),
              lambda: emit_q_norope(ctx, xn1T, wk_h, kt1_o, 1.0, tag="k1"),
              lambda: emit_vg_proj(ctx, xn1T, wv_c, v1_o, tag="v1", silu=False))
        phase(lambda: emit_ffn1(ctx, xn1T, wg1_d, wu1_d, wd1_d, sg_sc, su_sc, sd_bc,
                                fg1_o, tag="f1"))
        ctx.stage()
        for s_ in ("op0_st", "ln1_xnst", "q1_qst", "k1_qst", "v1_st", "f1_fgst"):
            ctx.wait("sync", s_, ctx.v(s_))
        ctx.finalize()
    return nc


def inmaps_C(inputs, w, c, a_results, b_results):
    x = inputs["L0_attn_hidden_states"]
    maps = []
    for core in range(NCORE):
        b, t0 = core // 4, (core % 4) * TPC
        O_tok = np.zeros((TPC, D), dtype=NPBF16)
        for hc in range(4):
            src = b_results[b * 4 + hc]["o_d"]
            for j in range(4):
                h = hc * 4 + j
                O_tok[:, h * 128:(h + 1) * 128] = src[j][t0:t0 + TPC, :]
        maps.append({
            "o_d": O_tok,
            "sg_d": a_results[core]["sg_o"],
            "fg_d": a_results[core]["fg_o"],
            "x_d": np.ascontiguousarray(x[b, t0:t0 + TPC, :]),
            "normw_d": w["l0_normw_t"],
            "wo_c": w["l0_woT_c"],
            "wq_h": w["l1_wqT_h"], "wk_h": w["l1_wkT_h"], "wv_c": w["l1_wvT_c"],
            "wg1_d": w["l1_wg_t"], "wu1_d": w["l1_wu_t"], "wd1_d": w["l1_wd_t"],
            "sg_sc_d": w["l1_sg_t"], "su_sc_d": w["l1_su_t"], "sd_bc_d": w["l1_sd_t"],
            "identd": c["ident"],
        })
    return maps




def emit_lam(ctx, lam_d, ones_d, nlam_b, tag="lam"):
    lamv = ctx.sbuf(f"{tag}_v", [1, 128], F32)
    lscr = ctx.sbuf(f"{tag}_scr", [1, 128], F32)
    lamt = ctx.sbuf(f"{tag}_t", [1, 8], F32)
    onesb = ctx.sbuf(f"{tag}_ones1", [1, 128], F32)
    ps_l = ctx.psum(f"{tag}_ps", [128, 4])
    ctx.dma("sync", onesb[:, :], ones_d, f"{tag}_ld")
    for idx, (qrow, krow) in enumerate(((0, 2), (1, 3))):
        ctx.dma("sync", lamv[:, :], lam_d[qrow:qrow + 1, :], f"{tag}_ld")
        ctx.dma("sync", lscr[:, :], lam_d[krow:krow + 1, :], f"{tag}_ld")
        lv = ctx.v(f"{tag}_ld")
        ctx.wait("vector", f"{tag}_ld", lv)
        ctx.op("vector", lambda e: e.tensor_mul(lamv[:, :], lamv[:, :], lscr[:, :]),
               inc=(f"{tag}_ch", 1))
        ctx.wait("vector", f"{tag}_ch", ctx.v(f"{tag}_ch"))
        ctx.op("vector", lambda e, idx=idx: e.reduce_sum(lamt[:, idx:idx + 1], lamv[:, :], AX.X),
               inc=(f"{tag}_ch", 1))
        ctx.wait("vector", f"{tag}_ch", ctx.v(f"{tag}_ch"))
        ctx.wait("sync", f"{tag}_ch", ctx.v(f"{tag}_ch"))
    ctx.wait("scalar", f"{tag}_ch", ctx.v(f"{tag}_ch"))
    ctx.op("scalar", lambda e: e.activation(lamt[:, 2:4], lamt[:, 0:2], AF.Exp),
           inc=(f"{tag}_sc", 1))
    ctx.wait("vector", f"{tag}_sc", ctx.v(f"{tag}_sc"))
    ctx.op("vector", lambda e: e.tensor_sub(lamt[:, 4:5], lamt[:, 3:4], lamt[:, 2:3]),
           inc=(f"{tag}_ch", 1))
    ctx.wait("vector", f"{tag}_ch", ctx.v(f"{tag}_ch"))
    ctx.op("vector", lambda e: e.tensor_single_scalar(lamt[:, 5:6], lamt[:, 4:5], -0.8, OP.add),
           inc=(f"{tag}_ch", 1))
    ctx.wait("tensor", f"{tag}_ch", ctx.v(f"{tag}_ch"))
    ctx.op("tensor", lambda e: e.matmul(ps_l[:, 0:1], onesb[:, :], lamt[:, 5:6],
                                        start=True, stop=True),
           inc=(f"{tag}_mm", 1))
    ctx.wait("vector", f"{tag}_mm", ctx.v(f"{tag}_mm"))
    ctx.op("vector", lambda e: e.tensor_copy(nlam_b[:, :], ps_l[:, 0:1]),
           inc=(f"{tag}_bc", 1))


def emit_attn_l1(ctx, qt_d, kt_d, v_d, nlam_b, o_d, tag="at1"):
    NKC = S // 128
    SQB = 256
    NSQB = S // SQB
    GPH = NSQB * (NKC // 2)
    qt = [ctx.sbuf(f"{tag}_qt{i}", [128, S], BF16) for i in range(2)]
    kt = [ctx.sbuf(f"{tag}_kt{i}", [128, S], BF16) for i in range(2)]
    ve = [ctx.sbuf(f"{tag}_ve{i}", [128, NKC * 129], BF16) for i in range(2)]
    pexp = [ctx.sbuf(f"{tag}_px{i}", [128, 1024], BF16) for i in range(2)]
    onrm = [ctx.sbuf(f"{tag}_on{i}", [128, 128], BF16) for i in range(2)]
    rr = ctx.sbuf(f"{tag}_rr", [128, 8], F32)
    oscr = ctx.sbuf(f"{tag}_oscr", [128, 128], F32)
    sscr = ctx.sbuf(f"{tag}_sscr", [128, 128], F32)
    ps_s = ctx.psum(f"{tag}_pss", [128, 2048])
    ps_o = [ctx.psum(f"{tag}_pso{i}", [128, 132]) for i in range(4)]

    def po_ap(par, idx, a, b):
        return ps_o[idx][:, a:b]
    for i in range(2):
        ctx.op("vector", lambda e, i=i: e.memset(
            ve[i][:, :].rearrange("p (c x) -> p c x", x=129)[:, :, 128:129], 1.0))
    ctx.nopinc("vector", f"{tag}_ones", 1)

    gtot = 0
    for j in range(4):
        hb = j % 2
        if j >= 2:
            ctx.wait("sync", f"{tag}_smm", (j - 1) * GPH)
            ctx.wait("sync", f"{tag}_av", (j - 1) * GPH)
        ctx.dma("sync", qt[hb][:, :], qt_d[j], f"{tag}_qld")
        ctx.dma("sync", kt[hb][:, :], kt_d[j], f"{tag}_kld")
        vsrc = v_d[j].rearrange("(c p) d -> p c d", p=128)
        vdst = ve[hb][:, :].rearrange("p (c x) -> p c x", x=129)[:, :, 0:128]
        ctx.dma("sync", vdst, vsrc, f"{tag}_vld")
        ldv = (ctx.v(f"{tag}_qld"), ctx.v(f"{tag}_kld"), ctx.v(f"{tag}_vld"))
        ctx.wait("tensor", f"{tag}_qld", ldv[0])
        ctx.wait("tensor", f"{tag}_kld", ldv[1])
        ctx.wait("tensor", f"{tag}_vld", ldv[2])
        if j == 0:
            ctx.wait("tensor", f"{tag}_ones", 1)
        for sqb in range(NSQB):
            par = sqb % 2
            nrm_prev = ctx.v(f"{tag}_nrm")
            if nrm_prev > 0:
                ctx.wait("tensor", f"{tag}_nrm", nrm_prev)
            def emit_av(g, b):
                for kcl in range(2):
                    kc = 2 * g + kcl
                    for st in range(2):
                        ctx.op("tensor", lambda e, st=st, kc=kc, kcl=kcl, b=b, hb=hb,
                               par=par: e.matmul(
                            po_ap(par, st, 0, 129),
                            pexp[b][:, kcl * 512 + st * 128: kcl * 512 + (st + 1) * 128],
                            ve[hb][:, kc * 129:(kc + 1) * 129],
                            start=(kc == 0), stop=(kc == NKC - 1)))
                    for st in range(2):
                        ctx.op("tensor", lambda e, st=st, kc=kc, kcl=kcl, b=b, hb=hb,
                               par=par: e.matmul(
                            po_ap(par, 2 + st, 0, 129),
                            pexp[b][:, kcl * 512 + 256 + st * 128: kcl * 512 + 256 + (st + 1) * 128],
                            ve[hb][:, kc * 129:(kc + 1) * 129],
                            start=(kc == 0), stop=(kc == NKC - 1)),
                            inc=(f"{tag}_av", 1) if (kcl == 1 and st == 1) else None)

            ps_s3 = ps_s[:, :].rearrange("p (k x) -> p k x", k=4)[:, :, 0:256]
            for g in range(NKC // 2):
                b = gtot % 2
                ctx.wait("tensor", f"{tag}_exp", ctx.v(f"{tag}_exp"))
                for kcl in range(2):
                    kc = 2 * g + kcl
                    ctx.op("tensor", lambda e, kc=kc, kcl=kcl, sqb=sqb, hb=hb: e.matmul(
                        ps_s[:, kcl * 1024: kcl * 1024 + 256],
                        kt[hb][0:64, kc * 128:(kc + 1) * 128],
                        qt[hb][0:64, sqb * SQB:(sqb + 1) * SQB], start=True, stop=True))
                    ctx.op("tensor", lambda e, kc=kc, kcl=kcl, sqb=sqb, hb=hb: e.matmul(
                        ps_s[:, kcl * 1024 + 512: kcl * 1024 + 768],
                        kt[hb][64:128, kc * 128:(kc + 1) * 128],
                        qt[hb][64:128, sqb * SQB:(sqb + 1) * SQB], start=True, stop=True),
                        inc=(f"{tag}_smm", 1) if kcl == 1 else None)
                ctx.wait("scalar", f"{tag}_smm", ctx.v(f"{tag}_smm"))
                ctx.wait("scalar", f"{tag}_av", max(0, gtot - 1))
                ctx.op("scalar", lambda e, b=b: e.activation(
                    pexp[b][:, :].rearrange("p (k x) -> p k x", k=4), ps_s3, AF.Exp),
                    inc=(f"{tag}_exp", 1))
                if g > 0:
                    ctx.wait("tensor", f"{tag}_exp", ctx.v(f"{tag}_exp") - 1)
                    emit_av(g - 1, 1 - b)
                gtot += 1
            ctx.wait("tensor", f"{tag}_exp", ctx.v(f"{tag}_exp"))
            emit_av(NKC // 2 - 1, (gtot - 1) % 2)
            avv = ctx.v(f"{tag}_av")
            ctx.wait("vector", f"{tag}_av", avv)
            for st in range(2):
                ctx.op("vector", lambda e, st=st, par=par: e.reciprocal(
                    rr[:, 0:1], po_ap(par, st, 128, 129)),
                    inc=(f"{tag}_rc", 1))
                ctx.op("vector", lambda e, st=st, par=par: e.reciprocal(
                    rr[:, 1:2], po_ap(par, 2 + st, 128, 129)),
                    inc=(f"{tag}_rc", 1))
                ctx.wait("vector", f"{tag}_rc", ctx.v(f"{tag}_rc"))
                ctx.op("vector", lambda e: e.tensor_mul(rr[:, 2:3], rr[:, 1:2], nlam_b[:, :]),
                       inc=(f"{tag}_rc", 1))
                ctx.wait("vector", f"{tag}_rc", ctx.v(f"{tag}_rc"))
                ctx.op("vector", lambda e, st=st, par=par: e.tensor_scalar(
                    oscr[:, :], po_ap(par, st, 0, 128), rr[:, 0:1], None, OP.mult),
                    inc=(f"{tag}_rc", 1))
                ctx.wait("vector", f"{tag}_rc", ctx.v(f"{tag}_rc"))
                ctx.op("vector", lambda e, st=st, par=par: e.scalar_tensor_tensor(
                    oscr[:, :], po_ap(par, 2 + st, 0, 128), rr[:, 2:3], oscr[:, :], OP.mult, OP.add),
                    inc=(f"{tag}_oc", 1))
                ocv = ctx.v(f"{tag}_oc")
                ctx.wait("scalar", f"{tag}_oc", ocv)
                ctx.op("scalar", lambda e: e.activation(sscr[:, :], oscr[:, :], AF.Square,
                                                        accum_out=rr[:, 3:4]),
                       inc=(f"{tag}_sq", 1))
                ctx.wait("scalar", f"{tag}_sq", ctx.v(f"{tag}_sq"))
                ctx.op("scalar", lambda e: e.activation(rr[:, 4:5], rr[:, 3:4], AF.Sqrt,
                                                        scale=np.float32(1.0 / HD),
                                                        bias=np.float32(1.1920928955078125e-07)),
                       inc=(f"{tag}_sqa", 1))
                ctx.wait("vector", f"{tag}_sqa", ctx.v(f"{tag}_sqa"))
                ctx.op("vector", lambda e: e.reciprocal(rr[:, 5:6], rr[:, 4:5]),
                       inc=(f"{tag}_rc", 1))
                ctx.wait("vector", f"{tag}_rc", ctx.v(f"{tag}_rc"))
                ctx.op("vector", lambda e: e.tensor_single_scalar(rr[:, 6:7], rr[:, 5:6], 0.2, OP.mult),
                       inc=(f"{tag}_rc", 1))
                ctx.wait("vector", f"{tag}_rc", ctx.v(f"{tag}_rc"))
                ob = st % 2
                g = ctx.v(f"{tag}_nrm")
                if g >= 2:
                    ctx.wait("vector", f"{tag}_ost", 16 * (g - 1))
                ctx.op("vector", lambda e, ob=ob: e.tensor_scalar(
                    onrm[ob][:, :], oscr[:, :], rr[:, 6:7], None, OP.mult),
                    inc=(f"{tag}_nrm", 1))
                ctx.wait("sync", f"{tag}_nrm", ctx.v(f"{tag}_nrm"))
                row = sqb * SQB + st * 128
                ctx.dma("sync", o_d[j][row:row + 128, :], onrm[ob][:, :], f"{tag}_ost")


def build_launch_D():
    nc = bass.Bass("TRN2", target_bir_lowering=False)
    qt_d = nc.dram_tensor("qt_d", [4, 128, S], BF16, kind="ExternalInput").ap()
    kt_d = nc.dram_tensor("kt_d", [4, 128, S], BF16, kind="ExternalInput").ap()
    v_d = nc.dram_tensor("v_d", [4, S, HD], BF16, kind="ExternalInput").ap()
    lam_d = nc.dram_tensor("lam_d", [4, 128], F32, kind="ExternalInput").ap()
    ones_d = nc.dram_tensor("ones_d", [1, 128], F32, kind="ExternalInput").ap()
    o_d = nc.dram_tensor("o_d", [4, S, HD], BF16, kind="ExternalOutput").ap()
    with ExitStack() as stack:
        ctx = Ctx(nc, stack)
        nlam_b = ctx.sbuf("nlam_b", [128, 1], F32)
        with ExitStack() as ph:
            ctx.push_scope(ph)
            emit_lam(ctx, lam_d, ones_d, nlam_b, tag="lam")
            ctx.pop_scope()
        ctx.stage()
        with ExitStack() as ph:
            ctx.push_scope(ph)
            emit_attn_l1(ctx, qt_d, kt_d, v_d, nlam_b, o_d, tag="at1")
            ctx.pop_scope()
        ctx.stage()
        ctx.wait("sync", "at1_ost", ctx.v("at1_ost"))
        ctx.finalize()
    return nc


def inmaps_D(w, c_results):
    lam = np.stack([w["l1_lambda_q1"][0], w["l1_lambda_q2"][0],
                    w["l1_lambda_k1"][0], w["l1_lambda_k2"][0]]).astype(np.float32)
    ones = np.ones((1, 128), dtype=np.float32)
    maps = []
    for core in range(NCORE):
        b, h0 = core // 4, (core % 4) * 4
        qt = np.zeros((4, 128, S), dtype=NPBF16)
        kt = np.zeros((4, 128, S), dtype=NPBF16)
        v = np.zeros((4, S, HD), dtype=NPBF16)
        for r in range(4):
            src = c_results[b * 4 + r]
            sl = slice(r * TPC, (r + 1) * TPC)
            for j in range(4):
                h = h0 + j
                qt[j][:, sl] = src["qt1_o"][h * 128:(h + 1) * 128, :]
                kt[j][:, sl] = src["kt1_o"][h * 128:(h + 1) * 128, :]
                v[j][sl, :] = src["v1_o"][:, h * 128:(h + 1) * 128]
        maps.append({"qt_d": qt, "kt_d": kt, "v_d": v, "lam_d": lam, "ones_d": ones})
    return maps


def build_launch_E():
    nc = bass.Bass("TRN2", target_bir_lowering=False)
    tT_d = nc.dram_tensor("tT_d", [D, TPC], BF16, kind="ExternalInput").ap()
    wo_c = nc.dram_tensor("wo_c", [4, D, 512], F32, kind="ExternalInput").ap()
    x_d = nc.dram_tensor("x_d", [TPC, D], F32, kind="ExternalInput").ap()
    fg_d = nc.dram_tensor("fg_d", [TPC, D], F32, kind="ExternalInput").ap()
    x2_o = nc.dram_tensor("x2_o", [TPC, D], F32, kind="ExternalOutput").ap()
    with ExitStack() as stack:
        ctx = Ctx(nc, stack)
        tT = ctx.sbuf("tT", [128, NDC * TPC], BF16)
        srcw = tT_d.rearrange("(dc p) c -> p dc c", p=128)
        ctx.dma("sync", tT[:, :].rearrange("p (dc c) -> p dc c", dc=NDC), srcw, "tld")
        ctx.wait("tensor", "tld", 16)
        ctx.stage()
        with ExitStack() as ph:
            ctx.push_scope(ph)
            emit_out_proj(ctx, tT, wo_c, x_d, fg_d, x2_o, tag="op1")
            ctx.pop_scope()
        ctx.stage()
        ctx.wait("sync", "op1_st", ctx.v("op1_st"))
        ctx.finalize()
    return nc


def inmaps_E(inputs, w, d_results, c_results):
    normed = np.zeros((B, NH, S, HD), dtype=NPBF16)
    for core in range(NCORE):
        b, h0 = core // 4, (core % 4) * 4
        for j in range(4):
            normed[b, h0 + j] = d_results[core]["o_d"][j]
    t = normed.reshape(B, NH * S, HD).transpose(0, 2, 1).reshape(B, S, D)
    maps = []
    for core in range(NCORE):
        b, t0 = core // 4, (core % 4) * TPC
        tT = np.ascontiguousarray(t[b, t0:t0 + TPC, :].T)
        maps.append({
            "tT_d": tT,
            "wo_c": w["l1_woT_c"],
            "x_d": c_results[core]["x1_o"],
            "fg_d": c_results[core]["fg1_o"],
        })
    return maps


def kernel(**inputs):
    c = host_consts()
    w = prep_weights(inputs)

    nc = build_launch_A()
    res = run_bass_kernel_spmd(nc, inmaps_A(inputs, w, c), core_ids=list(range(NCORE)))
    a_results = res.results

    nc = build_launch_B()
    res = run_bass_kernel_spmd(nc, inmaps_B(a_results, c), core_ids=list(range(NCORE)))
    b_results = res.results

    nc = build_launch_C()
    res = run_bass_kernel_spmd(nc, inmaps_C(inputs, w, c, a_results, b_results),
                               core_ids=list(range(NCORE)))
    c_results = res.results

    nc = build_launch_D()
    res = run_bass_kernel_spmd(nc, inmaps_D(w, c_results), core_ids=list(range(NCORE)))
    d_results = res.results

    nc = build_launch_E()
    res = run_bass_kernel_spmd(nc, inmaps_E(inputs, w, d_results, c_results),
                               core_ids=list(range(NCORE)))
    e_results = res.results

    out = np.zeros((B, S, D), dtype=np.float32)
    for core in range(NCORE):
        b, t0 = core // 4, (core % 4) * TPC
        out[b, t0:t0 + TPC, :] = e_results[core]["x2_o"]
    return out


# version 04 (final)
# speedup vs baseline: 1.5876x; 1.0011x over previous
import numpy as np
import ml_dtypes
from contextlib import ExitStack
import concourse.bass as bass
import concourse.mybir as mybir
from concourse.bass_utils import run_bass_kernel_spmd

F32 = mybir.dt.float32
BF16 = mybir.dt.bfloat16
I8 = mybir.dt.int8
AF = mybir.ActivationFunctionType
OP = mybir.AluOpType
AX = mybir.AxisListType
NPBF16 = ml_dtypes.bfloat16

B, S, D, NH, HD = 2, 4096, 2048, 16, 128
E, F, INTER = 8, 2048, 8192
NCORE = 8
TPC = 1024
NT = TPC // 128
NDC = D // 128
SCALE = np.float32(HD ** (-0.5))
GC = np.float32(1.0 / np.e)
GL = np.float32(0.5 - np.log(4.0 / 3.0))
GU = np.float32(0.5)

ENGS = ("tensor", "vector", "scalar", "sync", "gpsimd")


class Ctx:
    def __init__(self, nc, stack):
        self.nc = nc
        self.scopes = [stack]
        self.stages = []
        self.cur = None
        self.semh = {}
        self.semv = {}
        self.stage()

    def stage(self):
        self.cur = {e: [] for e in ENGS}
        self.stages.append(self.cur)

    def push_scope(self, st):
        self.scopes.append(st)

    def pop_scope(self):
        self.scopes.pop()

    def finalize(self):
        for st in self.stages:
            with self.nc.Block() as blk:
                for eng in ENGS:
                    lst = st[eng]
                    if lst:
                        getattr(blk, eng)(lambda e, lst=lst: [f(e) for f in lst])

    def sbuf(self, name, shape, dtype):
        return self.scopes[-1].enter_context(self.nc.sbuf_tensor(name, shape, dtype))

    def psum(self, name, shape, dtype=F32):
        return self.scopes[-1].enter_context(self.nc.psum_tensor(name, shape, dtype))

    def sem(self, name):
        if name not in self.semh:
            self.semh[name] = self.scopes[0].enter_context(self.nc.semaphore(name))
            self.semv[name] = 0
        return self.semh[name]

    def bump(self, name, by):
        self.sem(name)
        self.semv[name] += by
        return self.semv[name]

    def v(self, name):
        return self.semv.get(name, 0)

    def emit(self, eng, fn):
        self.cur[eng].append(fn)

    def wait(self, eng, name, val):
        if val <= 0:
            return
        h = self.sem(name)
        self.emit(eng, lambda e, h=h, val=val: e.wait_ge(h, val))

    def op(self, eng, fn, inc=None):
        if inc is None:
            self.emit(eng, fn)
            return None
        name, by = inc
        h = self.sem(name)
        v = self.bump(name, by)
        self.emit(eng, lambda e, fn=fn, h=h, by=by: fn(e).then_inc(h, by))
        return v

    def nopinc(self, eng, name, by=1):
        h = self.sem(name)
        v = self.bump(name, by)
        if eng in ("vector",):
            self.emit(eng, lambda e, h=h, by=by: e.engine_nop().then_inc(h, by))
        else:
            self.emit(eng, lambda e, h=h, by=by: e.nop().then_inc(h, by))
        return v

    def dma(self, eng, out_ap, in_ap, sem):
        h = self.sem(sem)
        v = self.bump(sem, 16)
        self.emit(eng, lambda e, o=out_ap, i=in_ap, h=h: e.dma_start(out=o, in_=i).then_inc(h, 16))
        return v


def host_consts():
    c = {}
    c["ident"] = np.eye(128, dtype=NPBF16)
    R = np.zeros((128, 128), dtype=np.float32)
    for i in range(64):
        R[i, i + 64] = -1.0
        R[i + 64, i] = 1.0
    c["rotT"] = np.ascontiguousarray(R.T).astype(NPBF16)
    inv_freq = 1.0 / (10000.0 ** (np.arange(0, HD, 2, dtype=np.float32) / HD))
    t = np.arange(S, dtype=np.float32)
    freqs = np.einsum("j,t->jt", inv_freq, t)
    emb = np.concatenate([freqs, freqs], axis=0)
    c["cosF"] = np.cos(emb).astype(np.float32)
    c["sinF"] = np.sin(emb).astype(np.float32)
    m4 = np.zeros((4, 128, 512), dtype=NPBF16)
    p = np.arange(128)[:, None]
    f = np.arange(512)[None, :]
    for j in range(4):
        m4[j] = (p + 128 * j <= f).astype(NPBF16)
    c["mask4"] = m4
    return c


def prep_weights(inp):
    w = {}
    for nm in ("wq", "wk"):
        wT = np.ascontiguousarray(inp[f"L0_attn_{nm}"].T)
        w[f"l0_{nm}T_h"] = np.ascontiguousarray(wT.reshape(D, NH, HD).transpose(1, 0, 2))
    for nm in ("wv", "wg"):
        wT = np.ascontiguousarray(inp[f"L0_attn_{nm}"].T)
        w[f"l0_{nm}T_c"] = np.ascontiguousarray(wT.reshape(D, 4, 512).transpose(1, 0, 2))
    woT = np.ascontiguousarray(inp["L0_attn_wo"].T)
    w["l0_woT_c"] = np.ascontiguousarray(woT.reshape(D, 4, 512).transpose(1, 0, 2))
    w["l0_Wr_t"] = np.ascontiguousarray(
        inp["L0_ffn_Wr"].reshape(NDC, 128, E).transpose(1, 0, 2).reshape(128, NDC * E))
    w["l0_Wg_t"] = np.ascontiguousarray(inp["L0_ffn_Wg"].reshape(E, D, 4, 512).transpose(0, 2, 1, 3))
    w["l0_Wu_t"] = np.ascontiguousarray(inp["L0_ffn_Wu"].reshape(E, D, 4, 512).transpose(0, 2, 1, 3))
    w["l0_Wd_t"] = np.ascontiguousarray(inp["L0_ffn_Wd"].reshape(E, F, 4, 512).transpose(0, 2, 1, 3))
    for nm in ("wq", "wk"):
        wT = np.ascontiguousarray(inp[f"L1_attn_{nm}"].T)
        w[f"l1_{nm}T_h"] = np.ascontiguousarray(wT.reshape(D, NH, HD).transpose(1, 0, 2))
    wT = np.ascontiguousarray(inp["L1_attn_wv"].T)
    w["l1_wvT_c"] = np.ascontiguousarray(wT.reshape(D, 4, 512).transpose(1, 0, 2))
    woT1 = np.ascontiguousarray(inp["L1_attn_wo"].T)
    w["l1_woT_c"] = np.ascontiguousarray(woT1.reshape(D, 4, 512).transpose(1, 0, 2))
    w["l1_wg_t"] = np.ascontiguousarray(inp["L1_ffn_w_gate"].reshape(D, 16, 512).transpose(1, 0, 2))
    w["l1_wu_t"] = np.ascontiguousarray(inp["L1_ffn_w_up"].reshape(D, 16, 512).transpose(1, 0, 2))
    w["l1_wd_t"] = np.ascontiguousarray(inp["L1_ffn_w_down"].reshape(INTER, 4, 512).transpose(1, 0, 2))
    w["l1_sg_t"] = np.ascontiguousarray(inp["L1_ffn_s_gate"].reshape(64, 128).T)
    w["l1_su_t"] = np.ascontiguousarray(inp["L1_ffn_s_up"].reshape(64, 128).T)
    w["l1_sd_t"] = np.ascontiguousarray(np.tile(inp["L1_ffn_s_down"][None, :], (128, 1)))
    w["l0_normw_t"] = np.ascontiguousarray(np.tile(inp["L0_attn_norm_w"][None, :], (128, 1)))
    w["l0_temp_t"] = np.ascontiguousarray(
        np.tile(np.asarray(inp["L0_ffn_temperature"], dtype=np.float32).reshape(1, 1), (128, 1)))
    for nm in ("lambda_q1", "lambda_q2", "lambda_k1", "lambda_k2"):
        w[f"l1_{nm}"] = np.ascontiguousarray(inp[f"L1_attn_{nm}"].reshape(1, HD))
    return w


def emit_ln_xnT(ctx, x_dram, xnT, xn_out_dram=None, tag="ln"):
    xt = [ctx.sbuf(f"{tag}_xt{i}", [128, D], F32) for i in range(2)]
    xnb = [ctx.sbuf(f"{tag}_xnb{i}", [128, D], BF16) for i in range(2)]
    scr = ctx.sbuf(f"{tag}_scr", [128, D], F32)
    st = [ctx.sbuf(f"{tag}_st{i}", [128, 8], F32) for i in range(2)]
    tp = [ctx.psum(f"{tag}_tp{i}", [128, 128], BF16) for i in range(2)]
    ident = ctx.ident
    for i in range(NT):
        b = i % 2
        if i >= 2:
            ctx.wait("sync", f"{tag}_xn", i - 1)
            ctx.wait("sync", f"{tag}_sq", i - 1)
        ctx.dma("sync", xt[b][:, :], x_dram[i * 128:(i + 1) * 128, :], f"{tag}_xld")
        ctx.wait("scalar", f"{tag}_xld", 16 * (i + 1))
        sti = st[b]
        if i >= 2:
            ctx.wait("scalar", f"{tag}_rc", i - 1)
            ctx.wait("scalar", f"{tag}_xn", i - 1)
        ctx.wait("vector", f"{tag}_xld", 16 * (i + 1))
        ctx.op("vector", lambda e, b=b, sti=sti: e.reduce_sum(sti[:, 0:1], xt[b][:, :], AX.X),
               inc=(f"{tag}_rs", 1))
        ctx.op("scalar", lambda e, b=b, sti=sti: e.activation(scr[:, :], xt[b][:, :], AF.Square,
                                                              accum_out=sti[:, 1:2]),
               inc=(f"{tag}_sq", 1))
        ctx.wait("scalar", f"{tag}_sq", ctx.v(f"{tag}_sq"))
        ctx.wait("scalar", f"{tag}_rs", ctx.v(f"{tag}_rs"))
        ctx.op("scalar", lambda e, sti=sti: e.mul(sti[:, 2:3], sti[:, 0:1], 1.0 / D),
               inc=(f"{tag}_sch", 1))
        ctx.wait("scalar", f"{tag}_sch", ctx.v(f"{tag}_sch"))
        ctx.op("scalar", lambda e, sti=sti: e.mul(sti[:, 3:4], sti[:, 2:3], sti[:, 2:3]),
               inc=(f"{tag}_sch", 1))
        ctx.wait("scalar", f"{tag}_sch", ctx.v(f"{tag}_sch"))
        ctx.op("scalar", lambda e, sti=sti: e.activation(sti[:, 3:4], sti[:, 3:4], AF.Identity,
                                                         scale=np.float32(-1.0),
                                                         bias=np.float32(1e-5)),
               inc=(f"{tag}_sch", 1))
        ctx.wait("scalar", f"{tag}_sch", ctx.v(f"{tag}_sch"))
        ctx.op("scalar", lambda e, sti=sti: e.activation(sti[:, 4:5], sti[:, 1:2], AF.Sqrt,
                                                         scale=np.float32(1.0 / D),
                                                         bias=sti[:, 3:4]),
               inc=(f"{tag}_sqa", 1))
        ctx.wait("vector", f"{tag}_sqa", ctx.v(f"{tag}_sqa"))
        ctx.op("vector", lambda e, sti=sti: e.reciprocal(sti[:, 5:6], sti[:, 4:5]),
               inc=(f"{tag}_rc", 1))
        rcv = ctx.v(f"{tag}_rc")
        ctx.wait("scalar", f"{tag}_rc", rcv)
        ctx.op("scalar", lambda e, sti=sti: e.mul(sti[:, 6:7], sti[:, 2:3], sti[:, 5:6]),
               inc=(f"{tag}_mrs", 1))
        if i >= 2:
            ctx.wait("vector", f"{tag}_tp", (i - 1) * NDC)
            if xn_out_dram is not None:
                ctx.wait("vector", f"{tag}_xnst", 16 * (i - 1))
        ctx.wait("vector", f"{tag}_mrs", ctx.v(f"{tag}_mrs"))
        ctx.wait("vector", f"{tag}_rc", rcv)
        ctx.wait("vector", f"{tag}_xld", 16 * (i + 1))
        ctx.op("vector", lambda e, b=b, sti=sti: e.tensor_scalar(xnb[b][:, :], xt[b][:, :],
                                                                 sti[:, 5:6], sti[:, 6:7],
                                                                 OP.mult, OP.subtract),
               inc=(f"{tag}_xn", 1))
        xnv = ctx.v(f"{tag}_xn")
        if xn_out_dram is not None:
            ctx.wait("sync", f"{tag}_xn", xnv)
            ctx.dma("sync", xn_out_dram[i * 128:(i + 1) * 128, :], xnb[b][:, :], f"{tag}_xnst")
        ctx.wait("tensor", f"{tag}_xn", xnv)
        for dc in range(NDC):
            g = i * NDC + dc
            pb = g % 2
            if g >= 2:
                ctx.wait("tensor", f"{tag}_tev", g - 1)
            ctx.op("tensor", lambda e, b=b, dc=dc, pb=pb: e.transpose(
                tp[pb][:, :], xnb[b][:, dc * 128:(dc + 1) * 128], ident[:, :]),
                inc=(f"{tag}_tp", 1))
            ctx.wait("vector", f"{tag}_tp", ctx.v(f"{tag}_tp"))
            ctx.op("vector", lambda e, dc=dc, i=i, pb=pb: e.tensor_copy(
                xnT[:, dc * TPC + i * 128: dc * TPC + (i + 1) * 128], tp[pb][:, :]),
                inc=(f"{tag}_tev", 1))


def emit_qk_proj(ctx, xnT, wT_h_dram, out_dram, cos_sb, sin_sb, tag):
    wsb = [ctx.sbuf(f"{tag}_w{i}", [128, NDC * 128], BF16) for i in range(2)]
    qraw = ctx.sbuf(f"{tag}_qraw", [128, TPC], BF16)
    t1 = ctx.sbuf(f"{tag}_t1", [128, 512], F32)
    t2 = ctx.sbuf(f"{tag}_t2", [128, 512], F32)
    rop = [ctx.sbuf(f"{tag}_rop{i}", [128, TPC], BF16) for i in range(2)]
    ps = [ctx.psum(f"{tag}_ps{i}", [128, 512]) for i in range(2)]
    psr = ctx.psum(f"{tag}_psr", [128, 512])
    rotT = ctx.rotT
    NSQB = TPC // 512
    for h in range(NH):
        wb = h % 2
        if h >= 2:
            ctx.wait("gpsimd", f"{tag}_mm", (h - 1) * NSQB)
        src = wT_h_dram[h].rearrange("(dc p) c -> p dc c", p=128)
        ctx.dma("gpsimd", wsb[wb][:, :].rearrange("p (dc c) -> p dc c", dc=NDC), src, f"{tag}_wld")
        ctx.wait("tensor", f"{tag}_wld", 16 * (h + 1))
        for sqb in range(NSQB):
            g = h * NSQB + sqb
            pb = g % 2
            if g >= 2:
                ctx.wait("tensor", f"{tag}_evac", g - 1)
            for dc in range(NDC):
                ctx.op("tensor", lambda e, dc=dc, sqb=sqb, pb=pb, wb=wb: e.matmul(
                    ps[pb][:, :], wsb[wb][:, dc * 128:(dc + 1) * 128],
                    xnT[:, dc * TPC + sqb * 512: dc * TPC + sqb * 512 + 512],
                    start=(dc == 0), stop=(dc == NDC - 1)),
                    inc=(f"{tag}_mm", 1) if dc == NDC - 1 else None)
            mmv = ctx.v(f"{tag}_mm")
            ctx.wait("scalar", f"{tag}_mm", mmv)
            if h > 0:
                ctx.wait("scalar", f"{tag}_rope", (h - 1) * NSQB + sqb + 1)
            ctx.op("scalar", lambda e, sqb=sqb, pb=pb: e.copy(qraw[:, sqb * 512:sqb * 512 + 512],
                                                              ps[pb][:, :]),
                   inc=(f"{tag}_evac", 1))
            evv = ctx.v(f"{tag}_evac")
            ctx.wait("tensor", f"{tag}_evac", evv)
            if g >= 1:
                ctx.wait("tensor", f"{tag}_t2", g)
            ctx.op("tensor", lambda e, sqb=sqb: e.matmul(
                psr[:, :], rotT[:, :], qraw[:, sqb * 512:sqb * 512 + 512], start=True, stop=True),
                inc=(f"{tag}_rot", 1))
            rotv = ctx.v(f"{tag}_rot")
            ctx.wait("vector", f"{tag}_evac", evv)
            rb = h % 2
            if h >= 2 and sqb == 0:
                ctx.wait("vector", f"{tag}_qst", 16 * (h - 1))
            ctx.op("vector", lambda e, sqb=sqb: e.tensor_mul(
                t1[:, :], qraw[:, sqb * 512:sqb * 512 + 512], cos_sb[:, sqb * 512:sqb * 512 + 512]))
            ctx.wait("vector", f"{tag}_rot", rotv)
            ctx.op("vector", lambda e, sqb=sqb: e.tensor_mul(t2[:, :], psr[:, :],
                                                             sin_sb[:, sqb * 512:sqb * 512 + 512]),
                   inc=(f"{tag}_t2", 1))
            ctx.wait("vector", f"{tag}_t2", ctx.v(f"{tag}_t2"))
            ctx.op("vector", lambda e, rb=rb, sqb=sqb: e.tensor_add(
                rop[rb][:, sqb * 512:sqb * 512 + 512], t1[:, :], t2[:, :]),
                inc=(f"{tag}_rope", 1))
        ctx.wait("sync", f"{tag}_rope", ctx.v(f"{tag}_rope"))
        ctx.dma("sync", out_dram[h * 128:(h + 1) * 128, :], rop[h % 2][:, :], f"{tag}_qst")


def emit_vg_proj(ctx, xnT, wT_c_dram, out_dram, tag, silu=False):
    wsb = [ctx.sbuf(f"{tag}_w{i}", [128, NDC * 512], BF16) for i in range(2)]
    stg = [ctx.sbuf(f"{tag}_stg{i}", [128, 512], BF16) for i in range(2)]
    ps = [ctx.psum(f"{tag}_ps{i}", [128, 512]) for i in range(2)]
    for cb in range(4):
        wb = cb % 2
        if cb >= 2:
            ctx.wait("gpsimd", f"{tag}_mm", (cb - 1) * NT)
        src = wT_c_dram[cb].rearrange("(dc p) c -> p dc c", p=128)
        ctx.dma("gpsimd", wsb[wb][:, :].rearrange("p (dc c) -> p dc c", dc=NDC), src, f"{tag}_wld")
        ctx.wait("tensor", f"{tag}_wld", 16 * (cb + 1))
        for i in range(NT):
            g = cb * NT + i
            pb = g % 2
            if g >= 2:
                ctx.wait("tensor", f"{tag}_evac", g - 1)
            for dc in range(NDC):
                ctx.op("tensor", lambda e, dc=dc, i=i, pb=pb, wb=wb: e.matmul(
                    ps[pb][:, :], xnT[:, dc * TPC + i * 128: dc * TPC + (i + 1) * 128],
                    wsb[wb][:, dc * 512:(dc + 1) * 512],
                    start=(dc == 0), stop=(dc == NDC - 1)),
                    inc=(f"{tag}_mm", 1) if dc == NDC - 1 else None)
            ctx.wait("scalar", f"{tag}_mm", ctx.v(f"{tag}_mm"))
            if g >= 2:
                ctx.wait("scalar", f"{tag}_st", 16 * (g - 1))
            sb = g % 2
            if silu:
                ctx.op("scalar", lambda e, sb=sb, pb=pb: e.activation(stg[sb][:, :], ps[pb][:, :], AF.Silu),
                       inc=(f"{tag}_evac", 1))
            else:
                ctx.op("scalar", lambda e, sb=sb, pb=pb: e.copy(stg[sb][:, :], ps[pb][:, :]),
                       inc=(f"{tag}_evac", 1))
            ctx.wait("sync", f"{tag}_evac", ctx.v(f"{tag}_evac"))
            ctx.dma("sync", out_dram[i * 128:(i + 1) * 128, cb * 512:(cb + 1) * 512], stg[sb][:, :],
                    f"{tag}_st")


def emit_router(ctx, xnT, wr_sb, rtemp, wfinal, tag="rt", dbg=None):
    ps = ctx.psum(f"{tag}_ps", [128, 16])
    inh = ctx.sbuf(f"{tag}_inh", [128, 8], F32)
    dist = ctx.sbuf(f"{tag}_dist", [128, 8], F32)
    zone = ctx.sbuf(f"{tag}_zone", [128, 8], F32)
    z2 = ctx.sbuf(f"{tag}_z2", [128, 8], F32)
    wz = ctx.sbuf(f"{tag}_wz", [128, 8], F32)
    fb = ctx.sbuf(f"{tag}_fb", [128, 8], F32)
    fbx = ctx.sbuf(f"{tag}_fbx", [128, 8], F32)
    m = ctx.sbuf(f"{tag}_m", [128, 8], F32)
    mask1 = ctx.sbuf(f"{tag}_mask1", [128, 8], F32)
    mask2 = ctx.sbuf(f"{tag}_mask2", [128, 8], F32)
    ctx.wait("tensor", "cwld", 16)

    def vch(fn):
        v = ctx.op("vector", fn, inc=(f"{tag}_vch", 1))
        ctx.wait("vector", f"{tag}_vch", v)

    for i in range(NT):
        if i > 0:
            ctx.wait("tensor", f"{tag}_sg", i)
        for dc in range(NDC):
            ctx.op("tensor", lambda e, dc=dc, i=i: e.matmul(
                ps[:, 0:8], xnT[:, dc * TPC + i * 128: dc * TPC + (i + 1) * 128],
                wr_sb[:, dc * 8:(dc + 1) * 8],
                start=(dc == 0), stop=(dc == NDC - 1)),
                inc=(f"{tag}_mm", 1) if dc == NDC - 1 else None)
        ctx.wait("scalar", f"{tag}_mm", ctx.v(f"{tag}_mm"))
        if i > 0:
            ctx.wait("scalar", f"{tag}_done", i)
        ctx.op("scalar", lambda e: e.activation(inh[:, :], ps[:, 0:8], AF.Sigmoid, scale=rtemp[:, 0:1]),
               inc=(f"{tag}_sg", 1))
        ctx.wait("scalar", f"{tag}_sg", ctx.v(f"{tag}_sg"))
        if dbg is not None:
            ctx.op("scalar", lambda e, i=i: e.copy(dbg[:, i * 8:(i + 1) * 8], inh[:, :]))
        ctx.op("scalar", lambda e: e.activation(dist[:, :], inh[:, :], AF.Abs, bias=np.float32(-GC)))
        ctx.nopinc("scalar", f"{tag}_sch", 1)
        ctx.wait("scalar", f"{tag}_sch", ctx.v(f"{tag}_sch"))
        ctx.nopinc("scalar", f"{tag}_sch", 1)
        ctx.wait("scalar", f"{tag}_sch", ctx.v(f"{tag}_sch"))
        ctx.op("scalar", lambda e: e.activation(wz[:, :], dist[:, :], AF.Exp, scale=np.float32(-10.0)))
        ctx.op("scalar", lambda e: e.activation(fb[:, :], dist[:, :], AF.Exp,
                                                scale=np.float32(-1.0 / 0.3)),
               inc=(f"{tag}_sc", 1))
        ctx.wait("vector", f"{tag}_sc", ctx.v(f"{tag}_sc"))
        vch(lambda e: e.tensor_single_scalar(zone[:, :], inh[:, :], float(GL), OP.is_ge))
        vch(lambda e: e.tensor_single_scalar(z2[:, :], inh[:, :], float(GU), OP.is_le))
        vch(lambda e: e.tensor_mul(zone[:, :], zone[:, :], z2[:, :]))
        vch(lambda e: e.tensor_mul(wz[:, :], wz[:, :], zone[:, :]))
        vch(lambda e: e.reduce_sum(m[:, 0:1], wz[:, :], AX.X))
        vch(lambda e: e.tensor_single_scalar(m[:, 1:2], m[:, 0:1], 1e-8, OP.is_lt))
        vch(lambda e: e.reduce_max(m[:, 2:3], fb[:, :], AX.X))
        vch(lambda e: e.tensor_scalar(mask1[:, :], fb[:, :], m[:, 2:3], None, OP.is_equal))
        vch(lambda e: e.scalar_tensor_tensor(fbx[:, :], mask1[:, :], -1e30, fb[:, :],
                                             OP.mult, OP.add))
        vch(lambda e: e.reduce_max(m[:, 3:4], fbx[:, :], AX.X))
        vch(lambda e: e.tensor_scalar(mask2[:, :], fbx[:, :], m[:, 3:4], None, OP.is_equal))
        vch(lambda e: e.tensor_add(mask1[:, :], mask1[:, :], mask2[:, :]))
        vch(lambda e: e.tensor_mul(mask1[:, :], fb[:, :], mask1[:, :]))
        vch(lambda e: e.reduce_sum(m[:, 4:5], mask1[:, :], AX.X))
        vch(lambda e: e.tensor_single_scalar(m[:, 4:5], m[:, 4:5], 1e-8, OP.max))
        vch(lambda e: e.reciprocal(m[:, 5:6], m[:, 4:5]))
        vch(lambda e: e.tensor_scalar(mask1[:, :], mask1[:, :], m[:, 5:6], None, OP.mult))
        vch(lambda e: e.scalar_tensor_tensor(wz[:, :], mask1[:, :], m[:, 1:2], wz[:, :],
                                             OP.mult, OP.add))
        vch(lambda e: e.reduce_sum(m[:, 6:7], wz[:, :], AX.X))
        vch(lambda e: e.tensor_single_scalar(m[:, 6:7], m[:, 6:7], 1e-8, OP.max))
        vch(lambda e: e.reciprocal(m[:, 7:8], m[:, 6:7]))
        ctx.op("vector", lambda e, i=i: e.tensor_scalar(wfinal[:, i * 8:(i + 1) * 8], wz[:, :],
                                                        m[:, 7:8], None, OP.mult),
               inc=(f"{tag}_done", 1))


def emit_moe(ctx, xnT, wg_dram, wu_dram, wd_dram, wfinal, fg_out_dram, tag="moe"):
    wgp = [ctx.sbuf(f"{tag}_wg{i}", [128, NDC * 128], BF16) for i in range(2)]
    wup = [ctx.sbuf(f"{tag}_wu{i}", [128, NDC * 128], BF16) for i in range(2)]
    wd_sb = [ctx.sbuf(f"{tag}_wd{i}", [128, NDC * 512], BF16) for i in range(2)]
    fuseT = ctx.sbuf(f"{tag}_fuseT", [128, NDC * TPC], BF16)
    sg_scr = [ctx.sbuf(f"{tag}_sg{i}", [128, 512], BF16) for i in range(2)]
    acc = ctx.sbuf(f"{tag}_acc", [128, NT * D], F32)
    fg_stg = [ctx.sbuf(f"{tag}_fg{i}", [128, 512], F32) for i in range(2)]
    psg = [ctx.psum(f"{tag}_psg{i}", [128, 512]) for i in range(2)]
    psu = [ctx.psum(f"{tag}_psu{i}", [128, 512]) for i in range(2)]
    psd = [ctx.psum(f"{tag}_psd{i}", [128, 512]) for i in range(2)]
    NSQB = TPC // 512

    for e_i in range(E):
        for fs in range(4):
            for fc in range(4):
                fcg = fs * 4 + fc
                pidx = (e_i * 4 + fs) * 4 + fc
                pb = pidx % 2
                if pidx >= 2:
                    ctx.wait("gpsimd", f"{tag}_gmm", 2 * (pidx - 1))
                    ctx.wait("gpsimd", f"{tag}_umm", 2 * (pidx - 1))
                srcg = wg_dram[e_i, fs][:, fc * 128:(fc + 1) * 128].rearrange(
                    "(c p) f -> p c f", p=128)
                ctx.dma("gpsimd", wgp[pb][:, :].rearrange("p (c f) -> p c f", c=NDC),
                        srcg, f"{tag}_wgld")
                srcu = wu_dram[e_i, fs][:, fc * 128:(fc + 1) * 128].rearrange(
                    "(c p) f -> p c f", p=128)
                ctx.dma("gpsimd", wup[pb][:, :].rearrange("p (c f) -> p c f", c=NDC),
                        srcu, f"{tag}_wuld")
                ctx.wait("tensor", f"{tag}_wgld", 16 * (pidx + 1))
                ctx.wait("tensor", f"{tag}_wuld", 16 * (pidx + 1))
                for sqb in range(NSQB):
                    git = e_i * 16 * NSQB + fcg * NSQB + sqb
                    b = git % 2
                    if git >= 2:
                        ctx.wait("tensor", f"{tag}_silu", git - 1)
                        ctx.wait("tensor", f"{tag}_fuse", git - 1)
                    for dc in range(NDC):
                        ctx.op("tensor", lambda e, dc=dc, sqb=sqb, pb=pb, b=b: e.matmul(
                            psg[b][:, :], wgp[pb][:, dc * 128:(dc + 1) * 128],
                            xnT[:, dc * TPC + sqb * 512: dc * TPC + sqb * 512 + 512],
                            start=(dc == 0), stop=(dc == NDC - 1)),
                            inc=(f"{tag}_gmm", 1) if dc == NDC - 1 else None)
                    for dc in range(NDC):
                        ctx.op("tensor", lambda e, dc=dc, sqb=sqb, pb=pb, b=b: e.matmul(
                            psu[b][:, :], wup[pb][:, dc * 128:(dc + 1) * 128],
                            xnT[:, dc * TPC + sqb * 512: dc * TPC + sqb * 512 + 512],
                            start=(dc == 0), stop=(dc == NDC - 1)),
                            inc=(f"{tag}_umm", 1) if dc == NDC - 1 else None)
                    gv = ctx.v(f"{tag}_gmm")
                    uv = ctx.v(f"{tag}_umm")
                    ctx.wait("scalar", f"{tag}_gmm", gv)
                    if git >= 2:
                        ctx.wait("scalar", f"{tag}_fuse", git - 1)
                    ctx.op("scalar", lambda e, b=b: e.activation(sg_scr[b][:, :], psg[b][:, :],
                                                                 AF.Silu),
                           inc=(f"{tag}_silu", 1))
                    ctx.wait("vector", f"{tag}_silu", ctx.v(f"{tag}_silu"))
                    ctx.wait("vector", f"{tag}_umm", uv)
                    ctx.op("vector", lambda e, fcg=fcg, sqb=sqb, b=b: e.tensor_tensor(
                        fuseT[:, fcg * TPC + sqb * 512: fcg * TPC + sqb * 512 + 512],
                        psu[b][:, :], sg_scr[b][:, :], OP.mult),
                        inc=(f"{tag}_fuse", 1))
        fusev = ctx.v(f"{tag}_fuse")
        for ds in range(4):
            sidx = e_i * 4 + ds
            wb = sidx % 2
            if sidx >= 2:
                ctx.wait("gpsimd", f"{tag}_dmm", (sidx - 1) * NT)
            srcd = wd_dram[e_i, ds].rearrange("(c p) f -> p c f", p=128)
            ctx.dma("gpsimd", wd_sb[wb][:, :].rearrange("p (c f) -> p c f", c=NDC),
                    srcd, f"{tag}_wdld")
            ctx.wait("tensor", f"{tag}_wdld", 16 * (sidx + 1))
            if ds == 0:
                ctx.wait("tensor", f"{tag}_fuse", fusev)
            for i in range(NT):
                git = sidx * NT + i
                pb = git % 2
                if git >= 2:
                    ctx.wait("tensor", f"{tag}_dps", git - 1)
                for fcg in range(NDC):
                    ctx.op("tensor", lambda e, fcg=fcg, i=i, pb=pb, wb=wb: e.matmul(
                        psd[pb][:, :], fuseT[:, fcg * TPC + i * 128: fcg * TPC + (i + 1) * 128],
                        wd_sb[wb][:, fcg * 512:(fcg + 1) * 512],
                        start=(fcg == 0), stop=(fcg == NDC - 1)),
                        inc=(f"{tag}_dmm", 1) if fcg == NDC - 1 else None)
                ctx.wait("vector", f"{tag}_dmm", ctx.v(f"{tag}_dmm"))
                if e_i == 0:
                    ctx.op("vector", lambda e, i=i, ds=ds, pb=pb, e_i=e_i: e.tensor_scalar(
                        acc[:, i * D + ds * 512: i * D + (ds + 1) * 512], psd[pb][:, :],
                        wfinal[:, i * 8 + e_i: i * 8 + e_i + 1], None, OP.mult),
                        inc=(f"{tag}_dps", 1))
                else:
                    ctx.op("vector", lambda e, i=i, ds=ds, pb=pb, e_i=e_i: e.scalar_tensor_tensor(
                        acc[:, i * D + ds * 512: i * D + (ds + 1) * 512], psd[pb][:, :],
                        wfinal[:, i * 8 + e_i: i * 8 + e_i + 1],
                        acc[:, i * D + ds * 512: i * D + (ds + 1) * 512],
                        OP.mult, OP.add),
                        inc=(f"{tag}_dps", 1))
    ctx.wait("scalar", f"{tag}_dps", ctx.v(f"{tag}_dps"))
    for i in range(NT):
        for dsb in range(4):
            g = i * 4 + dsb
            b = g % 2
            if g >= 2:
                ctx.wait("scalar", f"{tag}_fgst", 16 * (g - 1))
            ctx.op("scalar", lambda e, i=i, dsb=dsb, b=b: e.activation(
                fg_stg[b][:, :], acc[:, i * D + dsb * 512: i * D + (dsb + 1) * 512], AF.Silu),
                inc=(f"{tag}_fgev", 1))
            ctx.wait("sync", f"{tag}_fgev", ctx.v(f"{tag}_fgev"))
            ctx.dma("sync", fg_out_dram[i * 128:(i + 1) * 128, dsb * 512:(dsb + 1) * 512],
                    fg_stg[b][:, :], f"{tag}_fgst")


def build_launch_A():
    nc = bass.Bass("TRN2", target_bir_lowering=False)
    x_tok = nc.dram_tensor("x_tok", [TPC, D], F32, kind="ExternalInput").ap()
    wq_h = nc.dram_tensor("wq_h", [NH, D, HD], F32, kind="ExternalInput").ap()
    wk_h = nc.dram_tensor("wk_h", [NH, D, HD], F32, kind="ExternalInput").ap()
    wv_c = nc.dram_tensor("wv_c", [4, D, 512], F32, kind="ExternalInput").ap()
    wg_c = nc.dram_tensor("wg_c", [4, D, 512], F32, kind="ExternalInput").ap()
    wr_d = nc.dram_tensor("wr_d", [128, NDC * E], F32, kind="ExternalInput").ap()
    temp_d = nc.dram_tensor("temp_d", [128, 1], F32, kind="ExternalInput").ap()
    mwg = nc.dram_tensor("mwg", [E, 4, D, 512], F32, kind="ExternalInput").ap()
    mwu = nc.dram_tensor("mwu", [E, 4, D, 512], F32, kind="ExternalInput").ap()
    mwd = nc.dram_tensor("mwd", [E, 4, F, 512], F32, kind="ExternalInput").ap()
    identd = nc.dram_tensor("identd", [128, 128], BF16, kind="ExternalInput").ap()
    rotTd = nc.dram_tensor("rotTd", [128, 128], BF16, kind="ExternalInput").ap()
    cosd = nc.dram_tensor("cosd", [128, TPC], F32, kind="ExternalInput").ap()
    sind = nc.dram_tensor("sind", [128, TPC], F32, kind="ExternalInput").ap()

    qt_o = nc.dram_tensor("qt_o", [D, TPC], BF16, kind="ExternalOutput").ap()
    kt_o = nc.dram_tensor("kt_o", [D, TPC], BF16, kind="ExternalOutput").ap()
    v_o = nc.dram_tensor("v_o", [TPC, D], BF16, kind="ExternalOutput").ap()
    sg_o = nc.dram_tensor("sg_o", [TPC, D], BF16, kind="ExternalOutput").ap()
    fg_o = nc.dram_tensor("fg_o", [TPC, D], F32, kind="ExternalOutput").ap()
    xn_o = nc.dram_tensor("xn_o", [TPC, D], BF16, kind="ExternalOutput").ap()
    wr_o = nc.dram_tensor("wr_o", [128, NT * E], F32, kind="ExternalOutput").ap()
    inh_o = nc.dram_tensor("inh_o", [128, 2 * NT * E], F32, kind="ExternalOutput").ap()

    with ExitStack() as stack:
        ctx = Ctx(nc, stack)
        xnT = ctx.sbuf("xnT", [128, NDC * TPC], BF16)
        ctx.ident = ctx.sbuf("identsb", [128, 128], BF16)
        ctx.rotT = ctx.sbuf("rotTsb", [128, 128], BF16)
        cosQ = ctx.sbuf("cosQ", [128, TPC], F32)
        sinQ = ctx.sbuf("sinQ", [128, TPC], F32)
        cosK = ctx.sbuf("cosK", [128, TPC], F32)
        sinK = ctx.sbuf("sinK", [128, TPC], F32)
        wr_sb = ctx.sbuf("wr_sb", [128, NDC * E], BF16)
        rtemp = ctx.sbuf("rtemp", [128, 1], F32)
        tempsb = ctx.sbuf("tempsb", [128, 1], F32)
        wfinal = ctx.sbuf("wfinal", [128, NT * E], F32)

        ctx.dma("sync", ctx.ident[:, :], identd, "cld")
        ctx.dma("sync", ctx.rotT[:, :], rotTd, "cld")
        ctx.dma("sync", cosK[:, :], cosd, "cld")
        ctx.dma("sync", sinK[:, :], sind, "cld")
        ctx.dma("sync", tempsb[:, :], temp_d, "cld")
        ctx.dma("gpsimd", wr_sb[:, :], wr_d, "cwld")
        ctx.wait("vector", "cld", ctx.v("cld"))
        ctx.op("vector", lambda e: e.tensor_single_scalar(cosQ[:, :], cosK[:, :], float(SCALE), OP.mult))
        ctx.op("vector", lambda e: e.tensor_single_scalar(sinQ[:, :], sinK[:, :], float(SCALE), OP.mult))
        ctx.op("vector", lambda e: e.reciprocal(rtemp[:, :], tempsb[:, :]))

        def phase(*emitfns):
            ctx.stage()
            with ExitStack() as ph:
                ctx.push_scope(ph)
                for fn in emitfns:
                    fn()
                ctx.pop_scope()

        inhdbg = ctx.sbuf("inhdbg", [128, 2 * NT * E], F32)
        phase(lambda: emit_ln_xnT(ctx, x_tok, xnT, xn_out_dram=xn_o, tag="ln0"))
        phase(lambda: emit_qk_proj(ctx, xnT, wq_h, qt_o, cosQ, sinQ, tag="q0"),
              lambda: emit_qk_proj(ctx, xnT, wk_h, kt_o, cosK, sinK, tag="k0"),
              lambda: emit_router(ctx, xnT, wr_sb, rtemp, wfinal, tag="rt", dbg=inhdbg))
        phase(lambda: emit_vg_proj(ctx, xnT, wv_c, v_o, tag="v0", silu=False),
              lambda: emit_vg_proj(ctx, xnT, wg_c, sg_o, tag="g0", silu=True))
        ctx.wait("sync", "rt_done", ctx.v("rt_done"))
        ctx.dma("sync", inh_o, inhdbg[:, :], "wr_st")
        ctx.wait("sync", "rt_done", ctx.v("rt_done"))
        ctx.dma("sync", wr_o, wfinal[:, :], "wr_st")
        phase(lambda: emit_moe(ctx, xnT, mwg, mwu, mwd, wfinal, fg_o, tag="moe"))

        ctx.stage()
        for s in ("ln0_xnst", "q0_qst", "k0_qst", "v0_st", "g0_st", "wr_st", "moe_fgst"):
            ctx.wait("sync", s, ctx.v(s))
        ctx.finalize()
    return nc


def inmaps_A(inputs, w, c):
    x = inputs["L0_attn_hidden_states"]
    maps = []
    for core in range(NCORE):
        b, t0 = core // 4, (core % 4) * TPC
        maps.append({
            "x_tok": np.ascontiguousarray(x[b, t0:t0 + TPC, :]),
            "wq_h": w["l0_wqT_h"], "wk_h": w["l0_wkT_h"],
            "wv_c": w["l0_wvT_c"], "wg_c": w["l0_wgT_c"],
            "wr_d": w["l0_Wr_t"], "temp_d": w["l0_temp_t"],
            "mwg": w["l0_Wg_t"], "mwu": w["l0_Wu_t"], "mwd": w["l0_Wd_t"],
            "identd": c["ident"], "rotTd": c["rotT"],
            "cosd": np.ascontiguousarray(c["cosF"][:, t0:t0 + TPC]),
            "sind": np.ascontiguousarray(c["sinF"][:, t0:t0 + TPC]),
        })
    return maps




def emit_attn_l0(ctx, qt_d, kt_d, v_d, mask_sb, o_d, tag="at0"):
    NKC = S // 128
    NSQB = S // 512
    GPH = sum(sqb + 1 for sqb in range(NSQB))
    qt = [ctx.sbuf(f"{tag}_qt{i}", [128, S], BF16) for i in range(2)]
    kt = [ctx.sbuf(f"{tag}_kt{i}", [128, S], BF16) for i in range(2)]
    ve = [ctx.sbuf(f"{tag}_ve{i}", [128, NKC * 129], BF16) for i in range(2)]
    pexp = [ctx.sbuf(f"{tag}_px{i}", [128, 2048], BF16) for i in range(2)]
    onrm = [ctx.sbuf(f"{tag}_on{i}", [128, 128], BF16) for i in range(2)]
    rden = ctx.sbuf(f"{tag}_rd", [128, 4], F32)
    ps_s = ctx.psum(f"{tag}_pss", [128, 2048])
    ps_o = [ctx.psum(f"{tag}_pso{i}", [128, 132]) for i in range(4)]

    def po_ap(par, st, a, b):
        return ps_o[st][:, a:b]
    for i in range(2):
        ctx.op("vector", lambda e, i=i: e.memset(
            ve[i][:, :].rearrange("p (c x) -> p c x", x=129)[:, :, 128:129], 1.0))
    ctx.nopinc("vector", f"{tag}_ones", 1)

    gtot = 0
    for j in range(4):
        hb = j % 2
        if j >= 2:
            ctx.wait("sync", f"{tag}_smm", (j - 1) * GPH)
            ctx.wait("sync", f"{tag}_av", (j - 1) * GPH)
        ctx.dma("sync", qt[hb][:, :], qt_d[j], f"{tag}_qld")
        ctx.dma("sync", kt[hb][:, :], kt_d[j], f"{tag}_kld")
        vsrc = v_d[j].rearrange("(c p) d -> p c d", p=128)
        vdst = ve[hb][:, :].rearrange("p (c x) -> p c x", x=129)[:, :, 0:128]
        ctx.dma("sync", vdst, vsrc, f"{tag}_vld")
        ldv = (ctx.v(f"{tag}_qld"), ctx.v(f"{tag}_kld"), ctx.v(f"{tag}_vld"))
        ctx.wait("tensor", f"{tag}_qld", ldv[0])
        ctx.wait("tensor", f"{tag}_kld", ldv[1])
        ctx.wait("tensor", f"{tag}_vld", ldv[2])
        if j == 0:
            ctx.wait("tensor", f"{tag}_ones", 1)
        for sqb in range(NSQB):
            ng = sqb + 1
            par = sqb % 2
            nrm_prev = ctx.v(f"{tag}_nrm")
            if nrm_prev > 0:
                ctx.wait("tensor", f"{tag}_nrm", nrm_prev)
            def emit_av(g, b):
                for kcl in range(4):
                    kc = 4 * g + kcl
                    for st in range(4):
                        ctx.op("tensor", lambda e, st=st, kcl=kcl, kc=kc, b=b,
                               hb=hb, g=g, ng=ng, par=par: e.matmul(
                            po_ap(par, st, 0, 129),
                            pexp[b][:, kcl * 512 + st * 128: kcl * 512 + (st + 1) * 128],
                            ve[hb][:, kc * 129:(kc + 1) * 129],
                            start=(g == 0 and kcl == 0), stop=(g == ng - 1 and kcl == 3)),
                            inc=(f"{tag}_av", 1) if (kcl == 3 and st == 3) else None)

            for g in range(ng):
                b = gtot % 2
                ctx.wait("tensor", f"{tag}_exp", ctx.v(f"{tag}_exp"))
                for kcl in range(4):
                    kc = 4 * g + kcl
                    ctx.op("tensor", lambda e, kcl=kcl, kc=kc, sqb=sqb, hb=hb: e.matmul(
                        ps_s[:, kcl * 512:(kcl + 1) * 512],
                        kt[hb][:, kc * 128:(kc + 1) * 128],
                        qt[hb][:, sqb * 512:sqb * 512 + 512], start=True, stop=True),
                        inc=(f"{tag}_smm", 1) if kcl == 3 else None)
                ctx.wait("scalar", f"{tag}_smm", ctx.v(f"{tag}_smm"))
                ctx.wait("scalar", f"{tag}_av", max(0, gtot - 1))
                ctx.op("scalar", lambda e, b=b: e.activation(
                    pexp[b][:, :], ps_s[:, :], AF.Exp),
                    inc=(f"{tag}_exp", 1))
                if g == sqb:
                    ctx.wait("vector", f"{tag}_exp", ctx.v(f"{tag}_exp"))
                    for kcl in range(4):
                        ctx.op("vector", lambda e, kcl=kcl, b=b: e.tensor_mul(
                            pexp[b][:, kcl * 512:(kcl + 1) * 512],
                            pexp[b][:, kcl * 512:(kcl + 1) * 512],
                            mask_sb[:, kcl * 512:(kcl + 1) * 512]),
                            inc=(f"{tag}_msk", 1) if kcl == 3 else None)
                if g > 0:
                    ctx.wait("tensor", f"{tag}_exp", ctx.v(f"{tag}_exp") - 1)
                    emit_av(g - 1, 1 - b)
                gtot += 1
            ctx.wait("tensor", f"{tag}_msk", ctx.v(f"{tag}_msk"))
            emit_av(ng - 1, (gtot - 1) % 2)
            avv = ctx.v(f"{tag}_av")
            ctx.wait("vector", f"{tag}_av", avv)
            for st in range(4):
                ctx.op("vector", lambda e, st=st, par=par: e.reciprocal(
                    rden[:, st:st + 1], po_ap(par, st, 128, 129)),
                    inc=(f"{tag}_rc", 1))
                ctx.wait("vector", f"{tag}_rc", ctx.v(f"{tag}_rc"))
                ob = st % 2
                if j > 0 or sqb > 0 or st >= 2:
                    ctx.wait("vector", f"{tag}_ost", max(0, 16 * (ctx.v(f"{tag}_nrm") - 1)))
                ctx.op("vector", lambda e, st=st, ob=ob, par=par: e.tensor_scalar(
                    onrm[ob][:, :], po_ap(par, st, 0, 128), rden[:, st:st + 1], None, OP.mult),
                    inc=(f"{tag}_nrm", 1))
                ctx.wait("sync", f"{tag}_nrm", ctx.v(f"{tag}_nrm"))
                ctx.dma("sync", o_d[j][sqb * 512 + st * 128: sqb * 512 + (st + 1) * 128, :],
                        onrm[ob][:, :], f"{tag}_ost")


def build_launch_B():
    nc = bass.Bass("TRN2", target_bir_lowering=False)
    qt_d = nc.dram_tensor("qt_d", [4, 128, S], BF16, kind="ExternalInput").ap()
    kt_d = nc.dram_tensor("kt_d", [4, 128, S], BF16, kind="ExternalInput").ap()
    v_d = nc.dram_tensor("v_d", [4, S, HD], BF16, kind="ExternalInput").ap()
    mask_d = nc.dram_tensor("mask_d", [128, 4 * 512], BF16, kind="ExternalInput").ap()
    o_d = nc.dram_tensor("o_d", [4, S, HD], BF16, kind="ExternalOutput").ap()
    with ExitStack() as stack:
        ctx = Ctx(nc, stack)
        mask_sb = ctx.sbuf("mask_sb", [128, 4 * 512], BF16)
        ctx.dma("sync", mask_sb[:, :], mask_d, "cld")
        ctx.wait("tensor", "cld", 16)
        ctx.wait("vector", "cld", 16)
        ctx.stage()
        emit_attn_l0(ctx, qt_d, kt_d, v_d, mask_sb, o_d, tag="at0")
        ctx.stage()
        ctx.wait("sync", "at0_ost", ctx.v("at0_ost"))
        ctx.finalize()
    return nc


def inmaps_B(a_results, c):
    maps = []
    m4 = np.ascontiguousarray(c["mask4"].transpose(1, 0, 2).reshape(128, 4 * 512))
    for core in range(NCORE):
        b, h0 = core // 4, (core % 4) * 4
        qt = np.zeros((4, 128, S), dtype=NPBF16)
        kt = np.zeros((4, 128, S), dtype=NPBF16)
        v = np.zeros((4, S, HD), dtype=NPBF16)
        for r in range(4):
            src = a_results[b * 4 + r]
            sl = slice(r * TPC, (r + 1) * TPC)
            for j in range(4):
                h = h0 + j
                qt[j][:, sl] = src["qt_o"][h * 128:(h + 1) * 128, :]
                kt[j][:, sl] = src["kt_o"][h * 128:(h + 1) * 128, :]
                v[j][sl, :] = src["v_o"][:, h * 128:(h + 1) * 128]
        maps.append({"qt_d": qt, "kt_d": kt, "v_d": v, "mask_d": m4})
    return maps




def emit_l0_epi(ctx, o_dram, sg_dram, normw_sb, gatedT, tag="ep0"):
    ot = [ctx.sbuf(f"{tag}_ot{i}", [128, D], BF16) for i in range(2)]
    sgt = [ctx.sbuf(f"{tag}_sg{i}", [128, D], BF16) for i in range(2)]
    gat = [ctx.sbuf(f"{tag}_ga{i}", [128, D], BF16) for i in range(2)]
    scr = ctx.sbuf(f"{tag}_scr", [128, 128], F32)
    nws = ctx.sbuf(f"{tag}_nws", [128, 128], F32)
    ssq = ctx.sbuf(f"{tag}_ssq", [128, NH], F32)
    rstd = ctx.sbuf(f"{tag}_rstd", [128, NH], F32)
    tp = [ctx.psum(f"{tag}_tp{i}", [128, 128], BF16) for i in range(2)]
    ident = ctx.ident
    for i in range(NT):
        b = i % 2
        if i >= 2:
            ctx.wait("sync", f"{tag}_ga", (i - 1) * NH)
            ctx.wait("sync", f"{tag}_sq", (i - 1) * NH)
        ctx.dma("sync", ot[b][:, :], o_dram[i * 128:(i + 1) * 128, :], f"{tag}_old")
        ctx.dma("sync", sgt[b][:, :], sg_dram[i * 128:(i + 1) * 128, :], f"{tag}_old")
        lv = ctx.v(f"{tag}_old")
        ctx.wait("scalar", f"{tag}_old", lv)
        if i >= 1:
            ctx.wait("scalar", f"{tag}_rc", i)
        for h in range(NH):
            ctx.op("scalar", lambda e, b=b, h=h: e.activation(
                scr[:, :], ot[b][:, h * 128:(h + 1) * 128], AF.Square,
                accum_out=ssq[:, h:h + 1]),
                inc=(f"{tag}_sq", 1))
        ctx.wait("scalar", f"{tag}_sq", ctx.v(f"{tag}_sq"))
        ctx.op("scalar", lambda e: e.activation(ssq[:, :], ssq[:, :], AF.Sqrt,
                                                scale=np.float32(1.0 / HD),
                                                bias=np.float32(1e-5)),
               inc=(f"{tag}_sqa", 1))
        ctx.wait("vector", f"{tag}_sqa", ctx.v(f"{tag}_sqa"))
        ctx.op("vector", lambda e: e.reciprocal(rstd[:, :], ssq[:, :]),
               inc=(f"{tag}_rc", 1))
        rcv = ctx.v(f"{tag}_rc")
        ctx.wait("vector", f"{tag}_rc", rcv)
        ctx.wait("vector", f"{tag}_old", lv)
        if i >= 2:
            ctx.wait("vector", f"{tag}_tp", (i - 1) * NDC)
        for h in range(NH):
            ctx.op("vector", lambda e, b=b, h=h: e.tensor_mul(
                nws[:, :], normw_sb[:, :], sgt[b][:, h * 128:(h + 1) * 128]),
                inc=(f"{tag}_nw", 1))
            ctx.wait("vector", f"{tag}_nw", ctx.v(f"{tag}_nw"))
            ctx.op("vector", lambda e, b=b, h=h: e.scalar_tensor_tensor(
                gat[b][:, h * 128:(h + 1) * 128], ot[b][:, h * 128:(h + 1) * 128],
                rstd[:, h:h + 1], nws[:, :], OP.mult, OP.mult),
                inc=(f"{tag}_ga", 1))
        gav = ctx.v(f"{tag}_ga")
        ctx.wait("tensor", f"{tag}_ga", gav)
        for dc in range(NDC):
            g = i * NDC + dc
            pb = g % 2
            if g >= 2:
                ctx.wait("tensor", f"{tag}_tev", g - 1)
            ctx.op("tensor", lambda e, b=b, dc=dc, pb=pb: e.transpose(
                tp[pb][:, :], gat[b][:, dc * 128:(dc + 1) * 128], ident[:, :]),
                inc=(f"{tag}_tp", 1))
            ctx.wait("vector", f"{tag}_tp", ctx.v(f"{tag}_tp"))
            ctx.op("vector", lambda e, dc=dc, i=i, pb=pb: e.tensor_copy(
                gatedT[:, dc * TPC + i * 128: dc * TPC + (i + 1) * 128], tp[pb][:, :]),
                inc=(f"{tag}_tev", 1))


def emit_out_proj(ctx, aT, woT_c_dram, x_dram, fg_dram, xout_dram, tag):
    wsb = ctx.sbuf(f"{tag}_w", [128, NDC * 512], BF16)
    xsb = [ctx.sbuf(f"{tag}_x{i}", [128, 512], F32) for i in range(2)]
    fsb = [ctx.sbuf(f"{tag}_f{i}", [128, 512], F32) for i in range(2)]
    t1 = ctx.sbuf(f"{tag}_t1", [128, 512], F32)
    stg = [ctx.sbuf(f"{tag}_stg{i}", [128, 512], F32) for i in range(2)]
    ps = [ctx.psum(f"{tag}_ps{i}", [128, 512]) for i in range(2)]
    for cb in range(4):
        if cb > 0:
            ctx.wait("gpsimd", f"{tag}_mm", cb * NT)
        srcw = woT_c_dram[cb].rearrange("(dc p) c -> p dc c", p=128)
        ctx.dma("gpsimd", wsb[:, :].rearrange("p (dc c) -> p dc c", dc=NDC), srcw, f"{tag}_wld")
        ctx.wait("tensor", f"{tag}_wld", ctx.v(f"{tag}_wld"))
        for i in range(NT):
            g = cb * NT + i
            b = g % 2
            if g >= 2:
                ctx.wait("sync", f"{tag}_ev", g - 1)
            ctx.dma("sync", xsb[b][:, :], x_dram[i * 128:(i + 1) * 128, cb * 512:(cb + 1) * 512],
                    f"{tag}_xld")
            ctx.dma("sync", fsb[b][:, :], fg_dram[i * 128:(i + 1) * 128, cb * 512:(cb + 1) * 512],
                    f"{tag}_xld")
            xlv = ctx.v(f"{tag}_xld")
            if g >= 2:
                ctx.wait("tensor", f"{tag}_ev", g - 1)
            for dc in range(NDC):
                ctx.op("tensor", lambda e, dc=dc, i=i, b=b: e.matmul(
                    ps[b][:, :], aT[:, dc * TPC + i * 128: dc * TPC + (i + 1) * 128],
                    wsb[:, dc * 512:(dc + 1) * 512],
                    start=(dc == 0), stop=(dc == NDC - 1)),
                    inc=(f"{tag}_mm", 1) if dc == NDC - 1 else None)
            ctx.wait("vector", f"{tag}_mm", ctx.v(f"{tag}_mm"))
            ctx.wait("vector", f"{tag}_xld", xlv)
            if g >= 2:
                ctx.wait("vector", f"{tag}_st", 16 * (g - 1))
            ctx.op("vector", lambda e, b=b: e.tensor_mul(t1[:, :], ps[b][:, :], fsb[b][:, :]),
                   inc=(f"{tag}_t1", 1))
            ctx.wait("vector", f"{tag}_t1", ctx.v(f"{tag}_t1"))
            ctx.op("vector", lambda e, b=b: e.tensor_add(stg[b][:, :], t1[:, :], xsb[b][:, :]),
                   inc=(f"{tag}_ev", 1))
            ctx.wait("sync", f"{tag}_ev", ctx.v(f"{tag}_ev"))
            ctx.dma("sync", xout_dram[i * 128:(i + 1) * 128, cb * 512:(cb + 1) * 512],
                    stg[b][:, :], f"{tag}_st")


def emit_q_norope(ctx, xnT, wT_h_dram, out_dram, scale, tag):
    wsb = [ctx.sbuf(f"{tag}_w{i}", [128, NDC * 128], BF16) for i in range(2)]
    rop = [ctx.sbuf(f"{tag}_rop{i}", [128, TPC], BF16) for i in range(2)]
    ps = [ctx.psum(f"{tag}_ps{i}", [128, 512]) for i in range(2)]
    NSQB = TPC // 512
    for h in range(NH):
        wb = h % 2
        if h >= 2:
            ctx.wait("gpsimd", f"{tag}_mm", (h - 1) * NSQB)
        src = wT_h_dram[h].rearrange("(dc p) c -> p dc c", p=128)
        ctx.dma("gpsimd", wsb[wb][:, :].rearrange("p (dc c) -> p dc c", dc=NDC), src, f"{tag}_wld")
        ctx.wait("tensor", f"{tag}_wld", 16 * (h + 1))
        rb = h % 2
        if h >= 2:
            ctx.wait("scalar", f"{tag}_qst", 16 * (h - 1))
        for sqb in range(NSQB):
            g = h * NSQB + sqb
            pb = g % 2
            if g >= 2:
                ctx.wait("tensor", f"{tag}_evac", g - 1)
            for dc in range(NDC):
                ctx.op("tensor", lambda e, dc=dc, sqb=sqb, pb=pb, wb=wb: e.matmul(
                    ps[pb][:, :], wsb[wb][:, dc * 128:(dc + 1) * 128],
                    xnT[:, dc * TPC + sqb * 512: dc * TPC + sqb * 512 + 512],
                    start=(dc == 0), stop=(dc == NDC - 1)),
                    inc=(f"{tag}_mm", 1) if dc == NDC - 1 else None)
            ctx.wait("scalar", f"{tag}_mm", ctx.v(f"{tag}_mm"))
            if scale == 1.0:
                ctx.op("scalar", lambda e, sqb=sqb, pb=pb, rb=rb: e.copy(
                    rop[rb][:, sqb * 512:sqb * 512 + 512], ps[pb][:, :]),
                    inc=(f"{tag}_evac", 1))
            else:
                ctx.op("scalar", lambda e, sqb=sqb, pb=pb, rb=rb: e.activation(
                    rop[rb][:, sqb * 512:sqb * 512 + 512], ps[pb][:, :], AF.Identity,
                    scale=np.float32(scale)),
                    inc=(f"{tag}_evac", 1))
        ctx.wait("sync", f"{tag}_evac", ctx.v(f"{tag}_evac"))
        ctx.dma("sync", out_dram[h * 128:(h + 1) * 128, :], rop[rb][:, :], f"{tag}_qst")


def emit_ffn1(ctx, xnT, wg_dram, wu_dram, wd_dram, sg_sc, su_sc, sd_bc, fg_out, tag="f1"):
    NFC = INTER // 128
    wgp = [ctx.sbuf(f"{tag}_wg{i}", [128, NDC * 128], BF16) for i in range(2)]
    wup = [ctx.sbuf(f"{tag}_wu{i}", [128, NDC * 128], BF16) for i in range(2)]
    wd_sb = [ctx.sbuf(f"{tag}_wd{i}", [128, NDC * 512], BF16) for i in range(2)]
    fuse = ctx.sbuf(f"{tag}_fuse", [128, NFC * 512], BF16)
    sgs = [ctx.sbuf(f"{tag}_sgs{i}", [128, 512], BF16) for i in range(2)]
    t1 = ctx.sbuf(f"{tag}_t1", [128, 512], F32)
    stg = [ctx.sbuf(f"{tag}_stg{i}", [128, 512], F32) for i in range(2)]
    psg = [ctx.psum(f"{tag}_psg{i}", [128, 512]) for i in range(2)]
    psu = [ctx.psum(f"{tag}_psu{i}", [128, 512]) for i in range(2)]
    psd = [ctx.psum(f"{tag}_psd{i}", [128, 512]) for i in range(4)]

    for sqb in range(TPC // 512):
        for fs in range(16):
            for fc in range(4):
                fcg = fs * 4 + fc
                git = (sqb * 16 + fs) * 4 + fc
                pb = git % 2
                if git >= 2:
                    ctx.wait("gpsimd", f"{tag}_gmm", git - 1)
                    ctx.wait("gpsimd", f"{tag}_umm", git - 1)
                srcg = wg_dram[fs][:, fc * 128:(fc + 1) * 128].rearrange(
                    "(c p) f -> p c f", p=128)
                ctx.dma("gpsimd", wgp[pb][:, :].rearrange("p (c f) -> p c f", c=NDC),
                        srcg, f"{tag}_wgld")
                srcu = wu_dram[fs][:, fc * 128:(fc + 1) * 128].rearrange(
                    "(c p) f -> p c f", p=128)
                ctx.dma("gpsimd", wup[pb][:, :].rearrange("p (c f) -> p c f", c=NDC),
                        srcu, f"{tag}_wuld")
                ctx.wait("tensor", f"{tag}_wgld", 16 * (git + 1))
                ctx.wait("tensor", f"{tag}_wuld", 16 * (git + 1))
                if git >= 2:
                    ctx.wait("tensor", f"{tag}_silu", git - 1)
                    ctx.wait("tensor", f"{tag}_fuse", git - 1)
                for dc in range(NDC):
                    ctx.op("tensor", lambda e, dc=dc, sqb=sqb, pb=pb: e.matmul(
                        psg[pb][:, :], wgp[pb][:, dc * 128:(dc + 1) * 128],
                        xnT[:, dc * TPC + sqb * 512: dc * TPC + sqb * 512 + 512],
                        start=(dc == 0), stop=(dc == NDC - 1)),
                        inc=(f"{tag}_gmm", 1) if dc == NDC - 1 else None)
                for dc in range(NDC):
                    ctx.op("tensor", lambda e, dc=dc, sqb=sqb, pb=pb: e.matmul(
                        psu[pb][:, :], wup[pb][:, dc * 128:(dc + 1) * 128],
                        xnT[:, dc * TPC + sqb * 512: dc * TPC + sqb * 512 + 512],
                        start=(dc == 0), stop=(dc == NDC - 1)),
                        inc=(f"{tag}_umm", 1) if dc == NDC - 1 else None)
                ctx.wait("scalar", f"{tag}_gmm", ctx.v(f"{tag}_gmm"))
                if git >= 2:
                    ctx.wait("scalar", f"{tag}_fuse", git - 1)
                ctx.op("scalar", lambda e, fcg=fcg, pb=pb: e.activation(
                    sgs[pb][:, :], psg[pb][:, :], AF.Silu, scale=sg_sc[:, fcg:fcg + 1]),
                    inc=(f"{tag}_silu", 1))
                ctx.wait("vector", f"{tag}_silu", ctx.v(f"{tag}_silu"))
                ctx.wait("vector", f"{tag}_umm", ctx.v(f"{tag}_umm"))
                ctx.op("vector", lambda e, fcg=fcg, pb=pb: e.scalar_tensor_tensor(
                    fuse[:, fcg * 512:(fcg + 1) * 512], psu[pb][:, :],
                    su_sc[:, fcg:fcg + 1], sgs[pb][:, :], OP.mult, OP.mult),
                    inc=(f"{tag}_fuse", 1))
        fusev = ctx.v(f"{tag}_fuse")
        for ds in range(4):
            for fs2 in range(4):
                lidx = (sqb * 4 + ds) * 4 + fs2
                wb = lidx % 2
                if lidx >= 2:
                    ctx.wait("gpsimd", f"{tag}_dmm", (lidx - 1) * 4)
                srcd = wd_dram[ds][fs2 * 2048:(fs2 + 1) * 2048, :].rearrange(
                    "(c p) f -> p c f", p=128)
                ctx.dma("gpsimd", wd_sb[wb][:, :].rearrange("p (c f) -> p c f", c=NDC),
                        srcd, f"{tag}_wdld")
                ctx.wait("tensor", f"{tag}_wdld", 16 * (lidx + 1))
                if fs2 == 0:
                    if ds == 0:
                        ctx.wait("tensor", f"{tag}_fuse", fusev)
                    ctx.wait("tensor", f"{tag}_ev", ctx.v(f"{tag}_ev"))
                for sqt in range(4):
                    for fcl in range(NDC):
                        fcg = fs2 * NDC + fcl
                        ctx.op("tensor", lambda e, fcg=fcg, sqt=sqt, fs2=fs2, fcl=fcl,
                               wb=wb: e.matmul(
                            psd[sqt][:, :],
                            fuse[:, fcg * 512 + sqt * 128: fcg * 512 + (sqt + 1) * 128],
                            wd_sb[wb][:, fcl * 512:(fcl + 1) * 512],
                            start=(fs2 == 0 and fcl == 0), stop=(fs2 == 3 and fcl == NDC - 1)),
                            inc=(f"{tag}_dmm", 1) if fcl == NDC - 1 else None)
            ctx.wait("vector", f"{tag}_dmm", ctx.v(f"{tag}_dmm"))
            for sqt in range(4):
                g = (sqb * 4 + ds) * 4 + sqt
                b = g % 2
                ctx.wait("vector", f"{tag}_ev", ctx.v(f"{tag}_ev"))
                ctx.op("vector", lambda e, sqt=sqt, ds=ds: e.tensor_mul(
                    t1[:, :], psd[sqt][:, :], sd_bc[:, ds * 512:(ds + 1) * 512]),
                    inc=(f"{tag}_t1", 1))
                ctx.wait("scalar", f"{tag}_t1", ctx.v(f"{tag}_t1"))
                if g >= 2:
                    ctx.wait("scalar", f"{tag}_fgst", 16 * (g - 1))
                ctx.op("scalar", lambda e, b=b: e.activation(stg[b][:, :], t1[:, :], AF.Silu),
                       inc=(f"{tag}_ev", 1))
                ctx.wait("sync", f"{tag}_ev", ctx.v(f"{tag}_ev"))
                row = sqb * 512 + sqt * 128
                ctx.dma("sync", fg_out[row:row + 128, ds * 512:(ds + 1) * 512], stg[b][:, :],
                        f"{tag}_fgst")


def build_launch_C():
    nc = bass.Bass("TRN2", target_bir_lowering=False)
    o_d = nc.dram_tensor("o_d", [TPC, D], BF16, kind="ExternalInput").ap()
    sg_d = nc.dram_tensor("sg_d", [TPC, D], BF16, kind="ExternalInput").ap()
    x_d = nc.dram_tensor("x_d", [TPC, D], F32, kind="ExternalInput").ap()
    fg_d = nc.dram_tensor("fg_d", [TPC, D], F32, kind="ExternalInput").ap()
    normw_d = nc.dram_tensor("normw_d", [128, 128], F32, kind="ExternalInput").ap()
    wo_c = nc.dram_tensor("wo_c", [4, D, 512], F32, kind="ExternalInput").ap()
    wq_h = nc.dram_tensor("wq_h", [NH, D, HD], F32, kind="ExternalInput").ap()
    wk_h = nc.dram_tensor("wk_h", [NH, D, HD], F32, kind="ExternalInput").ap()
    wv_c = nc.dram_tensor("wv_c", [4, D, 512], F32, kind="ExternalInput").ap()
    wg1_d = nc.dram_tensor("wg1_d", [16, D, 512], I8, kind="ExternalInput").ap()
    wu1_d = nc.dram_tensor("wu1_d", [16, D, 512], I8, kind="ExternalInput").ap()
    wd1_d = nc.dram_tensor("wd1_d", [4, INTER, 512], I8, kind="ExternalInput").ap()
    sg_sc_d = nc.dram_tensor("sg_sc_d", [128, 64], F32, kind="ExternalInput").ap()
    su_sc_d = nc.dram_tensor("su_sc_d", [128, 64], F32, kind="ExternalInput").ap()
    sd_bc_d = nc.dram_tensor("sd_bc_d", [128, D], F32, kind="ExternalInput").ap()
    identd = nc.dram_tensor("identd", [128, 128], BF16, kind="ExternalInput").ap()

    x1_o = nc.dram_tensor("x1_o", [TPC, D], F32, kind="ExternalOutput").ap()
    qt1_o = nc.dram_tensor("qt1_o", [D, TPC], BF16, kind="ExternalOutput").ap()
    kt1_o = nc.dram_tensor("kt1_o", [D, TPC], BF16, kind="ExternalOutput").ap()
    v1_o = nc.dram_tensor("v1_o", [TPC, D], BF16, kind="ExternalOutput").ap()
    fg1_o = nc.dram_tensor("fg1_o", [TPC, D], F32, kind="ExternalOutput").ap()
    xn1_o = nc.dram_tensor("xn1_o", [TPC, D], BF16, kind="ExternalOutput").ap()

    with ExitStack() as stack:
        ctx = Ctx(nc, stack)
        ctx.ident = ctx.sbuf("identsb", [128, 128], BF16)
        gatedT = ctx.sbuf("gatedT", [128, NDC * TPC], BF16)
        xn1T = ctx.sbuf("xn1T", [128, NDC * TPC], BF16)
        normw_sb = ctx.sbuf("normw_sb", [128, 128], F32)
        sg_sc = ctx.sbuf("sg_sc", [128, 64], F32)
        su_sc = ctx.sbuf("su_sc", [128, 64], F32)
        sd_bc = ctx.sbuf("sd_bc", [128, D], F32)
        ctx.dma("sync", ctx.ident[:, :], identd, "cld")
        ctx.dma("sync", normw_sb[:, :], normw_d, "cld")
        ctx.dma("sync", sg_sc[:, :], sg_sc_d, "cld")
        ctx.dma("sync", su_sc[:, :], su_sc_d, "cld")
        ctx.dma("sync", sd_bc[:, :], sd_bc_d, "cld")

        def phase(*emitfns):
            ctx.stage()
            with ExitStack() as ph:
                ctx.push_scope(ph)
                for fn in emitfns:
                    fn()
                ctx.pop_scope()

        phase(lambda: emit_l0_epi(ctx, o_d, sg_d, normw_sb, gatedT, tag="ep0"))
        phase(lambda: emit_out_proj(ctx, gatedT, wo_c, x_d, fg_d, x1_o, tag="op0"))
        phase(lambda: emit_ln_xnT(ctx, x1_o, xn1T, xn_out_dram=xn1_o, tag="ln1"))
        phase(lambda: emit_q_norope(ctx, xn1T, wq_h, qt1_o, float(SCALE), tag="q1"),
              lambda: emit_q_norope(ctx, xn1T, wk_h, kt1_o, 1.0, tag="k1"),
              lambda: emit_vg_proj(ctx, xn1T, wv_c, v1_o, tag="v1", silu=False))
        phase(lambda: emit_ffn1(ctx, xn1T, wg1_d, wu1_d, wd1_d, sg_sc, su_sc, sd_bc,
                                fg1_o, tag="f1"))
        ctx.stage()
        for s_ in ("op0_st", "ln1_xnst", "q1_qst", "k1_qst", "v1_st", "f1_fgst"):
            ctx.wait("sync", s_, ctx.v(s_))
        ctx.finalize()
    return nc


def inmaps_C(inputs, w, c, a_results, b_results):
    x = inputs["L0_attn_hidden_states"]
    maps = []
    for core in range(NCORE):
        b, t0 = core // 4, (core % 4) * TPC
        O_tok = np.zeros((TPC, D), dtype=NPBF16)
        for hc in range(4):
            src = b_results[b * 4 + hc]["o_d"]
            for j in range(4):
                h = hc * 4 + j
                O_tok[:, h * 128:(h + 1) * 128] = src[j][t0:t0 + TPC, :]
        maps.append({
            "o_d": O_tok,
            "sg_d": a_results[core]["sg_o"],
            "fg_d": a_results[core]["fg_o"],
            "x_d": np.ascontiguousarray(x[b, t0:t0 + TPC, :]),
            "normw_d": w["l0_normw_t"],
            "wo_c": w["l0_woT_c"],
            "wq_h": w["l1_wqT_h"], "wk_h": w["l1_wkT_h"], "wv_c": w["l1_wvT_c"],
            "wg1_d": w["l1_wg_t"], "wu1_d": w["l1_wu_t"], "wd1_d": w["l1_wd_t"],
            "sg_sc_d": w["l1_sg_t"], "su_sc_d": w["l1_su_t"], "sd_bc_d": w["l1_sd_t"],
            "identd": c["ident"],
        })
    return maps




def emit_lam(ctx, lam_d, ones_d, nlam_b, tag="lam"):
    lamv = ctx.sbuf(f"{tag}_v", [1, 128], F32)
    lscr = ctx.sbuf(f"{tag}_scr", [1, 128], F32)
    lamt = ctx.sbuf(f"{tag}_t", [1, 8], F32)
    onesb = ctx.sbuf(f"{tag}_ones1", [1, 128], F32)
    ps_l = ctx.psum(f"{tag}_ps", [128, 4])
    ctx.dma("sync", onesb[:, :], ones_d, f"{tag}_ld")
    for idx, (qrow, krow) in enumerate(((0, 2), (1, 3))):
        ctx.dma("sync", lamv[:, :], lam_d[qrow:qrow + 1, :], f"{tag}_ld")
        ctx.dma("sync", lscr[:, :], lam_d[krow:krow + 1, :], f"{tag}_ld")
        lv = ctx.v(f"{tag}_ld")
        ctx.wait("vector", f"{tag}_ld", lv)
        ctx.op("vector", lambda e: e.tensor_mul(lamv[:, :], lamv[:, :], lscr[:, :]),
               inc=(f"{tag}_ch", 1))
        ctx.wait("vector", f"{tag}_ch", ctx.v(f"{tag}_ch"))
        ctx.op("vector", lambda e, idx=idx: e.reduce_sum(lamt[:, idx:idx + 1], lamv[:, :], AX.X),
               inc=(f"{tag}_ch", 1))
        ctx.wait("vector", f"{tag}_ch", ctx.v(f"{tag}_ch"))
        ctx.wait("sync", f"{tag}_ch", ctx.v(f"{tag}_ch"))
    ctx.wait("scalar", f"{tag}_ch", ctx.v(f"{tag}_ch"))
    ctx.op("scalar", lambda e: e.activation(lamt[:, 2:4], lamt[:, 0:2], AF.Exp),
           inc=(f"{tag}_sc", 1))
    ctx.wait("vector", f"{tag}_sc", ctx.v(f"{tag}_sc"))
    ctx.op("vector", lambda e: e.tensor_sub(lamt[:, 4:5], lamt[:, 3:4], lamt[:, 2:3]),
           inc=(f"{tag}_ch", 1))
    ctx.wait("vector", f"{tag}_ch", ctx.v(f"{tag}_ch"))
    ctx.op("vector", lambda e: e.tensor_single_scalar(lamt[:, 5:6], lamt[:, 4:5], -0.8, OP.add),
           inc=(f"{tag}_ch", 1))
    ctx.wait("tensor", f"{tag}_ch", ctx.v(f"{tag}_ch"))
    ctx.op("tensor", lambda e: e.matmul(ps_l[:, 0:1], onesb[:, :], lamt[:, 5:6],
                                        start=True, stop=True),
           inc=(f"{tag}_mm", 1))
    ctx.wait("vector", f"{tag}_mm", ctx.v(f"{tag}_mm"))
    ctx.op("vector", lambda e: e.tensor_copy(nlam_b[:, :], ps_l[:, 0:1]),
           inc=(f"{tag}_bc", 1))


def emit_attn_l1(ctx, qt_d, kt_d, v_d, nlam_b, o_d, tag="at1"):
    NKC = S // 128
    SQB = 256
    NSQB = S // SQB
    GPH = NSQB * (NKC // 2)
    qt = [ctx.sbuf(f"{tag}_qt{i}", [128, S], BF16) for i in range(2)]
    kt = [ctx.sbuf(f"{tag}_kt{i}", [128, S], BF16) for i in range(2)]
    ve = [ctx.sbuf(f"{tag}_ve{i}", [128, NKC * 129], BF16) for i in range(2)]
    pexp = [ctx.sbuf(f"{tag}_px{i}", [128, 1024], BF16) for i in range(2)]
    onrm = [ctx.sbuf(f"{tag}_on{i}", [128, 128], BF16) for i in range(2)]
    rr = ctx.sbuf(f"{tag}_rr", [128, 8], F32)
    oscr = ctx.sbuf(f"{tag}_oscr", [128, 128], F32)
    sscr = ctx.sbuf(f"{tag}_sscr", [128, 128], F32)
    ps_s = ctx.psum(f"{tag}_pss", [128, 2048])
    ps_o = [ctx.psum(f"{tag}_pso{i}", [128, 132]) for i in range(4)]
    posb = ctx.sbuf(f"{tag}_posb", [128, 4 * 132], F32)

    def po_ap(par, idx, a, b):
        return ps_o[idx][:, a:b]
    for i in range(2):
        ctx.op("vector", lambda e, i=i: e.memset(
            ve[i][:, :].rearrange("p (c x) -> p c x", x=129)[:, :, 128:129], 1.0))
    ctx.nopinc("vector", f"{tag}_ones", 1)

    gtot = 0
    for j in range(4):
        hb = j % 2
        if j >= 2:
            ctx.wait("sync", f"{tag}_smm", (j - 1) * GPH)
            ctx.wait("sync", f"{tag}_av", (j - 1) * GPH)
        ctx.dma("sync", qt[hb][:, :], qt_d[j], f"{tag}_qld")
        ctx.dma("sync", kt[hb][:, :], kt_d[j], f"{tag}_kld")
        vsrc = v_d[j].rearrange("(c p) d -> p c d", p=128)
        vdst = ve[hb][:, :].rearrange("p (c x) -> p c x", x=129)[:, :, 0:128]
        ctx.dma("sync", vdst, vsrc, f"{tag}_vld")
        ldv = (ctx.v(f"{tag}_qld"), ctx.v(f"{tag}_kld"), ctx.v(f"{tag}_vld"))
        ctx.wait("tensor", f"{tag}_qld", ldv[0])
        ctx.wait("tensor", f"{tag}_kld", ldv[1])
        ctx.wait("tensor", f"{tag}_vld", ldv[2])
        if j == 0:
            ctx.wait("tensor", f"{tag}_ones", 1)
        for sqb in range(NSQB):
            par = sqb % 2
            pev_prev = ctx.v(f"{tag}_pev")
            if pev_prev > 0:
                ctx.wait("tensor", f"{tag}_pev", pev_prev)
            def emit_av(g, b):
                for kcl in range(2):
                    kc = 2 * g + kcl
                    for st in range(2):
                        ctx.op("tensor", lambda e, st=st, kc=kc, kcl=kcl, b=b, hb=hb,
                               par=par: e.matmul(
                            po_ap(par, st, 0, 129),
                            pexp[b][:, kcl * 512 + st * 128: kcl * 512 + (st + 1) * 128],
                            ve[hb][:, kc * 129:(kc + 1) * 129],
                            start=(kc == 0), stop=(kc == NKC - 1)))
                    for st in range(2):
                        ctx.op("tensor", lambda e, st=st, kc=kc, kcl=kcl, b=b, hb=hb,
                               par=par: e.matmul(
                            po_ap(par, 2 + st, 0, 129),
                            pexp[b][:, kcl * 512 + 256 + st * 128: kcl * 512 + 256 + (st + 1) * 128],
                            ve[hb][:, kc * 129:(kc + 1) * 129],
                            start=(kc == 0), stop=(kc == NKC - 1)),
                            inc=(f"{tag}_av", 1) if (kcl == 1 and st == 1) else None)

            ps_s3 = ps_s[:, :].rearrange("p (k x) -> p k x", k=4)[:, :, 0:256]
            for g in range(NKC // 2):
                b = gtot % 2
                ctx.wait("tensor", f"{tag}_exp", ctx.v(f"{tag}_exp"))
                for kcl in range(2):
                    kc = 2 * g + kcl
                    ctx.op("tensor", lambda e, kc=kc, kcl=kcl, sqb=sqb, hb=hb: e.matmul(
                        ps_s[:, kcl * 1024: kcl * 1024 + 256],
                        kt[hb][0:64, kc * 128:(kc + 1) * 128],
                        qt[hb][0:64, sqb * SQB:(sqb + 1) * SQB], start=True, stop=True))
                    ctx.op("tensor", lambda e, kc=kc, kcl=kcl, sqb=sqb, hb=hb: e.matmul(
                        ps_s[:, kcl * 1024 + 512: kcl * 1024 + 768],
                        kt[hb][64:128, kc * 128:(kc + 1) * 128],
                        qt[hb][64:128, sqb * SQB:(sqb + 1) * SQB], start=True, stop=True),
                        inc=(f"{tag}_smm", 1) if kcl == 1 else None)
                ctx.wait("scalar", f"{tag}_smm", ctx.v(f"{tag}_smm"))
                ctx.wait("scalar", f"{tag}_av", max(0, gtot - 1))
                ctx.op("scalar", lambda e, b=b: e.activation(
                    pexp[b][:, :].rearrange("p (k x) -> p k x", k=4), ps_s3, AF.Exp),
                    inc=(f"{tag}_exp", 1))
                if g > 0:
                    ctx.wait("tensor", f"{tag}_exp", ctx.v(f"{tag}_exp") - 1)
                    emit_av(g - 1, 1 - b)
                gtot += 1
            ctx.wait("tensor", f"{tag}_exp", ctx.v(f"{tag}_exp"))
            emit_av(NKC // 2 - 1, (gtot - 1) % 2)
            avv = ctx.v(f"{tag}_av")
            ctx.wait("scalar", f"{tag}_av", avv)
            ctx.wait("scalar", f"{tag}_oc", ctx.v(f"{tag}_oc"))
            for idx in range(4):
                ctx.op("scalar", lambda e, idx=idx, par=par: e.copy(
                    posb[:, idx * 132:(idx + 1) * 132], po_ap(par, idx, 0, 132)),
                    inc=(f"{tag}_pev", 1) if idx == 3 else None)
            ctx.wait("vector", f"{tag}_pev", ctx.v(f"{tag}_pev"))
            for st in range(2):
                ctx.op("vector", lambda e, st=st: e.reciprocal(
                    rr[:, 0:1], posb[:, st * 132 + 128: st * 132 + 129]),
                    inc=(f"{tag}_rc", 1))
                ctx.op("vector", lambda e, st=st: e.reciprocal(
                    rr[:, 1:2], posb[:, (2 + st) * 132 + 128: (2 + st) * 132 + 129]),
                    inc=(f"{tag}_rc", 1))
                ctx.wait("vector", f"{tag}_rc", ctx.v(f"{tag}_rc"))
                ctx.op("vector", lambda e: e.tensor_mul(rr[:, 2:3], rr[:, 1:2], nlam_b[:, :]),
                       inc=(f"{tag}_rc", 1))
                ctx.wait("vector", f"{tag}_rc", ctx.v(f"{tag}_rc"))
                ctx.op("vector", lambda e, st=st: e.tensor_scalar(
                    oscr[:, :], posb[:, st * 132: st * 132 + 128], rr[:, 0:1], None, OP.mult),
                    inc=(f"{tag}_rc", 1))
                ctx.wait("vector", f"{tag}_rc", ctx.v(f"{tag}_rc"))
                ctx.op("vector", lambda e, st=st: e.scalar_tensor_tensor(
                    oscr[:, :], posb[:, (2 + st) * 132: (2 + st) * 132 + 128],
                    rr[:, 2:3], oscr[:, :], OP.mult, OP.add),
                    inc=(f"{tag}_oc", 1))
                ocv = ctx.v(f"{tag}_oc")
                ctx.wait("scalar", f"{tag}_oc", ocv)
                ctx.op("scalar", lambda e: e.activation(sscr[:, :], oscr[:, :], AF.Square,
                                                        accum_out=rr[:, 3:4]),
                       inc=(f"{tag}_sq", 1))
                ctx.wait("scalar", f"{tag}_sq", ctx.v(f"{tag}_sq"))
                ctx.op("scalar", lambda e: e.activation(rr[:, 4:5], rr[:, 3:4], AF.Sqrt,
                                                        scale=np.float32(1.0 / HD),
                                                        bias=np.float32(1.1920928955078125e-07)),
                       inc=(f"{tag}_sqa", 1))
                ctx.wait("vector", f"{tag}_sqa", ctx.v(f"{tag}_sqa"))
                ctx.op("vector", lambda e: e.reciprocal(rr[:, 5:6], rr[:, 4:5]),
                       inc=(f"{tag}_rc", 1))
                ctx.wait("vector", f"{tag}_rc", ctx.v(f"{tag}_rc"))
                ctx.op("vector", lambda e: e.tensor_single_scalar(rr[:, 6:7], rr[:, 5:6], 0.2, OP.mult),
                       inc=(f"{tag}_rc", 1))
                ctx.wait("vector", f"{tag}_rc", ctx.v(f"{tag}_rc"))
                ob = st % 2
                g = ctx.v(f"{tag}_nrm")
                if g >= 2:
                    ctx.wait("vector", f"{tag}_ost", 16 * (g - 1))
                ctx.op("vector", lambda e, ob=ob: e.tensor_scalar(
                    onrm[ob][:, :], oscr[:, :], rr[:, 6:7], None, OP.mult),
                    inc=(f"{tag}_nrm", 1))
                ctx.wait("sync", f"{tag}_nrm", ctx.v(f"{tag}_nrm"))
                row = sqb * SQB + st * 128
                ctx.dma("sync", o_d[j][row:row + 128, :], onrm[ob][:, :], f"{tag}_ost")


def build_launch_D():
    nc = bass.Bass("TRN2", target_bir_lowering=False)
    qt_d = nc.dram_tensor("qt_d", [4, 128, S], BF16, kind="ExternalInput").ap()
    kt_d = nc.dram_tensor("kt_d", [4, 128, S], BF16, kind="ExternalInput").ap()
    v_d = nc.dram_tensor("v_d", [4, S, HD], BF16, kind="ExternalInput").ap()
    lam_d = nc.dram_tensor("lam_d", [4, 128], F32, kind="ExternalInput").ap()
    ones_d = nc.dram_tensor("ones_d", [1, 128], F32, kind="ExternalInput").ap()
    o_d = nc.dram_tensor("o_d", [4, S, HD], BF16, kind="ExternalOutput").ap()
    with ExitStack() as stack:
        ctx = Ctx(nc, stack)
        nlam_b = ctx.sbuf("nlam_b", [128, 1], F32)
        with ExitStack() as ph:
            ctx.push_scope(ph)
            emit_lam(ctx, lam_d, ones_d, nlam_b, tag="lam")
            ctx.pop_scope()
        ctx.stage()
        with ExitStack() as ph:
            ctx.push_scope(ph)
            emit_attn_l1(ctx, qt_d, kt_d, v_d, nlam_b, o_d, tag="at1")
            ctx.pop_scope()
        ctx.stage()
        ctx.wait("sync", "at1_ost", ctx.v("at1_ost"))
        ctx.finalize()
    return nc


def inmaps_D(w, c_results):
    lam = np.stack([w["l1_lambda_q1"][0], w["l1_lambda_q2"][0],
                    w["l1_lambda_k1"][0], w["l1_lambda_k2"][0]]).astype(np.float32)
    ones = np.ones((1, 128), dtype=np.float32)
    maps = []
    for core in range(NCORE):
        b, h0 = core // 4, (core % 4) * 4
        qt = np.zeros((4, 128, S), dtype=NPBF16)
        kt = np.zeros((4, 128, S), dtype=NPBF16)
        v = np.zeros((4, S, HD), dtype=NPBF16)
        for r in range(4):
            src = c_results[b * 4 + r]
            sl = slice(r * TPC, (r + 1) * TPC)
            for j in range(4):
                h = h0 + j
                qt[j][:, sl] = src["qt1_o"][h * 128:(h + 1) * 128, :]
                kt[j][:, sl] = src["kt1_o"][h * 128:(h + 1) * 128, :]
                v[j][sl, :] = src["v1_o"][:, h * 128:(h + 1) * 128]
        maps.append({"qt_d": qt, "kt_d": kt, "v_d": v, "lam_d": lam, "ones_d": ones})
    return maps


def build_launch_E():
    nc = bass.Bass("TRN2", target_bir_lowering=False)
    tT_d = nc.dram_tensor("tT_d", [D, TPC], BF16, kind="ExternalInput").ap()
    wo_c = nc.dram_tensor("wo_c", [4, D, 512], F32, kind="ExternalInput").ap()
    x_d = nc.dram_tensor("x_d", [TPC, D], F32, kind="ExternalInput").ap()
    fg_d = nc.dram_tensor("fg_d", [TPC, D], F32, kind="ExternalInput").ap()
    x2_o = nc.dram_tensor("x2_o", [TPC, D], F32, kind="ExternalOutput").ap()
    with ExitStack() as stack:
        ctx = Ctx(nc, stack)
        tT = ctx.sbuf("tT", [128, NDC * TPC], BF16)
        srcw = tT_d.rearrange("(dc p) c -> p dc c", p=128)
        ctx.dma("sync", tT[:, :].rearrange("p (dc c) -> p dc c", dc=NDC), srcw, "tld")
        ctx.wait("tensor", "tld", 16)
        ctx.stage()
        with ExitStack() as ph:
            ctx.push_scope(ph)
            emit_out_proj(ctx, tT, wo_c, x_d, fg_d, x2_o, tag="op1")
            ctx.pop_scope()
        ctx.stage()
        ctx.wait("sync", "op1_st", ctx.v("op1_st"))
        ctx.finalize()
    return nc


def inmaps_E(inputs, w, d_results, c_results):
    normed = np.zeros((B, NH, S, HD), dtype=NPBF16)
    for core in range(NCORE):
        b, h0 = core // 4, (core % 4) * 4
        for j in range(4):
            normed[b, h0 + j] = d_results[core]["o_d"][j]
    t = normed.reshape(B, NH * S, HD).transpose(0, 2, 1).reshape(B, S, D)
    maps = []
    for core in range(NCORE):
        b, t0 = core // 4, (core % 4) * TPC
        tT = np.ascontiguousarray(t[b, t0:t0 + TPC, :].T)
        maps.append({
            "tT_d": tT,
            "wo_c": w["l1_woT_c"],
            "x_d": c_results[core]["x1_o"],
            "fg_d": c_results[core]["fg1_o"],
        })
    return maps


def kernel(**inputs):
    c = host_consts()
    w = prep_weights(inputs)

    nc = build_launch_A()
    res = run_bass_kernel_spmd(nc, inmaps_A(inputs, w, c), core_ids=list(range(NCORE)))
    a_results = res.results

    nc = build_launch_B()
    res = run_bass_kernel_spmd(nc, inmaps_B(a_results, c), core_ids=list(range(NCORE)))
    b_results = res.results

    nc = build_launch_C()
    res = run_bass_kernel_spmd(nc, inmaps_C(inputs, w, c, a_results, b_results),
                               core_ids=list(range(NCORE)))
    c_results = res.results

    nc = build_launch_D()
    res = run_bass_kernel_spmd(nc, inmaps_D(w, c_results), core_ids=list(range(NCORE)))
    d_results = res.results

    nc = build_launch_E()
    res = run_bass_kernel_spmd(nc, inmaps_E(inputs, w, d_results, c_results),
                               core_ids=list(range(NCORE)))
    e_results = res.results

    out = np.zeros((B, S, D), dtype=np.float32)
    for core in range(NCORE):
        b, t0 = core // 4, (core % 4) * TPC
        out[b, t0:t0 + TPC, :] = e_results[core]["x2_o"]
    return out
